# version 08
# baseline (speedup 1.0000x reference)
import contextlib
import math

import numpy as np
import concourse.bass as bass
import concourse.mybir as mybir
from concourse.bass_utils import run_bass_kernel_spmd
from concourse.bass_isa import ReduceOp
from concourse.library_config import attn as _attn_lib
from concourse.library_overlay import lower_extended_insts

F32 = mybir.dt.float32
F16 = mybir.dt.float16
ALU = mybir.AluOpType
ACT = mybir.ActivationFunctionType
AX = mybir.AxisListType

T = 2048
E = 1024
H = 16
HD = 64
EC = 8
NPAIR = 8
KC = 16
QT = 512
NQT = T // QT
LN2_ = math.log(2.0)
WQ, WK, WV, WG = 0, 1, 2, 3
SERIAL_ATTN = False


def _host_constants():
    t = np.arange(T, dtype=np.float64)
    j = np.arange(HD)
    inv = 1.0 / (10000.0 ** ((2.0 * (j % 32)) / HD))
    ang = t[None, :] * inv[:, None]
    cos = np.cos(ang)
    sin = np.sin(ang)
    sgn = np.where(j % 2 == 0, -1.0, 1.0)
    sin_s = sin * sgn[:, None]
    cos2 = np.concatenate([cos, cos], axis=0).astype(np.float32)
    sin2 = np.concatenate([sin_s, sin_s], axis=0).astype(np.float32)
    ii = np.arange(128)
    U = (ii[:, None] > ii[None, :]).astype(np.float32)
    L = (ii[:, None] <= ii[None, :]).astype(np.float32)
    kramp = (ii[:, None] + 128.0 * np.arange(KC)[None, :]).astype(np.float32)
    qramp = np.arange(QT, dtype=np.float32)[None, :]
    ident = np.eye(128, dtype=np.float32)
    ones_col = np.ones((128, 1), dtype=np.float32)
    return dict(cos2=cos2, sin2=sin2, U=U, L=L, kramp=kramp, qramp=qramp,
                ident=ident, ones_col=ones_col)


def _blk_pair(wt):
    return np.ascontiguousarray(wt.reshape(EC, 128, EC, 128).transpose(2, 1, 0, 3))


def _col8(v):
    return np.ascontiguousarray(v.reshape(8, 128).T)


class _Sems:
    def __init__(self, nc, stack, names):
        self.h = {}
        self.n = {}
        for name in names:
            self.h[name] = stack.enter_context(nc.semaphore(name))
            self.n[name] = 0

    def inc(self, ins, name, k=1):
        ins.then_inc(self.h[name], k)
        self.n[name] += k
        return self.n[name]

    def cur(self, name):
        return self.n[name]


def build(tap=None):
    nc = bass.Bass("TRN2", target_bir_lowering=False,
                   detect_race_conditions=False)

    x_fm = nc.dram_tensor("x_fm", [E, T], F32, kind="ExternalInput").ap()
    consts = {
        "cos2": [128, T], "sin2": [128, T], "U": [128, 128], "L": [128, 128],
        "kramp": [128, KC], "qramp": [1, QT], "ident": [128, 128],
        "ones_col": [128, 1],
    }
    cst = {k: nc.dram_tensor("c_" + k, sh, F32, kind="ExternalInput").ap()
           for k, sh in consts.items()}
    pnames = ["bq", "bk", "bv", "bg", "bo", "gamma", "beta", "mu_r", "mu_k"]
    par = {k: nc.dram_tensor("p_" + k, [128, 8], F32, kind="ExternalInput").ap()
           for k in pnames}
    bd = nc.dram_tensor("p_bd", [1, H], F32, kind="ExternalInput").ap()
    wdT = nc.dram_tensor("w_dT", [EC, 128, H], F32, kind="ExternalInput").ap()
    wblk = {m: nc.dram_tensor("w_" + m, [8, 128, EC, 128], F32,
                              kind="ExternalInput").ap()
            for m in ["q", "k", "v", "g", "o", "r", "k2", "v2"]}

    def sp16(name, tapped):
        kind = "ExternalOutput" if tapped else "Internal"
        return nc.dram_tensor(name, [2, EC, 128, T], F16, kind=kind).ap()

    h_sp = sp16("h_sp", tap in ("h", "all", "allq"))
    gon_sp = sp16("gon_sp", tap in ("gon", "all", "allq"))
    x1_sp = sp16("x1_sp", tap in ("x1", "all", "allq"))
    r_sp = nc.dram_tensor("r_sp", [EC, 128, T], F32,
                          kind="ExternalOutput" if tap in ("r", "all", "allq")
                          else "Internal").ap()
    if tap == "qk":
        dbg_qk = nc.dram_tensor("dbg_qk", [4, 128, T], F16,
                                kind="ExternalOutput").ap()
    if tap == "og":
        dbg_og = nc.dram_tensor("dbg_og", [2, 64, T], F32,
                                kind="ExternalOutput").ap()
    if tap in ("pb", "pb0"):
        dbg_pb = nc.dram_tensor("dbg_pb", [2, 128, KC * QT], F32,
                                kind="ExternalOutput").ap()
    if tap == "cs0":
        dbg_cs = nc.dram_tensor("dbg_cs", [66, QT], F32,
                                kind="ExternalOutput").ap()
    if tap in ("ld", "hm"):
        dbg_ld = nc.dram_tensor("dbg_ld", [4, 1, H], F32,
                                kind="ExternalOutput").ap()
        dbg_hm = nc.dram_tensor("dbg_hm", [128, EC], F32,
                                kind="ExternalOutput").ap()
        dbg_hs = nc.dram_tensor("dbg_hs", [2, 128, EC], F32,
                                kind="ExternalOutput").ap()
    y = nc.dram_tensor("y", [E, T], F32, kind="ExternalOutput").ap()
    if tap == "all":
        dbg_b = nc.dram_tensor("dbg_b", [NPAIR, 4, 2], F32,
                               kind="ExternalOutput").ap()
        dbg_ogall = nc.dram_tensor("dbg_ogall", [NPAIR, 2, 64, T], F32,
                                   kind="ExternalOutput").ap()
        dbg_gate = nc.dram_tensor("dbg_gate", [NPAIR, 2, 64, T], F32,
                                  kind="ExternalOutput").ap()
        dbg_r2 = nc.dram_tensor("dbg_r2", [NPAIR, 1, 16], F32,
                                kind="ExternalOutput").ap()

    TAP_PAIRS = 0 if tap in ("ld", "hm") else (1 if tap in ("qk", "og", "pb", "pb0", "cs0") else NPAIR)
    tap_now = [None]

    with contextlib.ExitStack() as st:
        sem = _Sems(nc, st, ["LD", "TE", "SC", "VE", "GP",
                             "LX0", "LX1", "LCS", "LW0", "LW1", "LH0", "LH1",
                             "STH", "STG", "STX", "STR", "STY"])
        sb = lambda name, shape, dt=F32: st.enter_context(
            nc.sbuf_tensor(name, shape, dt))
        block = st.enter_context(nc.Block())

        c_U = sb("sb_U", [128, 128])
        c_L = sb("sb_L", [128, 128])
        c_kramp = sb("sb_kramp", [128, KC])
        c_qramp = sb("sb_qramp", [1, QT])
        c_ident = sb("sb_ident", [128, 128])
        c_ones = sb("sb_ones", [128, 1])
        ones16 = sb("sb_ones16", [128, 1], F16)
        p_sb = {k: sb("p" + k, [128, 8]) for k in pnames}
        p_bd = sb("pbd", [1, H])
        pdrow = sb("pdrow", [1, H])
        ldrow = sb("ldrow", [1, H])
        ldnrow = sb("ldnrow", [1, H])
        hrow = sb("hrow", [1, H])
        ldb = sb("ldb", [128, H])
        ldks = sb("ldks", [128, H * KC])
        eks = sb("eks", [128, H * KC])
        hsum = sb("hsum", [128, EC])
        hsum2 = sb("hsum2", [128, EC])
        hmean = sb("hmean", [128, EC])
        browq = sb("browq", [1, 2])
        browk = sb("browk", [1, 2])
        bbc = sb("bbc", [128, 2])
        ldnb = sb("ldnb", [128, 2])
        bfh = sb("bfh", [128, 2 * KC])
        gnb = sb("gnb", [64, 4])
        gnc = sb("gnc", [64, 4])
        r2t = sb("r2t", [1, 16])
        lqrow = sb("lqrow", [1, 8])
        ctb = sb("ctb", [128, 2])
        flsh = sb("flsh", [8, 1024])

        tok = {}
        prev = {}
        stB = None

        def vsync(vector, ins):
            t = sem.inc(ins, "VE", 1)
            vector.wait_ge(sem.h["VE"], t)
            return t

        def vpub(vector, ins):
            t = sem.inc(ins, "VE", 1)
            vector.wait_ge(sem.h["VE"], t)
            return sem.inc(vector.tensor_copy(flsh[:, 512:1024],
                                              flsh[:, 0:512]), "VE", 1)

        def spub(scalar, ins):
            t = sem.inc(ins, "SC", 1)
            scalar.wait_ge(sem.h["SC"], t)
            return sem.inc(scalar.activation(flsh[:, 0:512],
                                             flsh[:, 512:1024], ACT.Identity),
                           "SC", 1)

        with nc.sbuf_tensor("xb0", [128, T], F32) as xb0, \
             nc.sbuf_tensor("xb1", [128, T], F32) as xb1, \
             nc.sbuf_tensor("sq0", [128, T], F32) as sq0, \
             nc.sbuf_tensor("sq1", [128, T], F32) as sq1, \
             nc.sbuf_tensor("m1row", [1, T], F32) as m1row, \
             nc.sbuf_tensor("rstd1row", [1, T], F32) as rstd1row, \
             nc.sbuf_tensor("m1b", [128, T], F32) as m1b, \
             nc.sbuf_tensor("rstd1b", [128, T], F32) as rstd1b, \
             nc.sbuf_tensor("hhib", [128, T], F16) as hhib, \
             nc.sbuf_tensor("hlob", [128, T], F16) as hlob, \
             nc.sbuf_tensor("wdTs", [128, EC * H], F32) as w_dT, \
             nc.psum_tensor("psS", [1, T], F32) as psS, \
             nc.psum_tensor("psQ", [1, T], F32) as psQ:

            xb = [xb0, xb1]
            sq = [sq0, sq1]
            t1b = sq0
            h32b = sq1
            vrow = psS[0:1, T // 2:T]

            @block.sync
            def _(sync):
                for k, tile in [("U", c_U), ("L", c_L), ("kramp", c_kramp),
                                ("qramp", c_qramp), ("ident", c_ident),
                                ("ones_col", c_ones)]:
                    sem.inc(sync.dma_start(out=tile[:], in_=cst[k]), "LD", 16)
                for k in pnames:
                    sem.inc(sync.dma_start(out=p_sb[k][:], in_=par[k]), "LD", 16)
                sem.inc(sync.dma_start(out=p_bd[:], in_=bd), "LD", 16)
                for ec in range(EC):
                    sem.inc(sync.dma_start(out=w_dT[:, ec * H:(ec + 1) * H],
                                           in_=wdT[ec]), "LD", 16)
                tok["consts"] = sem.cur("LD")

            @block.gpsimd
            def _(gpsimd):
                gpsimd.load_library(_attn_lib)
                sem.inc(gpsimd.engine_nop(), "GP", 1)

            @block.vector
            def _(vector):
                vector.wait_ge(sem.h["LD"], tok["consts"])
                vector.memset(ones16[:], 1.0)
                vector.memset(ctb[:, 0:1], 1e-5)
                sem.inc(vector.memset(ctb[:, 1:2], -5.0 * LN2_), "VE", 1)

            for ec in range(EC):
                @block.sync
                def _(sync, ec=ec):
                    if ec >= 2:
                        sync.wait_ge(sem.h["TE"], tok[("st_te", ec - 2)])
                    tok[("x0", ec)] = sem.inc(
                        sync.dma_start(out=xb[ec % 2][:],
                                       in_=x_fm[ec * 128:(ec + 1) * 128, :]),
                        "LX%d" % (ec % 2), 16)

                @block.scalar
                def _(scalar, ec=ec):
                    scalar.wait_ge(sem.h["LX%d" % (ec % 2)], tok[("x0", ec)])
                    if ec >= 2:
                        scalar.wait_ge(sem.h["TE"], tok[("st_te", ec - 2)])
                    tok[("sq", ec)] = sem.inc(
                        scalar.activation(sq[ec % 2][:], xb[ec % 2][:], ACT.Square),
                        "SC", 1)

                @block.tensor
                def _(tensor, ec=ec):
                    tensor.wait_ge(sem.h["LX%d" % (ec % 2)], tok[("x0", ec)])
                    if ec == 0:
                        tensor.wait_ge(sem.h["LD"], tok["consts"])
                    for qt in range(NQT):
                        tensor.matmul(psS[0:1, qt * QT:(qt + 1) * QT], c_ones[:],
                                      xb[ec % 2][:, qt * QT:(qt + 1) * QT],
                                      start=(ec == 0), stop=(ec == EC - 1))
                    tensor.wait_ge(sem.h["SC"], tok[("sq", ec)])
                    ins = None
                    for qt in range(NQT):
                        ins = tensor.matmul(psQ[0:1, qt * QT:(qt + 1) * QT],
                                            c_ones[:],
                                            sq[ec % 2][:, qt * QT:(qt + 1) * QT],
                                            start=(ec == 0), stop=(ec == EC - 1))
                    tok[("st_te", ec)] = sem.inc(ins, "TE", 1)

            @block.vector
            def _(vector):
                vector.wait_ge(sem.h["TE"], tok[("st_te", EC - 1)])
                vector.tensor_scalar(m1row[:], psS[0:1, :], 1.0 / E, None, ALU.mult)
                vector.tensor_tensor(rstd1row[:], m1row[:], m1row[:], ALU.mult)
                tok["var"] = sem.inc(
                    vector.scalar_tensor_tensor(rstd1row[:], psQ[0:1, :], 1.0 / E,
                                                rstd1row[:], ALU.mult,
                                                ALU.subtract), "VE", 1)

            @block.scalar
            def _(scalar):
                scalar.wait_ge(sem.h["VE"], tok["var"])
                tok["sqrt"] = sem.inc(
                    scalar.activation(rstd1row[:], rstd1row[:], ACT.Sqrt,
                                      bias=ctb[0:1, 0:1]), "SC", 1)

            @block.vector
            def _(vector):
                vector.wait_ge(sem.h["SC"], tok["sqrt"])
                tok["rstd"] = vpub(
                    vector, vector.reciprocal(rstd1row[:], rstd1row[:]))

            @block.gpsimd
            def _(gpsimd):
                gpsimd.wait_ge(sem.h["VE"], tok["rstd"])
                gpsimd.partition_broadcast(m1b[:], m1row[:], 128)
                tok["bcast"] = sem.inc(
                    gpsimd.partition_broadcast(rstd1b[:], rstd1row[:], 128), "GP", 1)

            for ec in range(EC):
                @block.sync
                def _(sync, ec=ec):
                    if ec >= 2:
                        sync.wait_ge(sem.h["VE"], tok[("h32", ec - 2)])
                    else:
                        sync.wait_ge(sem.h["SC"], tok[("sq", EC - 1)])
                        sync.wait_ge(sem.h["TE"], tok[("st_te", EC - 1)])
                    tok[("x1", ec)] = sem.inc(
                        sync.dma_start(out=xb[ec % 2][:],
                                       in_=x_fm[ec * 128:(ec + 1) * 128, :]),
                        "LX%d" % (ec % 2), 16)

                @block.vector
                def _(vector, ec=ec):
                    vector.wait_ge(sem.h["LX%d" % (ec % 2)], tok[("x1", ec)])
                    if ec == 0:
                        vector.wait_ge(sem.h["GP"], tok["bcast"])
                    if ec >= 1:
                        vector.wait_ge(sem.h["SC"], tok[("hhi", ec - 1)])
                    vector.tensor_tensor(t1b[:], xb[ec % 2][:], m1b[:], ALU.subtract)
                    tok[("h32", ec)] = sem.inc(
                        vector.tensor_tensor(h32b[:], t1b[:], rstd1b[:], ALU.mult),
                        "VE", 1)

                @block.scalar
                def _(scalar, ec=ec):
                    scalar.wait_ge(sem.h["VE"], tok[("h32", ec)])
                    if ec >= 1:
                        scalar.wait_ge(sem.h["STH"], tok[("hsp", ec - 1)])
                    tok[("hhi", ec)] = sem.inc(
                        scalar.activation(hhib[:], h32b[:], ACT.Copy), "SC", 1)

                @block.vector
                def _(vector, ec=ec):
                    vector.wait_ge(sem.h["SC"], tok[("hhi", ec)])
                    vector.scalar_tensor_tensor(
                        hlob[:], h32b[:], 1.0, hhib[:], ALU.mult, ALU.subtract)
                    vsync(vector, vector.reduce_sum(hsum[:, ec:ec + 1], hhib[:],
                                                    AX.X))
                    tok[("hlo", ec)] = vsync(
                        vector, vector.reduce_sum(hsum2[:, ec:ec + 1], hlob[:],
                                                  AX.X))

                @block.sync
                def _(sync, ec=ec):
                    sync.wait_ge(sem.h["VE"], tok[("hlo", ec)])
                    sem.inc(sync.dma_start(out=h_sp[0, ec], in_=hhib[:]), "STH", 16)
                    tok[("hsp", ec)] = sem.inc(
                        sync.dma_start(out=h_sp[1, ec], in_=hlob[:]), "STH", 16)

            @block.vector
            def _(vector):
                vector.wait_ge(sem.h["VE"], tok[("hlo", EC - 1)])
                tok["hmean"] = vpub(
                    vector,
                    vector.tensor_tensor(hmean[:], hsum[:], hsum2[:], ALU.add))

            if tap == "hm":
                @block.sync
                def _(sync):
                    sync.wait_ge(sem.h["VE"], tok["hmean"])
                    sem.inc(sync.dma_start(out=dbg_hm, in_=hmean[:]), "STY", 16)
                    sem.inc(sync.dma_start(out=dbg_hs[0], in_=hsum[:]), "STY", 16)
                    sem.inc(sync.dma_start(out=dbg_hs[1], in_=hsum2[:]), "STY", 16)
                    sync.wait_ge(sem.h["STY"], sem.cur("STY"))
                    sync.wait_ge(sem.h["STH"], sem.cur("STH"))

            @block.tensor
            def _(tensor):
                tensor.wait_ge(sem.h["VE"], tok["hmean"])
                ins = None
                for ec in range(EC):
                    ins = tensor.matmul(psS[0:1, 0:H], hmean[:, ec:ec + 1],
                                        w_dT[:, ec * H:(ec + 1) * H],
                                        start=(ec == 0), stop=(ec == EC - 1),
                                        skip_group_check=True)
                tok["pdmm"] = sem.inc(ins, "TE", 1)

            @block.vector
            def _(vector):
                vector.wait_ge(sem.h["TE"], tok["pdmm"])
                tok["pdl"] = vpub(
                    vector,
                    vector.scalar_tensor_tensor(pdrow[:], psS[0:1, 0:H], 1.0 / T,
                                                p_bd[:], ALU.mult, ALU.add))

            @block.scalar
            def _(scalar):
                scalar.wait_ge(sem.h["VE"], tok["pdl"])
                ts = sem.inc(scalar.activation(pdrow[:], pdrow[:], ACT.Sigmoid),
                             "SC", 1)
                scalar.wait_ge(sem.h["SC"], ts)
                tok["pd"] = spub(
                    scalar,
                    scalar.activation(hrow[:], pdrow[:], ACT.Exp,
                                      bias=ctb[0:1, 1:2], scale=-8.0 * LN2_))

            @block.vector
            def _(vector):
                vector.wait_ge(sem.h["SC"], tok["pd"])
                tok["w1"] = vpub(
                    vector,
                    vector.tensor_scalar(ldnrow[:], hrow[:], -1.0, 1.0, ALU.mult,
                                         ALU.add))

            @block.scalar
            def _(scalar):
                scalar.wait_ge(sem.h["VE"], tok["w1"])
                tok["ld"] = spub(
                    scalar, scalar.activation(ldrow[:], ldnrow[:], ACT.Ln))

            @block.vector
            def _(vector):
                vector.wait_ge(sem.h["SC"], tok["ld"])
                tok["ldn"] = sem.inc(
                    vector.tensor_scalar(ldnrow[:], ldrow[:], -1.0, None, ALU.mult),
                    "VE", 1)

            @block.gpsimd
            def _(gpsimd):
                gpsimd.wait_ge(sem.h["VE"], tok["ldn"])
                ins = None
                for h in range(H):
                    ins = gpsimd.partition_broadcast(ldb[:, h:h + 1],
                                                     ldrow[0:1, h:h + 1], 128)
                tok["ldb"] = sem.inc(ins, "GP", 1)

            @block.vector
            def _(vector):
                vector.wait_ge(sem.h["GP"], tok["ldb"])
                ins = None
                for h in range(H):
                    ins = vector.tensor_scalar(ldks[:, h * KC:(h + 1) * KC],
                                               c_kramp[:], ldb[:, h:h + 1],
                                               None, ALU.mult)
                tok["ldks"] = sem.inc(ins, "VE", 1)

            @block.scalar
            def _(scalar):
                scalar.wait_ge(sem.h["VE"], tok["ldks"])
                tok["eks"] = sem.inc(
                    scalar.activation(eks[:], ldks[:], ACT.Exp, scale=-1.0), "SC", 1)

        if tap == "h":
            @block.sync
            def _(sync):
                sync.wait_ge(sem.h["STH"], sem.cur("STH"))
                sync.wait_ge(sem.h["SC"], sem.cur("SC"))
            lower_extended_insts(nc)
            return nc

        if tap == "hm":
            lower_extended_insts(nc)
            return nc

        if tap == "ld":
            @block.sync
            def _(sync):
                sync.wait_ge(sem.h["SC"], tok["eks"])
                sem.inc(sync.dma_start(out=dbg_ld[0], in_=pdrow[:]), "STY", 16)
                sem.inc(sync.dma_start(out=dbg_ld[1], in_=hrow[:]), "STY", 16)
                sem.inc(sync.dma_start(out=dbg_ld[2], in_=ldrow[:]), "STY", 16)
                sem.inc(sync.dma_start(out=dbg_ld[3], in_=ldnrow[:]), "STY", 16)
                sem.inc(sync.dma_start(out=dbg_hm, in_=hmean[:]), "STY", 16)
                sem.inc(sync.dma_start(out=dbg_hs[0], in_=hsum[:]), "STY", 16)
                sem.inc(sync.dma_start(out=dbg_hs[1], in_=hsum2[:]), "STY", 16)
                sync.wait_ge(sem.h["STY"], sem.cur("STY"))
                sync.wait_ge(sem.h["STH"], sem.cur("STH"))
            lower_extended_insts(nc)
            return nc

        stB = st.enter_context(contextlib.ExitStack())
        sbB = lambda name, shape, dt=F32: stB.enter_context(
            nc.sbuf_tensor(name, shape, dt))
        arena = sbB("arena", [128, 2 * KC * QT])
        wst_ = sbB("wst0", [128, EC * 128])
        wst = [wst_, wst_]
        whi = [sbB(f"whi{i}", [128, EC * 128], F16) for i in range(4)]
        wlo = [sbB(f"wlo{i}", [128, EC * 128], F16) for i in range(4)]
        hstr = [[sbB(f"hstr{i}_{d}", [128, T], F16) for d in range(2)]
                for i in range(2)]
        qhi = sbB("qhi", [128, T], F16)
        qlo = sbB("qlo", [128, T], F16)
        khi = sbB("khi", [128, T], F16)
        klo = sbB("klo", [128, T], F16)
        vtm = [sbB(f"vtm{hh}", [128, KC * 65]) for hh in range(2)]
        vsc = [sbB(f"vsc{hh}", [128, KC * 65]) for hh in range(2)]
        gateh = [sbB(f"gateh{hh}", [64, T]) for hh in range(2)]
        ogt = [sbB(f"ogt{hh}", [64, T]) for hh in range(2)]
        stripL = sbB("stripL", [128, 2 * 2 * 128])
        stripU = sbB("stripU", [128, 2 * 2 * 128])
        csc1 = sbB("csc1", [65, QT])
        linvb = sbB("linvb", [64, QT])
        dbb1v = sbB("dbb1", [65, QT])
        dbb1 = [dbb1v, dbb1v]
        r1t = sbB("r1t", [1, QT])
        sqsc = sbB("sqsc", [128, T], F16)

        cosv = arena[:, 0:T]
        sinv = arena[:, T:2 * T]
        pbuf = [arena[:, 0:KC * QT], arena[:, KC * QT:2 * KC * QT]]
        shuf_mask = [i + 1 if i % 2 == 0 else i - 1 for i in range(32)]

        def wsl(buf, ec):
            return buf[:, ec * 128:(ec + 1) * 128]

        def _mm(tensor, dst, lhsT, rhs, key, state, stop=False):
            first = key not in state
            state[key] = True
            return tensor.matmul(dst, lhsT, rhs, start=first, stop=stop,
                                 skip_group_check=True)

        def emit_wchain(pp):
            wm = [("q", ("hu1", pp - 1, EC - 1)), ("k", ("hu1", pp - 1, EC - 1)),
                  ("v", ("hu2", pp - 1, EC - 1)), ("g", ("hu2", pp - 1, EC - 1))]
            for i, (m, conflict) in enumerate(wm):
                @block.sync
                def _(sync, pp=pp, i=i, m=m):
                    guard = ("wsp", pp, i - 1) if i >= 1 else \
                        ("wsp", pp - 1, 3)
                    if guard in tok:
                        sync.wait_ge(sem.h["VE"], tok[guard])
                    tok[("w32", pp, i)] = sem.inc(
                        sync.dma_start(out=wst[i % 2][:], in_=wblk[m][pp]),
                        "LW%d" % (i % 2), 16)

                @block.scalar
                def _(scalar, pp=pp, i=i, conflict=conflict):
                    scalar.wait_ge(sem.h["LW%d" % (i % 2)], tok[("w32", pp, i)])
                    if pp > 0:
                        scalar.wait_ge(sem.h["TE"], tok[conflict])
                    tok[("whi", pp, i)] = sem.inc(
                        scalar.activation(whi[i][:], wst[i % 2][:], ACT.Copy),
                        "SC", 1)

                @block.vector
                def _(vector, pp=pp, i=i):
                    vector.wait_ge(sem.h["SC"], tok[("whi", pp, i)])
                    tok[("wsp", pp, i)] = sem.inc(
                        vector.scalar_tensor_tensor(
                            wlo[i][:], wst[i % 2][:], 1.0, whi[i][:],
                            ALU.mult, ALU.subtract), "VE", 1)

        emit_wchain(0)

        for p in range(TAP_PAIRS):
            with nc.psum_tensor(f"psA_{p}", [128, T], F32) as psA, \
                 nc.psum_tensor(f"psB_{p}", [128, T], F32) as psB, \
                 nc.sbuf_tensor(f"q32_{p}", [128, T], F32) as q32:

                s1 = arena[:, 2 * T:3 * T]

                def wbuf_hi(i):
                    return whi[i]

                def wbuf_lo(i):
                    return wlo[i]

                @block.gpsimd
                def _(gpsimd, p=p):
                    if p > 0:
                        gpsimd.wait_ge(sem.h["TE"], prev[("pvdone", p - 1)])
                    sem.inc(gpsimd.dma_start(out=cosv, in_=cst["cos2"]),
                            "LCS", 16)
                    tok[("cs", p)] = sem.inc(
                        gpsimd.dma_start(out=sinv, in_=cst["sin2"]), "LCS", 16)

                for ec in range(EC):
                    @block.sync
                    def _(sync, p=p, ec=ec):
                        if ec >= 2:
                            sync.wait_ge(sem.h["TE"], tok[("hu1", p, ec - 2)])
                        elif p > 0:
                            sync.wait_ge(sem.h["TE"],
                                         prev[("hu2", p - 1, EC - 2 + ec)])
                        else:
                            sync.wait_ge(sem.h["STH"], tok[("hsp", EC - 1)])
                        sem.inc(sync.dma_start(out=hstr[ec % 2][0][:],
                                               in_=h_sp[0, ec]),
                                "LH%d" % (ec % 2), 16)
                        tok[("hl1", p, ec)] = sem.inc(
                            sync.dma_start(out=hstr[ec % 2][1][:],
                                           in_=h_sp[1, ec]),
                            "LH%d" % (ec % 2), 16)

                    @block.tensor
                    def _(tensor, p=p, ec=ec):
                        tensor.wait_ge(sem.h["LH%d" % (ec % 2)],
                                       tok[("hl1", p, ec)])
                        if ec == 0:
                            if p > 0:
                                tensor.wait_ge(sem.h["SC"],
                                               prev[("expdone", p - 1)])
                            tensor.wait_ge(sem.h["VE"], tok[("wsp", p, WK)])
                        ins = None
                        for i, ps in ((WQ, psA), (WK, psB)):
                            first = (ec == 0)
                            for qt in range(NQT):
                                sl = slice(qt * QT, (qt + 1) * QT)
                                tensor.matmul(ps[:, sl], wsl(wbuf_hi(i), ec),
                                              hstr[ec % 2][0][:, sl],
                                              start=first, stop=False)
                                tensor.matmul(ps[:, sl], wsl(wbuf_hi(i), ec),
                                              hstr[ec % 2][1][:, sl],
                                              start=False, stop=False)
                                ins = tensor.matmul(ps[:, sl], wsl(wbuf_lo(i), ec),
                                                    hstr[ec % 2][0][:, sl],
                                                    start=False,
                                                    stop=(ec == EC - 1))
                        tok[("hu1", p, ec)] = sem.inc(ins, "TE", 1)
                tok[("qkmm", p)] = tok[("hu1", p, EC - 1)]

                @block.scalar
                def _(scalar, p=p):
                    scalar.wait_ge(sem.h["TE"], tok[("qkmm", p)])
                    if p > 0:
                        scalar.wait_ge(sem.h["VE"], prev[("rope", p - 1)])
                    tok[("qev", p)] = sem.inc(
                        scalar.activation(q32[:], psA[:], ACT.Identity,
                                          bias=p_sb["bq"][:, p:p + 1]), "SC", 1)

                def rope_ops(vector, srct, dhi, dlo):
                    vector.stream_shuffle(s1[:], srct[:], shuf_mask)
                    vector.tensor_tensor(s1[:], s1[:], sinv, ALU.mult)
                    vector.tensor_tensor(srct[:], srct[:], cosv, ALU.mult)
                    vector.tensor_tensor(srct[:], srct[:], s1[:], ALU.add)
                    vector.tensor_copy(dhi[:], srct[:])
                    return vector.scalar_tensor_tensor(dlo[:], srct[:], 1.0,
                                                       dhi[:], ALU.mult,
                                                       ALU.subtract)

                @block.vector
                def _(vector, p=p):
                    vector.wait_ge(sem.h["SC"], tok[("qev", p)])
                    vector.wait_ge(sem.h["LCS"], tok[("cs", p)])
                    if p > 0:
                        vector.wait_ge(sem.h["TE"], prev[("scdone", p - 1)])
                    tok[("ropeq", p)] = sem.inc(rope_ops(vector, q32, qhi, qlo),
                                                "VE", 1)

                @block.scalar
                def _(scalar, p=p):
                    scalar.wait_ge(sem.h["VE"], tok[("ropeq", p)])
                    tok[("kev", p)] = sem.inc(
                        scalar.activation(q32[:], psB[:], ACT.Identity,
                                          bias=p_sb["bk"][:, p:p + 1]), "SC", 1)

                @block.vector
                def _(vector, p=p):
                    vector.wait_ge(sem.h["SC"], tok[("kev", p)])
                    tok[("rope", p)] = sem.inc(rope_ops(vector, q32, khi, klo),
                                               "VE", 1)
                    prev[("rope", p)] = tok[("rope", p)]

                for ec in range(EC):
                    @block.sync
                    def _(sync, p=p, ec=ec):
                        if ec >= 2:
                            sync.wait_ge(sem.h["TE"], tok[("hu2", p, ec - 2)])
                        else:
                            sync.wait_ge(sem.h["TE"], tok[("hu1", p, EC - 2 + ec)])
                        sem.inc(sync.dma_start(out=hstr[ec % 2][0][:],
                                               in_=h_sp[0, ec]),
                                "LH%d" % (ec % 2), 16)
                        tok[("hl2", p, ec)] = sem.inc(
                            sync.dma_start(out=hstr[ec % 2][1][:],
                                           in_=h_sp[1, ec]),
                            "LH%d" % (ec % 2), 16)

                    @block.tensor
                    def _(tensor, p=p, ec=ec):
                        tensor.wait_ge(sem.h["LH%d" % (ec % 2)],
                                       tok[("hl2", p, ec)])
                        if ec == 0:
                            tensor.wait_ge(sem.h["SC"], tok[("kev", p)])
                            tensor.wait_ge(sem.h["VE"], tok[("wsp", p, WV)])
                            tensor.wait_ge(sem.h["VE"], tok[("wsp", p, WG)])
                        ins = None
                        for i, ps in ((WV, psA), (WG, psB)):
                            first = (ec == 0)
                            for qt in range(NQT):
                                sl = slice(qt * QT, (qt + 1) * QT)
                                tensor.matmul(ps[:, sl], wsl(wbuf_hi(i), ec),
                                              hstr[ec % 2][0][:, sl],
                                              start=first, stop=False)
                                tensor.matmul(ps[:, sl], wsl(wbuf_hi(i), ec),
                                              hstr[ec % 2][1][:, sl],
                                              start=False, stop=False)
                                ins = tensor.matmul(ps[:, sl], wsl(wbuf_lo(i), ec),
                                                    hstr[ec % 2][0][:, sl],
                                                    start=False,
                                                    stop=(ec == EC - 1))
                        tok[("hu2", p, ec)] = sem.inc(ins, "TE", 1)
                        prev[("hu2", p, ec)] = tok[("hu2", p, ec)]
                tok[("vgmm", p)] = tok[("hu2", p, EC - 1)]
                prev[("projdone", p)] = tok[("vgmm", p)]

                @block.scalar
                def _(scalar, p=p):
                    scalar.wait_ge(sem.h["VE"], tok[("rope", p)])
                    if p > 0:
                        scalar.wait_ge(sem.h["VE"], prev[("gsq", p - 1)])
                    tok[("sqq", p)] = sem.inc(
                        scalar.activation(sqsc[:], qhi[:], ACT.Square), "SC", 1)

                @block.scalar
                def _(scalar, p=p):
                    scalar.wait_ge(sem.h["TE"], tok[("vgmm", p)])
                    scalar.wait_ge(sem.h["VE"], tok[("rope", p)])
                    tok[("vev", p)] = sem.inc(
                        scalar.activation(q32[:], psA[:], ACT.Identity,
                                          bias=p_sb["bv"][:, p:p + 1]), "SC", 1)
                    if p > 0:
                        scalar.wait_ge(sem.h["VE"], prev[("gonve", p - 1)])
                    scalar.activation(gateh[0][:], psB[0:64, :], ACT.Silu,
                                      bias=p_sb["bg"][0:64, p:p + 1])
                    tok[("gev", p)] = sem.inc(
                        scalar.activation(gateh[1][:], psB[64:128, :], ACT.Silu,
                                          bias=p_sb["bg"][64:128, p:p + 1]),
                        "SC", 1)

                if p + 1 < TAP_PAIRS:
                    emit_wchain(p + 1)

                for c in range(KC):
                    @block.tensor
                    def _(tensor, p=p, c=c):
                        if c == 0:
                            tensor.wait_ge(sem.h["SC"], tok[("vev", p)])
                            if p > 0:
                                tensor.wait_ge(sem.h["VE"], prev[("vscd", p - 1)])
                        if c >= 2:
                            tensor.wait_ge(sem.h["VE"], tok[("vtc", p, c - 2)])
                        dst = psA[:, 0:128] if c % 2 == 0 else psA[:, QT:QT + 128]
                        tok[("vt", p, c)] = sem.inc(
                            tensor.transpose(dst, q32[:, c * 128:(c + 1) * 128],
                                             c_ident[:]), "TE", 1)

                    @block.vector
                    def _(vector, p=p, c=c):
                        vector.wait_ge(sem.h["TE"], tok[("vt", p, c)])
                        src = psA[:, 0:128] if c % 2 == 0 else psA[:, QT:QT + 128]
                        vector.tensor_copy(vtm[0][:, c * 65:c * 65 + 64],
                                           src[:, 0:64])
                        tok[("vtc", p, c)] = sem.inc(
                            vector.tensor_copy(vtm[1][:, c * 65:c * 65 + 64],
                                               src[:, 64:128]), "VE", 1)

                @block.vector
                def _(vector, p=p):
                    vector.wait_ge(sem.h["TE"], tok[("vt", p, KC - 1)])
                    vector.memset(vtm[0][:, 64::65], 1.0)
                    vector.memset(vtm[1][:, 64::65], 1.0)
                    vector.wait_ge(sem.h["SC"], tok["eks"])
                    ins = None
                    for hh in range(2):
                        h = 2 * p + hh
                        for c in range(KC):
                            ins = vector.tensor_scalar(
                                vsc[hh][:, c * 65:(c + 1) * 65],
                                vtm[hh][:, c * 65:(c + 1) * 65],
                                eks[:, h * KC + c:h * KC + c + 1], None, ALU.mult)
                    tok[("vscd", p)] = sem.inc(ins, "VE", 1)
                    prev[("vscd", p)] = tok[("vscd", p)]

                @block.tensor
                def _(tensor, p=p):
                    tensor.wait_ge(sem.h["SC"], tok[("sqq", p)])
                    tensor.wait_ge(sem.h["SC"], tok[("gev", p)])
                    tensor.wait_ge(sem.h["VE"], tok[("vtc", p, KC - 1)])
                    for qt in range(NQT):
                        sl = slice(qt * QT, (qt + 1) * QT)
                        tensor.matmul(psB[0:1, sl], ones16[0:64, 0:1],
                                      sqsc[0:64, sl], start=True, stop=True,
                                      skip_group_check=True)
                        ins = tensor.matmul(psA[0:1, sl], ones16[64:128, 0:1],
                                            sqsc[64:128, sl], start=True,
                                            stop=True, tile_position=(64, 0),
                                            skip_group_check=True)
                    tok[("nrmq", p)] = sem.inc(ins, "TE", 1)

                @block.vector
                def _(vector, p=p):
                    vector.wait_ge(sem.h["TE"], tok[("nrmq", p)])
                    vsync(vector, vector.tensor_reduce(browq[0:1, 0:1],
                                                       psB[0:1, :], AX.X, ALU.max))
                    tok[("nrq", p)] = vsync(
                        vector, vector.tensor_reduce(browq[0:1, 1:2], psA[0:1, :],
                                                     AX.X, ALU.max))

                @block.scalar
                def _(scalar, p=p):
                    scalar.wait_ge(sem.h["TE"], tok[("nrmq", p)])
                    tok[("sqk", p)] = sem.inc(
                        scalar.activation(sqsc[:], khi[:], ACT.Square), "SC", 1)

                @block.tensor
                def _(tensor, p=p):
                    tensor.wait_ge(sem.h["VE"], tok[("nrq", p)])
                    tensor.wait_ge(sem.h["SC"], tok[("sqk", p)])
                    for qt in range(NQT):
                        sl = slice(qt * QT, (qt + 1) * QT)
                        tensor.matmul(psB[0:1, sl], ones16[0:64, 0:1],
                                      sqsc[0:64, sl], start=True, stop=True,
                                      skip_group_check=True)
                        ins = tensor.matmul(psA[0:1, sl], ones16[64:128, 0:1],
                                            sqsc[64:128, sl], start=True,
                                            stop=True, tile_position=(64, 0),
                                            skip_group_check=True)
                    tok[("nrmk", p)] = sem.inc(ins, "TE", 1)

                @block.vector
                def _(vector, p=p):
                    vector.wait_ge(sem.h["TE"], tok[("nrmk", p)])
                    vsync(vector, vector.tensor_reduce(browk[0:1, 0:1],
                                                       psB[0:1, :], AX.X, ALU.max))
                    vsync(vector, vector.tensor_reduce(browk[0:1, 1:2],
                                                       psA[0:1, :], AX.X, ALU.max))
                    tok[("nred", p)] = vpub(
                        vector,
                        vector.tensor_tensor(browq[:], browq[:], browk[:],
                                             ALU.mult))

                @block.scalar
                def _(scalar, p=p):
                    scalar.wait_ge(sem.h["VE"], tok[("nred", p)])
                    tok[("brow", p)] = spub(
                        scalar, scalar.activation(browq[:], browq[:], ACT.Sqrt,
                                                  scale=0.0629))

                if tap == "all":
                    @block.sync
                    def _(sync, p=p):
                        sync.wait_ge(sem.h["SC"], tok[("brow", p)])
                        sem.inc(sync.dma_start(out=dbg_b[p, 0:1, :],
                                               in_=browq[:]), "STY", 16)
                        sem.inc(sync.dma_start(out=dbg_b[p, 1:2, :],
                                               in_=browk[:]), "STY", 16)

                @block.gpsimd
                def _(gpsimd, p=p):
                    gpsimd.wait_ge(sem.h["SC"], tok[("brow", p)])
                    if tap == "all":
                        gpsimd.wait_ge(sem.h["STY"], sem.cur("STY"))
                    gpsimd.partition_broadcast(bbc[:, 0:1], browq[0:1, 0:1], 128)
                    tok[("bbc", p)] = sem.inc(
                        gpsimd.partition_broadcast(bbc[:, 1:2], browq[0:1, 1:2],
                                                   128), "GP", 1)

                @block.vector
                def _(vector, p=p):
                    vector.wait_ge(sem.h["GP"], tok[("bbc", p)])
                    vector.wait_ge(sem.h["VE"], tok["ldks"])
                    ins = None
                    for hh in range(2):
                        h = 2 * p + hh
                        vector.tensor_scalar(bfh[:, hh * KC:(hh + 1) * KC],
                                             ldks[:, h * KC:(h + 1) * KC],
                                             bbc[:, hh:hh + 1], None, ALU.subtract)
                        vector.tensor_scalar(ldnb[:, hh:hh + 1],
                                             ldb[:, h:h + 1], -1.0, None,
                                             ALU.mult)
                        ins = vector.tensor_scalar(lqrow[0:1, hh * 4:hh * 4 + 4],
                                                   c_kramp[0:1, 0:KC:4],
                                                   ldnrow[0:1, h:h + 1], None,
                                                   ALU.mult)
                    tok[("bfh", p)] = vpub(vector, ins)

            with nc.psum_tensor(f"scp0_{p}", [128, 2 * QT], F32) as scp0, \
                 nc.psum_tensor(f"scp1_{p}", [128, 2 * QT], F32) as scp1, \
                 nc.psum_tensor(f"opA_{p}", [65, QT], F32) as opA, \
                 nc.psum_tensor(f"ofA_{p}", [65, QT], F32) as ofA, \
                 nc.psum_tensor(f"opB_{p}", [65, QT], F32) as opB, \
                 nc.psum_tensor(f"ofB_{p}", [65, QT], F32) as ofB, \
                 nc.sbuf_tensor(f"on32_{p}", [64, T], F32) as on32, \
                 nc.sbuf_tensor(f"gonh_{p}", [64, T], F16) as gonhi, \
                 nc.sbuf_tensor(f"gonl_{p}", [64, T], F16) as gonlo, \
                 nc.sbuf_tensor(f"gsum_{p}", [64, 4], F32) as gsum:

                scp = [scp0, scp1]
                ophh = [opA, opB]
                ofhh = [ofA, ofB]

                for qt in range(1 if tap == "pb0" else NQT):
                    pv_state = {}
                    for c in range(KC):
                        @block.tensor
                        def _(tensor, p=p, qt=qt, c=c):
                            g = qt * KC + c
                            if g >= 2:
                                gg = g - 2
                                tensor.wait_ge(
                                    sem.h["SC"],
                                    tok[("exp", p, gg // KC, gg % KC)])
                            if g == 0:
                                tensor.wait_ge(sem.h["VE"], tok[("rope", p)])
                                tensor.wait_ge(sem.h["VE"], tok[("nred", p)])
                            gsl = slice(qt * QT, (qt + 1) * QT)
                            par = c % 2
                            ksl = slice(c * 128, (c + 1) * 128)
                            ins = None
                            for hh in range(2):
                                ph = slice(hh * 64, hh * 64 + 64)
                                tp = (hh * 64, 0)
                                dst = scp[par][:, hh * QT:hh * QT + QT]
                                tensor.matmul(dst, khi[ph, ksl], qhi[ph, gsl],
                                              start=True, stop=False,
                                              tile_position=tp,
                                              skip_group_check=True)
                                tensor.matmul(dst, khi[ph, ksl], qlo[ph, gsl],
                                              start=False, stop=False,
                                              tile_position=tp,
                                              skip_group_check=True)
                                ins = tensor.matmul(dst, klo[ph, ksl],
                                                    qhi[ph, gsl],
                                                    start=False, stop=True,
                                                    tile_position=tp,
                                                    skip_group_check=True)
                            tok[("sc", p, qt, c)] = sem.inc(ins, "TE", 1)

                        @block.scalar
                        def _(scalar, p=p, qt=qt, c=c):
                            scalar.wait_ge(sem.h["TE"], tok[("sc", p, qt, c)])
                            if qt == 0 and c == 0:
                                scalar.wait_ge(sem.h["VE"], tok[("bfh", p)])
                                if p > 0:
                                    scalar.wait_ge(sem.h["TE"],
                                                   prev[("pvdone", p - 1)])
                            par = c % 2
                            ins = None
                            for hh in range(2):
                                src = scp[par][:, hh * QT:hh * QT + QT]
                                dst = pbuf[hh][:, c * QT:(c + 1) * QT]
                                ins = scalar.activation(
                                    dst, src, ACT.Exp, scale=0.125,
                                    bias=bfh[:, hh * KC + c:hh * KC + c + 1])
                            tok[("exp", p, qt, c)] = sem.inc(ins, "SC", 1)

                        def emit_strip(c, p=p, qt=qt):
                            @block.vector
                            def _(vector, p=p, qt=qt, c=c):
                                j = c - 4 * qt
                                vector.wait_ge(sem.h["SC"],
                                               tok[("exp", p, qt, c)])
                                if j >= 2:
                                    vector.wait_ge(sem.h["TE"],
                                                   tok[("pvc", p, qt, c - 2)])
                                ins = None
                                for hh in range(2):
                                    pb = pbuf[hh][:, c * QT + j * 128:
                                                  c * QT + j * 128 + 128]
                                    ssl = slice((j % 2) * 256 + hh * 128,
                                                (j % 2) * 256 + hh * 128 + 128)
                                    vector.tensor_tensor(stripL[:, ssl], pb,
                                                         c_L[:], ALU.mult)
                                    ins = vector.tensor_tensor(stripU[:, ssl], pb,
                                                               c_U[:], ALU.mult)
                                tok[("strip", p, qt, c)] = sem.inc(ins, "VE", 1)

                        if c // 4 == qt and not SERIAL_ATTN:
                            emit_strip(c)

                        def emit_pv(cc, p=p, qt=qt, pv_state=pv_state):
                            @block.tensor
                            def _(tensor, p=p, qt=qt, cc=cc):
                                tensor.wait_ge(sem.h["SC"],
                                               tok[("exp", p, qt, cc)])
                                if cc == 0:
                                    tensor.wait_ge(sem.h["VE"], tok[("vscd", p)])
                                    if qt > 0:
                                        tensor.wait_ge(sem.h["VE"],
                                                       tok[("comb", p, qt - 1)])
                                strad = (cc // 4 == qt)
                                if strad:
                                    tensor.wait_ge(sem.h["VE"],
                                                   tok[("strip", p, qt, cc)])
                                ins = None
                                for hh in range(2):
                                    vt = vtm[hh][:, cc * 65:(cc + 1) * 65]
                                    vs = vsc[hh][:, cc * 65:(cc + 1) * 65]
                                    pb = pbuf[hh][:, cc * QT:(cc + 1) * QT]
                                    op = ophh[hh]
                                    of = ofhh[hh]
                                    if cc < 4 * qt:
                                        ins = _mm(tensor, op[:, :], vs, pb,
                                                  ("op", hh), pv_state)
                                    elif cc >= 4 * (qt + 1):
                                        ins = _mm(tensor, of[:, :], vt, pb,
                                                  ("of", hh), pv_state,
                                                  stop=(cc == KC - 1))
                                    else:
                                        j = cc - 4 * qt
                                        w_f = 128 * j
                                        w_p0 = w_f + 128
                                        if w_f > 0:
                                            _mm(tensor, of[:, 0:w_f], vt,
                                                pb[:, 0:w_f], ("of", hh), pv_state)
                                        if QT - w_p0 > 0:
                                            _mm(tensor, op[:, w_p0:QT], vs,
                                                pb[:, w_p0:QT], ("op", hh),
                                                pv_state)
                                        ssl = slice((j % 2) * 256 + hh * 128,
                                                    (j % 2) * 256 + hh * 128 + 128)
                                        _mm(tensor, op[:, w_f:w_p0], vs,
                                            stripL[:, ssl], ("op", hh), pv_state,
                                            stop=True)
                                        ins = _mm(tensor, of[:, w_f:w_p0], vt,
                                                  stripU[:, ssl], ("of", hh),
                                                  pv_state, stop=(cc == KC - 1))
                                tok[("pvc", p, qt, cc)] = sem.inc(ins, "TE", 1)
                                if cc == KC - 1:
                                    tok[("pv", p, qt)] = tok[("pvc", p, qt, cc)]

                        if not SERIAL_ATTN:
                            if c >= 1:
                                emit_pv(c - 1)
                            if c == KC - 1:
                                emit_pv(c)
                    if SERIAL_ATTN:
                        for c in range(KC):
                            if c // 4 == qt:
                                emit_strip(c)
                            emit_pv(c)

                    for hh in range(2):
                        @block.vector
                        def _(vector, p=p, qt=qt, hh=hh):
                            gsl = slice(qt * QT, (qt + 1) * QT)
                            vector.wait_ge(sem.h["VE"], tok[("bfh", p)])
                            tok[("dr", p, qt, hh)] = sem.inc(
                                vector.tensor_scalar(
                                    r1t[:], c_qramp[:],
                                    ldnrow[0:1, 2 * p + hh:2 * p + hh + 1],
                                    lqrow[0:1, hh * 4 + qt:hh * 4 + qt + 1],
                                    ALU.mult, ALU.add), "VE", 1)

                        @block.scalar
                        def _(scalar, p=p, qt=qt, hh=hh):
                            scalar.wait_ge(sem.h["VE"], tok[("dr", p, qt, hh)])
                            tok[("dre", p, qt, hh)] = sem.inc(
                                scalar.activation(r1t[:], r1t[:], ACT.Exp),
                                "SC", 1)

                        @block.gpsimd
                        def _(gpsimd, p=p, qt=qt, hh=hh):
                            gpsimd.wait_ge(sem.h["SC"], tok[("dre", p, qt, hh)])
                            if hh == 1:
                                gpsimd.wait_ge(sem.h["VE"],
                                               tok[("linv", p, qt, 0)])
                            tok[("dbb", p, qt, hh)] = sem.inc(
                                gpsimd.partition_broadcast(dbb1[hh][:], r1t[:],
                                                           65), "GP", 1)

                        @block.vector
                        def _(vector, p=p, qt=qt, hh=hh):
                            vector.wait_ge(sem.h["TE"], tok[("pv", p, qt)])
                            vector.wait_ge(sem.h["GP"], tok[("dbb", p, qt, hh)])
                            vector.tensor_tensor(csc1[:], ofhh[hh][:],
                                                 dbb1[hh][:], ALU.mult)
                            vector.scalar_tensor_tensor(csc1[:], ophh[hh][:],
                                                        1.0, csc1[:], ALU.mult,
                                                        ALU.add)
                            tok[("linv", p, qt, hh)] = vpub(
                                vector, vector.reciprocal(r1t[:], csc1[64:65, :]))

                        if tap == "cs0" and qt == 0 and hh == 0:
                            @block.sync
                            def _(sync, p=p, qt=qt, hh=hh):
                                sync.wait_ge(sem.h["VE"], tok[("linv", p, qt, hh)])
                                sem.inc(sync.dma_start(out=dbg_cs[0:65, :],
                                                       in_=csc1[:]), "STY", 16)
                                sem.inc(sync.dma_start(out=dbg_cs[65:66, :],
                                                       in_=r1t[:]), "STY", 16)

                        @block.gpsimd
                        def _(gpsimd, p=p, qt=qt, hh=hh):
                            gpsimd.wait_ge(sem.h["VE"], tok[("linv", p, qt, hh)])
                            if tap == "cs0" and qt == 0 and hh == 0:
                                gpsimd.wait_ge(sem.h["STY"], sem.cur("STY"))
                            tok[("lbc", p, qt, hh)] = sem.inc(
                                gpsimd.partition_broadcast(linvb[:], r1t[:], 64),
                                "GP", 1)

                        @block.vector
                        def _(vector, p=p, qt=qt, hh=hh):
                            gsl = slice(qt * QT, (qt + 1) * QT)
                            vector.wait_ge(sem.h["GP"], tok[("lbc", p, qt, hh)])
                            tok[("comb", p, qt)] = sem.inc(
                                vector.tensor_tensor(ogt[hh][:, gsl],
                                                     csc1[0:64, :], linvb[:],
                                                     ALU.mult), "VE", 1)

                qq = 0 if tap == "pb0" else NQT - 1
                prev[("pvdone", p)] = tok[("pv", p, qq)]
                prev[("scdone", p)] = tok[("sc", p, qq, KC - 1)]
                prev[("combdone", p)] = tok[("comb", p, qq)]
                prev[("expdone", p)] = tok[("exp", p, qq, KC - 1)]

                if tap in ("pb", "pb0"):
                    @block.sync
                    def _(sync, p=p):
                        qq = 0 if tap == "pb0" else NQT - 1
                        sync.wait_ge(sem.h["SC"], tok[("exp", p, qq, KC - 1)])
                        sem.inc(sync.dma_start(out=dbg_pb[0], in_=pbuf[0]),
                                "STY", 16)
                        sem.inc(sync.dma_start(out=dbg_pb[1], in_=pbuf[1]),
                                "STY", 16)
                        sync.wait_ge(sem.h["STY"], sem.cur("STY"))
                    tap_now[0] = tap
                    break

                if tap in ("og", "cs0"):
                    @block.sync
                    def _(sync, p=p):
                        sync.wait_ge(sem.h["VE"], tok[("comb", p, NQT - 1)])
                        if tap == "og":
                            sem.inc(sync.dma_start(out=dbg_og[0], in_=ogt[0][:]),
                                    "STY", 16)
                            sem.inc(sync.dma_start(out=dbg_og[1], in_=ogt[1][:]),
                                    "STY", 16)
                        sync.wait_ge(sem.h["STY"], sem.cur("STY"))
                    tap_now[0] = tap
                    break

                if tap == "all":
                    @block.sync
                    def _(sync, p=p):
                        sync.wait_ge(sem.h["VE"], tok[("comb", p, NQT - 1)])
                        for hh in range(2):
                            sem.inc(sync.dma_start(out=dbg_ogall[p, hh],
                                                   in_=ogt[hh][:]), "STY", 16)
                            sem.inc(sync.dma_start(out=dbg_gate[p, hh],
                                                   in_=gateh[hh][:]), "STY", 16)

                for hh in range(2):
                    @block.vector
                    def _(vector, p=p, hh=hh):
                        vector.wait_ge(sem.h["VE"], tok[("comb", p, NQT - 1)])
                        tok[("gsum", p, hh)] = vsync(
                            vector, vector.reduce_sum(gsum[:, hh:hh + 1],
                                                      ogt[hh][:], AX.X))
                        vector.tensor_tensor(sqsc[0:64, :], ogt[hh][:],
                                             ogt[hh][:], ALU.mult)
                        tok[("gsq", p, hh)] = (vpub if hh == 1 else vsync)(
                            vector, vector.reduce_sum(gsum[:, 2 + hh:3 + hh],
                                                      sqsc[0:64, :], AX.X))
                prev[("gsq", p)] = tok[("gsq", p, 1)]

                @block.gpsimd
                def _(gpsimd, p=p):
                    gpsimd.wait_ge(sem.h["VE"], tok[("gsq", p, 1)])
                    gpsimd.wait_ge(sem.h["VE"], tok[("gsq", p, 0)])
                    tok[("gmm", p)] = sem.inc(
                        gpsimd.partition_all_reduce(gsum[:], gsum[:], 64,
                                                    ReduceOp.add), "GP", 1)

                @block.vector
                def _(vector, p=p):
                    vector.wait_ge(sem.h["GP"], tok[("gmm", p)])
                    NTOT = float(T * 64)
                    vsync(vector,
                          vector.tensor_scalar(r2t[0:1, 0:4], gsum[0:1, 0:4],
                                               1.0 / NTOT, None, ALU.mult))
                    vsync(vector,
                          vector.tensor_tensor(r2t[0:1, 4:6], r2t[0:1, 0:2],
                                               r2t[0:1, 0:2], ALU.mult))
                    tok[("gvar", p)] = vpub(
                        vector, vector.tensor_tensor(r2t[0:1, 2:4],
                                                     r2t[0:1, 2:4],
                                                     r2t[0:1, 4:6],
                                                     ALU.subtract))


                @block.scalar
                def _(scalar, p=p):
                    scalar.wait_ge(sem.h["VE"], tok[("gvar", p)])
                    tok[("gsqrt", p)] = spub(
                        scalar,
                        scalar.activation(r2t[0:1, 2:4], r2t[0:1, 2:4], ACT.Sqrt,
                                          bias=ctb[0:1, 0:1]))

                @block.vector
                def _(vector, p=p):
                    vector.wait_ge(sem.h["SC"], tok[("gsqrt", p)])
                    tok[("grstd", p)] = vpub(
                        vector, vector.reciprocal(r2t[0:1, 6:8], r2t[0:1, 2:4]))

                if tap == "all":
                    @block.sync
                    def _(sync, p=p):
                        sync.wait_ge(sem.h["VE"], tok[("grstd", p)])
                        sem.inc(sync.dma_start(out=dbg_r2[p], in_=r2t[:]),
                                "STY", 16)

                @block.gpsimd
                def _(gpsimd, p=p):
                    gpsimd.wait_ge(sem.h["VE"], tok[("grstd", p)])
                    ins = None
                    for hh in range(2):
                        gpsimd.partition_broadcast(gnb[:, hh:hh + 1],
                                                   r2t[0:1, hh:hh + 1], 64)
                        ins = gpsimd.partition_broadcast(gnb[:, 2 + hh:3 + hh],
                                                         r2t[0:1, 6 + hh:7 + hh],
                                                         64)
                    tok[("gnb", p)] = sem.inc(ins, "GP", 1)

                for hh in range(2):
                    @block.vector
                    def _(vector, p=p, hh=hh):
                        vector.wait_ge(sem.h["GP"], tok[("gnb", p)])
                        if hh == 0:
                            vector.wait_ge(sem.h["SC"], tok[("gev", p)])
                            if p > 0:
                                vector.wait_ge(sem.h["STG"],
                                               prev[("gonst", p - 1)])
                        else:
                            vector.wait_ge(sem.h["STG"], tok[("gonst", p, 0)])
                        ph = slice(hh * 64, hh * 64 + 64)
                        vsync(vector, vector.tensor_copy(
                            gnc[:, 0:1], p_sb["gamma"][ph, p:p + 1]))
                        vsync(vector, vector.tensor_copy(
                            gnc[:, 1:2], p_sb["beta"][ph, p:p + 1]))
                        vsync(vector, vector.tensor_tensor(
                            gnc[:, 2:3], gnc[:, 0:1],
                            gnb[:, 2 + hh:3 + hh], ALU.mult))
                        vsync(vector, vector.scalar_tensor_tensor(
                            gnc[:, 3:4], gnb[:, hh:hh + 1], 1.0,
                            gnc[:, 2:3], ALU.mult, ALU.mult))
                        vsync(vector, vector.tensor_tensor(
                            gnc[:, 3:4], gnc[:, 3:4],
                            gnc[:, 1:2], ALU.subtract))
                        vector.tensor_scalar(on32[:], ogt[hh][:], gnc[:, 2:3],
                                             gnc[:, 3:4], ALU.mult, ALU.subtract)
                        vector.tensor_tensor(on32[:], on32[:], gateh[hh][:],
                                             ALU.mult)
                        vector.tensor_copy(gonhi[:], on32[:])
                        tok[("gon", p, hh)] = sem.inc(
                            vector.scalar_tensor_tensor(gonlo[:], on32[:], 1.0,
                                                        gonhi[:], ALU.mult,
                                                        ALU.subtract), "VE", 1)

                    @block.sync
                    def _(sync, p=p, hh=hh):
                        sync.wait_ge(sem.h["VE"], tok[("gon", p, hh)])
                        ph = slice(hh * 64, hh * 64 + 64)
                        sem.inc(sync.dma_start(out=gon_sp[0, p, ph],
                                               in_=gonhi[:]), "STG", 16)
                        tok[("gonst", p, hh)] = sem.inc(
                            sync.dma_start(out=gon_sp[1, p, ph], in_=gonlo[:]),
                            "STG", 16)
                if tap == "all":
                    @block.sync
                    def _(sync, p=p):
                        sync.wait_ge(sem.h["STY"], sem.cur("STY"))
                prev[("gonve", p)] = tok[("gon", p, 1)]
                prev[("gonst", p)] = tok[("gonst", p, 1)]

        if tap == "qk":
            @block.sync
            def _(sync):
                sync.wait_ge(sem.h["VE"], tok[("rope", 0)])
                sem.inc(sync.dma_start(out=dbg_qk[0], in_=qhi[:]), "STY", 16)
                sem.inc(sync.dma_start(out=dbg_qk[1], in_=qlo[:]), "STY", 16)
                sem.inc(sync.dma_start(out=dbg_qk[2], in_=khi[:]), "STY", 16)
                sem.inc(sync.dma_start(out=dbg_qk[3], in_=klo[:]), "STY", 16)
                sync.wait_ge(sem.h["STY"], sem.cur("STY"))
            lower_extended_insts(nc)
            return nc

        if tap in ("og", "gon", "pb", "pb0", "cs0"):
            @block.sync
            def _(sync):
                for s in ["STH", "STG", "STY"]:
                    sync.wait_ge(sem.h[s], sem.cur(s))
                sync.wait_ge(sem.h["SC"], sem.cur("SC"))
                sync.wait_ge(sem.h["VE"], sem.cur("VE"))
            lower_extended_insts(nc)
            return nc

        stB.close()

        _k5 = "ExternalOutput" if tap in ("all", "allq") else "Internal"
        xr_sp = nc.dram_tensor("xr_sp", [2, EC, 128, T], F16, kind=_k5).ap()
        xk_sp = nc.dram_tensor("xk_sp", [2, EC, 128, T], F16, kind=_k5).ap()

        with contextlib.ExitStack() as s3:
            sb3 = lambda name, shape, dt=F32: s3.enter_context(
                nc.sbuf_tensor(name, shape, dt))
            gstr = [[sb3(f"gstr{i}_{d}", [128, T], F16) for d in range(2)]
                    for i in range(2)]
            w3 = [sb3(f"w3st{i}", [128, EC * 128]) for i in range(2)]
            w3hi = [sb3(f"w3hi{i}", [128, EC * 128], F16) for i in range(2)]
            w3lo = [sb3(f"w3lo{i}", [128, EC * 128], F16) for i in range(2)]
            xb3 = [sb3(f"xb3_{i}", [128, T]) for i in range(2)]
            x1c = sb3("x1c", [128, T])
            x1hi = sb3("x1hi", [128, T], F16)
            x1lo = sb3("x1lo", [128, T], F16)
            psO = s3.enter_context(nc.psum_tensor("psO", [128, T], F32))
            psO2 = s3.enter_context(nc.psum_tensor("psO2", [128, T], F32))
            psOs = [psO, psO2]

            @block.sync
            def _(sync):
                sync.wait_ge(sem.h["STG"], sem.cur("STG"))
                sync.wait_ge(sem.h["VE"], sem.cur("VE"))
                sync.wait_ge(sem.h["SC"], sem.cur("SC"))
                sync.wait_ge(sem.h["TE"], sem.cur("TE"))
                sync.wait_ge(sem.h["GP"], sem.cur("GP"))
                sem.inc(sync.dma_start(out=w3[0][:], in_=wblk["o"][0]),
                        "LW0", 16)
                tok[("w3", 0)] = sem.cur("LW0")

            def wchain(gi, nxt_src, fmm_key):
                sl = gi % 2

                @block.scalar
                def _(scalar, gi=gi, sl=sl):
                    scalar.wait_ge(sem.h["LW%d" % sl], tok[("w3", gi)])
                    if gi >= 2:
                        scalar.wait_ge(sem.h["TE"], tok[(fmm_key, gi - 2)])
                    tok[("w3hi", gi)] = sem.inc(
                        scalar.activation(w3hi[sl][:], w3[sl][:], ACT.Copy),
                        "SC", 1)

                @block.vector
                def _(vector, gi=gi, sl=sl):
                    vector.wait_ge(sem.h["SC"], tok[("w3hi", gi)])
                    tok[("w3sp", gi)] = sem.inc(
                        vector.scalar_tensor_tensor(
                            w3lo[sl][:], w3[sl][:], 1.0, w3hi[sl][:],
                            ALU.mult, ALU.subtract), "VE", 1)

                @block.sync
                def _(sync, gi=gi, nxt_src=nxt_src):
                    if nxt_src is not None:
                        osl = (gi + 1) % 2
                        if gi >= 1:
                            sync.wait_ge(sem.h["SC"], tok[("w3hi", gi)])
                        sem.inc(sync.dma_start(out=w3[osl][:], in_=nxt_src),
                                "LW%d" % osl, 16)
                        tok[("w3", gi + 1)] = sem.cur("LW%d" % osl)

            for dc in range(EC):
                wchain(dc, wblk["o"][dc + 1] if dc < EC - 1 else None, "p3mm")

                for ec in range(EC):
                    @block.sync
                    def _(sync, dc=dc, ec=ec):
                        gi = dc * EC + ec
                        if gi >= 2:
                            sync.wait_ge(sem.h["TE"], tok[("p3u", gi - 2)])
                        sem.inc(sync.dma_start(out=gstr[ec % 2][0][:],
                                               in_=gon_sp[0, ec]),
                                "LH%d" % (ec % 2), 16)
                        tok[("g3", gi)] = sem.inc(
                            sync.dma_start(out=gstr[ec % 2][1][:],
                                           in_=gon_sp[1, ec]),
                            "LH%d" % (ec % 2), 16)

                    @block.tensor
                    def _(tensor, dc=dc, ec=ec):
                        gi = dc * EC + ec
                        tensor.wait_ge(sem.h["LH%d" % (ec % 2)], tok[("g3", gi)])
                        if ec == 0:
                            tensor.wait_ge(sem.h["VE"], tok[("w3sp", dc)])
                            if dc >= 2:
                                tensor.wait_ge(sem.h["VE"],
                                               tok[("x1ev", dc - 2)])
                        ps = psOs[dc % 2]
                        whi_ = w3hi[dc % 2]
                        wlo_ = w3lo[dc % 2]
                        ins = None
                        first = (ec == 0)
                        for qt in range(NQT):
                            sl = slice(qt * QT, (qt + 1) * QT)
                            tensor.matmul(ps[:, sl], wsl(whi_, ec),
                                          gstr[ec % 2][0][:, sl],
                                          start=first, stop=False)
                            tensor.matmul(ps[:, sl], wsl(whi_, ec),
                                          gstr[ec % 2][1][:, sl],
                                          start=False, stop=False)
                            ins = tensor.matmul(ps[:, sl], wsl(wlo_, ec),
                                                gstr[ec % 2][0][:, sl],
                                                start=False,
                                                stop=(ec == EC - 1))
                        tok[("p3u", gi)] = sem.inc(ins, "TE", 1)
                tok[("p3mm", dc)] = tok[("p3u", dc * EC + EC - 1)]

                @block.sync
                def _(sync, dc=dc):
                    if dc >= 2:
                        sync.wait_ge(sem.h["VE"], tok[("x1c", dc - 2)])
                    tok[("x3", dc)] = sem.inc(
                        sync.dma_start(out=xb3[dc % 2][:],
                                       in_=x_fm[dc * 128:(dc + 1) * 128, :]),
                        "LX%d" % (dc % 2), 16)

                @block.vector
                def _(vector, dc=dc):
                    vector.wait_ge(sem.h["TE"], tok[("p3mm", dc)])
                    vector.wait_ge(sem.h["LX%d" % (dc % 2)], tok[("x3", dc)])
                    if dc >= 1:
                        vector.wait_ge(sem.h["STX"], tok[("x1sp", dc - 1)])
                    vector.tensor_scalar(x1c[:], psOs[dc % 2][:],
                                         p_sb["bo"][:, dc:dc + 1], None, ALU.add)
                    tok[("x1c", dc)] = sem.inc(
                        vector.tensor_tensor(x1c[:], x1c[:], xb3[dc % 2][:],
                                             ALU.add), "VE", 1)

                @block.scalar
                def _(scalar, dc=dc):
                    scalar.wait_ge(sem.h["VE"], tok[("x1c", dc)])
                    tok[("x1hi", dc)] = sem.inc(
                        scalar.activation(x1hi[:], x1c[:], ACT.Copy), "SC", 1)

                @block.vector
                def _(vector, dc=dc):
                    vector.wait_ge(sem.h["SC"], tok[("x1hi", dc)])
                    tok[("x1ev", dc)] = sem.inc(
                        vector.scalar_tensor_tensor(
                            x1lo[:], x1c[:], 1.0, x1hi[:], ALU.mult,
                            ALU.subtract), "VE", 1)

                @block.sync
                def _(sync, dc=dc):
                    sync.wait_ge(sem.h["VE"], tok[("x1ev", dc)])
                    sem.inc(sync.dma_start(out=x1_sp[0, dc], in_=x1hi[:]),
                            "STX", 16)
                    tok[("x1sp", dc)] = sem.inc(
                        sync.dma_start(out=x1_sp[1, dc], in_=x1lo[:]), "STX", 16)

        if tap == "x1":
            @block.sync
            def _(sync):
                sync.wait_ge(sem.h["STX"], sem.cur("STX"))
            lower_extended_insts(nc)
            return nc

        with contextlib.ExitStack() as s4:
            sb4 = lambda name, shape, dt=F32: s4.enter_context(
                nc.sbuf_tensor(name, shape, dt))
            xs4 = [[sb4(f"xs4_{i}_{d}", [128, T], F16) for d in range(2)]
                   for i in range(2)]
            t4 = [sb4(f"t4_{i}", [128, T]) for i in range(2)]
            sq4 = sb4("sq4", [128, T])
            m2b_ = sb4("m2b", [128, T])
            rstd2b_ = sb4("rstd2b", [128, T])

            with nc.sbuf_tensor("m2row", [1, T], F32) as m2row, \
                 nc.sbuf_tensor("rstd2row", [1, T], F32) as rstd2row, \
                 nc.psum_tensor("psS2", [1, T], F32) as psS2, \
                 nc.psum_tensor("psQ2", [1, T], F32) as psQ2:

                for ec in range(EC):
                    @block.sync
                    def _(sync, ec=ec):
                        if ec >= 2:
                            sync.wait_ge(sem.h["TE"], tok[("s4te", ec - 2)])
                        if ec == 0:
                            sync.wait_ge(sem.h["STX"], tok[("x1sp", EC - 1)])
                        sem.inc(sync.dma_start(out=xs4[ec % 2][0][:],
                                               in_=x1_sp[0, ec]),
                                "LH%d" % (ec % 2), 16)
                        tok[("x4", ec)] = sem.inc(
                            sync.dma_start(out=xs4[ec % 2][1][:],
                                           in_=x1_sp[1, ec]),
                            "LH%d" % (ec % 2), 16)

                    @block.vector
                    def _(vector, ec=ec):
                        vector.wait_ge(sem.h["LH%d" % (ec % 2)], tok[("x4", ec)])
                        if ec >= 2:
                            vector.wait_ge(sem.h["TE"], tok[("s4te", ec - 2)])
                        tok[("t4", ec)] = sem.inc(
                            vector.tensor_tensor(t4[ec % 2][:], xs4[ec % 2][0][:],
                                                 xs4[ec % 2][1][:], ALU.add),
                            "VE", 1)

                    @block.scalar
                    def _(scalar, ec=ec):
                        scalar.wait_ge(sem.h["VE"], tok[("t4", ec)])
                        if ec >= 1:
                            scalar.wait_ge(sem.h["TE"], tok[("s4te", ec - 1)])
                        tok[("sq4", ec)] = sem.inc(
                            scalar.activation(sq4[:], t4[ec % 2][:], ACT.Square),
                            "SC", 1)

                    @block.tensor
                    def _(tensor, ec=ec):
                        tensor.wait_ge(sem.h["VE"], tok[("t4", ec)])
                        for qt in range(NQT):
                            tensor.matmul(psS2[0:1, qt * QT:(qt + 1) * QT],
                                          c_ones[:],
                                          t4[ec % 2][:, qt * QT:(qt + 1) * QT],
                                          start=(ec == 0), stop=(ec == EC - 1))
                        tensor.wait_ge(sem.h["SC"], tok[("sq4", ec)])
                        ins = None
                        for qt in range(NQT):
                            ins = tensor.matmul(
                                psQ2[0:1, qt * QT:(qt + 1) * QT], c_ones[:],
                                sq4[:, qt * QT:(qt + 1) * QT],
                                start=(ec == 0), stop=(ec == EC - 1))
                        tok[("s4te", ec)] = sem.inc(ins, "TE", 1)

                @block.vector
                def _(vector):
                    vector.wait_ge(sem.h["TE"], tok[("s4te", EC - 1)])
                    vector.tensor_scalar(m2row[:], psS2[0:1, :], 1.0 / E, None,
                                         ALU.mult)
                    vector.tensor_tensor(rstd2row[:], m2row[:], m2row[:],
                                         ALU.mult)
                    tok["var2"] = sem.inc(
                        vector.scalar_tensor_tensor(rstd2row[:], psQ2[0:1, :],
                                                    1.0 / E, rstd2row[:],
                                                    ALU.mult, ALU.subtract),
                        "VE", 1)

                @block.scalar
                def _(scalar):
                    scalar.wait_ge(sem.h["VE"], tok["var2"])
                    tok["sqrt2"] = sem.inc(
                        scalar.activation(rstd2row[:], rstd2row[:], ACT.Sqrt,
                                          bias=ctb[0:1, 0:1]), "SC", 1)

                @block.vector
                def _(vector):
                    vector.wait_ge(sem.h["SC"], tok["sqrt2"])
                    tok["rstd2"] = vpub(
                        vector, vector.reciprocal(rstd2row[:], rstd2row[:]))

                @block.gpsimd
                def _(gpsimd):
                    gpsimd.wait_ge(sem.h["VE"], tok["rstd2"])
                    gpsimd.partition_broadcast(m2b_[:], m2row[:], 128)
                    tok["bcast2"] = sem.inc(
                        gpsimd.partition_broadcast(rstd2b_[:], rstd2row[:], 128),
                        "GP", 1)

            with nc.sbuf_tensor("g32", [128, T + 1], F32) as g32, \
                 nc.sbuf_tensor("xm32", [128, T], F32) as xm32, \
                 nc.sbuf_tensor("xphi", [128, T], F16) as xphi, \
                 nc.sbuf_tensor("xplo", [128, T], F16) as xplo, \
                 nc.sbuf_tensor("mur2", [128, 8], F32) as mur2, \
                 nc.sbuf_tensor("muk2", [128, 8], F32) as muk2:

                @block.vector
                def _(vector):
                    vector.wait_ge(sem.h["GP"], tok["bcast2"])
                    vector.tensor_scalar(mur2[:], p_sb["mu_r"][:], -1.0, 1.0,
                                         ALU.mult, ALU.add)
                    vector.memset(g32[:, 0:1], 0.0)
                    tok["mu2"] = vpub(
                        vector,
                        vector.tensor_scalar(muk2[:], p_sb["mu_k"][:], -1.0, 1.0,
                                             ALU.mult, ALU.add))

                for ec in range(EC):
                    @block.sync
                    def _(sync, ec=ec):
                        if ec >= 2:
                            sync.wait_ge(sem.h["VE"], tok[("gdone", ec - 2)])
                        sem.inc(sync.dma_start(out=xs4[ec % 2][0][:],
                                               in_=x1_sp[0, ec]),
                                "LH%d" % (ec % 2), 16)
                        tok[("x5", ec)] = sem.inc(
                            sync.dma_start(out=xs4[ec % 2][1][:],
                                           in_=x1_sp[1, ec]),
                            "LH%d" % (ec % 2), 16)

                    for which, mucol, mu2col, spt in (
                            ("xr", p_sb["mu_r"], mur2, xr_sp),
                            ("xk", p_sb["mu_k"], muk2, xk_sp)):
                        @block.vector
                        def _(vector, ec=ec, which=which, mucol=mucol,
                              mu2col=mu2col):
                            if which == "xk":
                                vector.wait_ge(sem.h["STH"],
                                               tok[("xrsp", ec)])
                            if which == "xr":
                                vector.wait_ge(sem.h["LH%d" % (ec % 2)],
                                               tok[("x5", ec)])
                                if ec == 0:
                                    vector.wait_ge(sem.h["VE"], tok["mu2"])
                                if ec >= 1:
                                    vector.wait_ge(sem.h["STH"],
                                                   tok[("xksp", ec - 1)])
                                vector.tensor_tensor(t4[ec % 2][:],
                                                     xs4[ec % 2][0][:],
                                                     xs4[ec % 2][1][:], ALU.add)
                                vector.tensor_tensor(t4[ec % 2][:],
                                                     t4[ec % 2][:], m2b_[:],
                                                     ALU.subtract)
                                vector.tensor_tensor(g32[:, 1:T + 1],
                                                     t4[ec % 2][:], rstd2b_[:],
                                                     ALU.mult)
                            vector.tensor_scalar(xm32[:], g32[:, 1:T + 1],
                                                 mucol[:, ec:ec + 1], None,
                                                 ALU.mult)
                            vector.scalar_tensor_tensor(xm32[:], g32[:, 0:T],
                                                        mu2col[:, ec:ec + 1],
                                                        xm32[:], ALU.mult,
                                                        ALU.add)
                            vector.tensor_copy(xphi[:], xm32[:])
                            tok[(which, ec)] = sem.inc(
                                vector.scalar_tensor_tensor(
                                    xplo[:], xm32[:], 1.0, xphi[:], ALU.mult,
                                    ALU.subtract), "VE", 1)
                            if which == "xk":
                                tok[("gdone", ec)] = tok[("xk", ec)]

                        @block.sync
                        def _(sync, ec=ec, which=which, spt=spt):
                            sync.wait_ge(sem.h["VE"], tok[(which, ec)])
                            sem.inc(sync.dma_start(out=spt[0, ec], in_=xphi[:]),
                                    "STH", 16)
                            tok[(which + "sp", ec)] = sem.inc(
                                sync.dma_start(out=spt[1, ec], in_=xplo[:]),
                                "STH", 16)

            with nc.sbuf_tensor("w6a", [128, EC * 128], F32) as w6a, \
                 nc.sbuf_tensor("w6b", [128, EC * 128], F32) as w6b, \
                 nc.sbuf_tensor("w6ha", [128, EC * 128], F16) as w6ha, \
                 nc.sbuf_tensor("w6hb", [128, EC * 128], F16) as w6hb, \
                 nc.sbuf_tensor("w6la", [128, EC * 128], F16) as w6la, \
                 nc.sbuf_tensor("w6lb", [128, EC * 128], F16) as w6lb, \
                 nc.sbuf_tensor("kahi", [128, EC * T], F16) as _kh, \
                 nc.sbuf_tensor("kalo", [128, EC * T], F16) as _kl, \
                 nc.sbuf_tensor("r32a", [128, T], F32) as r32a, \
                 nc.sbuf_tensor("r32b", [128, T], F32) as r32c, \
                 nc.sbuf_tensor("s6", [128, T], F32) as s6, \
                 nc.sbuf_tensor("y32", [128, T], F32) as y32, \
                 nc.psum_tensor("psF", [128, T], F32) as psF, \
                 nc.psum_tensor("psF2", [128, T], F32) as psF2:

                def kchunk_hi(dc):
                    return _kh[:, dc * T:(dc + 1) * T]

                def kchunk_lo(dc):
                    return _kl[:, dc * T:(dc + 1) * T]

                psFs = [psF, psF2]
                w6 = [w6a, w6b]
                w6h = [w6ha, w6hb]
                w6l = [w6la, w6lb]
                r32s = [r32a, r32c]

                FSRC = ([("r", d) for d in range(EC)]
                        + [("k2", d) for d in range(EC)]
                        + [("v2", d) for d in range(EC)])

                @block.sync
                def _(sync):
                    sync.wait_ge(sem.h["VE"], tok[("gdone", EC - 1)])
                    sem.inc(sync.dma_start(out=w6[0][:], in_=wblk["r"][0]),
                            "LW0", 16)
                    tok[("w6", 0)] = sem.cur("LW0")

                def w6chain(gi):
                    sl = gi % 2

                    @block.scalar
                    def _(scalar, gi=gi, sl=sl):
                        scalar.wait_ge(sem.h["LW%d" % sl], tok[("w6", gi)])
                        if gi >= 2:
                            scalar.wait_ge(sem.h["TE"], tok[("fmm", gi - 2)])
                        tok[("w6hi", gi)] = sem.inc(
                            scalar.activation(w6h[sl][:], w6[sl][:], ACT.Copy),
                            "SC", 1)

                    @block.vector
                    def _(vector, gi=gi, sl=sl):
                        vector.wait_ge(sem.h["SC"], tok[("w6hi", gi)])
                        tok[("w6sp", gi)] = sem.inc(
                            vector.scalar_tensor_tensor(
                                w6l[sl][:], w6[sl][:], 1.0, w6h[sl][:],
                                ALU.mult, ALU.subtract), "VE", 1)

                    @block.sync
                    def _(sync, gi=gi):
                        if gi + 1 < len(FSRC):
                            osl = (gi + 1) % 2
                            sync.wait_ge(sem.h["SC"], tok[("w6hi", gi)])
                            m, d = FSRC[gi + 1]
                            sem.inc(sync.dma_start(out=w6[osl][:],
                                                   in_=wblk[m][d]),
                                    "LW%d" % osl, 16)
                            tok[("w6", gi + 1)] = sem.cur("LW%d" % osl)

                def fmm_streamed(gi, spt):
                    for ec in range(EC):
                        @block.sync
                        def _(sync, gi=gi, ec=ec, spt=spt):
                            si = gi * EC + ec
                            if si >= 2:
                                sync.wait_ge(sem.h["TE"], tok[("f6u", si - 2)])
                            if gi == 0:
                                if si < 2:
                                    sync.wait_ge(sem.h["VE"],
                                                 tok[("xk", EC - 2 + (ec % 2))])
                                sync.wait_ge(sem.h["STH"], tok[("xrsp", ec)])
                            sem.inc(sync.dma_start(out=xs4[ec % 2][0][:],
                                                   in_=spt[0, ec]),
                                    "LH%d" % (ec % 2), 16)
                            tok[("f6l", si)] = sem.inc(
                                sync.dma_start(out=xs4[ec % 2][1][:],
                                               in_=spt[1, ec]),
                                "LH%d" % (ec % 2), 16)

                        @block.tensor
                        def _(tensor, gi=gi, ec=ec):
                            si = gi * EC + ec
                            tensor.wait_ge(sem.h["LH%d" % (ec % 2)],
                                           tok[("f6l", si)])
                            if ec == 0:
                                tensor.wait_ge(sem.h["VE"], tok[("w6sp", gi)])
                                if gi >= 2:
                                    tensor.wait_ge(sem.h["VE"],
                                                   tok[("fev", gi - 2)])
                            ps = psFs[gi % 2]
                            whi_ = w6h[gi % 2]
                            wlo_ = w6l[gi % 2]
                            first = (ec == 0)
                            ins = None
                            for qt in range(NQT):
                                sl = slice(qt * QT, (qt + 1) * QT)
                                tensor.matmul(ps[:, sl], wsl(whi_, ec),
                                              xs4[ec % 2][0][:, sl],
                                              start=first, stop=False)
                                tensor.matmul(ps[:, sl], wsl(whi_, ec),
                                              xs4[ec % 2][1][:, sl],
                                              start=False, stop=False)
                                ins = tensor.matmul(ps[:, sl], wsl(wlo_, ec),
                                                    xs4[ec % 2][0][:, sl],
                                                    start=False,
                                                    stop=(ec == EC - 1))
                            tok[("f6u", si)] = sem.inc(ins, "TE", 1)
                    tok[("fmm", gi)] = tok[("f6u", gi * EC + EC - 1)]

                for dc in range(EC):
                    gi = dc
                    w6chain(gi)
                    fmm_streamed(gi, xr_sp)

                    @block.scalar
                    def _(scalar, dc=dc):
                        scalar.wait_ge(sem.h["TE"], tok[("fmm", dc)])
                        if dc >= 2:
                            scalar.wait_ge(sem.h["STR"], tok[("rsp", dc - 2)])
                        tok[("rev", dc)] = sem.inc(
                            scalar.activation(r32s[dc % 2][:], psFs[dc % 2][:],
                                              ACT.Sigmoid), "SC", 1)

                    @block.vector
                    def _(vector, dc=dc):
                        vector.wait_ge(sem.h["SC"], tok[("rev", dc)])
                        tok[("fev", dc)] = sem.inc(
                            vector.tensor_copy(s6[:, 0:1],
                                               r32s[dc % 2][:, 0:1]), "VE", 1)

                    @block.sync
                    def _(sync, dc=dc):
                        sync.wait_ge(sem.h["SC"], tok[("rev", dc)])
                        tok[("rsp", dc)] = sem.inc(
                            sync.dma_start(out=r_sp[dc], in_=r32s[dc % 2][:]),
                            "STR", 16)

                for dc in range(EC):
                    gi = EC + dc
                    w6chain(gi)
                    fmm_streamed(gi, xk_sp)

                    @block.scalar
                    def _(scalar, dc=dc, gi=gi):
                        scalar.wait_ge(sem.h["TE"], tok[("fmm", gi)])
                        tok[("kev", dc)] = sem.inc(
                            scalar.activation(s6[:], psFs[gi % 2][:], ACT.Relu),
                            "SC", 1)

                    @block.vector
                    def _(vector, dc=dc, gi=gi):
                        vector.wait_ge(sem.h["SC"], tok[("kev", dc)])
                        vector.tensor_tensor(y32[:], s6[:], s6[:], ALU.mult)
                        vector.tensor_copy(kchunk_hi(dc), y32[:])
                        tok[("kact", dc)] = sem.inc(
                            vector.scalar_tensor_tensor(kchunk_lo(dc), y32[:],
                                                        1.0, kchunk_hi(dc),
                                                        ALU.mult, ALU.subtract),
                            "VE", 1)
                        tok[("fev", gi)] = tok[("kact", dc)]

                for dc in range(EC):
                    gi = 2 * EC + dc
                    w6chain(gi)

                    @block.tensor
                    def _(tensor, dc=dc, gi=gi):
                        tensor.wait_ge(sem.h["VE"], tok[("w6sp", gi)])
                        tensor.wait_ge(sem.h["VE"], tok[("fev", gi - 2)])
                        tensor.wait_ge(sem.h["VE"], tok[("kact", EC - 1)])
                        ps = psFs[gi % 2]
                        whi_ = w6h[gi % 2]
                        wlo_ = w6l[gi % 2]
                        ins = None
                        for ec in range(EC):
                            first = (ec == 0)
                            for qt in range(NQT):
                                sl = slice(qt * QT, (qt + 1) * QT)
                                tensor.matmul(ps[:, sl], wsl(whi_, ec),
                                              kchunk_hi(ec)[:, sl],
                                              start=first, stop=False)
                                tensor.matmul(ps[:, sl], wsl(whi_, ec),
                                              kchunk_lo(ec)[:, sl],
                                              start=False, stop=False)
                                ins = tensor.matmul(ps[:, sl], wsl(wlo_, ec),
                                                    kchunk_hi(ec)[:, sl],
                                                    start=False,
                                                    stop=(ec == EC - 1))
                        tok[("fmm", gi)] = sem.inc(ins, "TE", 1)

                    @block.sync
                    def _(sync, dc=dc):
                        if dc >= 2:
                            sync.wait_ge(sem.h["VE"], tok[("yev", dc - 2)])
                        sem.inc(sync.dma_start(out=r32s[dc % 2][:],
                                               in_=r_sp[dc]),
                                "LX%d" % (dc % 2), 16)
                        tok[("r6", dc)] = sem.cur("LX%d" % (dc % 2))
                        sem.inc(sync.dma_start(out=xs4[dc % 2][0][:],
                                               in_=x1_sp[0, dc]),
                                "LH%d" % (dc % 2), 16)
                        tok[("x6", dc)] = sem.inc(
                            sync.dma_start(out=xs4[dc % 2][1][:],
                                           in_=x1_sp[1, dc]),
                            "LH%d" % (dc % 2), 16)

                    @block.vector
                    def _(vector, dc=dc, gi=gi):
                        vector.wait_ge(sem.h["TE"], tok[("fmm", gi)])
                        vector.wait_ge(sem.h["LX%d" % (dc % 2)], tok[("r6", dc)])
                        vector.wait_ge(sem.h["LH%d" % (dc % 2)], tok[("x6", dc)])
                        if dc >= 1:
                            vector.wait_ge(sem.h["STY"], tok[("ysp", dc - 1)])
                        vector.tensor_tensor(s6[:], xs4[dc % 2][0][:],
                                             xs4[dc % 2][1][:], ALU.add)
                        vector.scalar_tensor_tensor(y32[:], psFs[gi % 2][:], 1.0,
                                                    r32s[dc % 2][:], ALU.mult,
                                                    ALU.mult)
                        tok[("yev", dc)] = sem.inc(
                            vector.tensor_tensor(y32[:], y32[:], s6[:], ALU.add),
                            "VE", 1)
                        tok[("fev", gi)] = tok[("yev", dc)]

                    @block.sync
                    def _(sync, dc=dc):
                        sync.wait_ge(sem.h["VE"], tok[("yev", dc)])
                        tok[("ysp", dc)] = sem.inc(
                            sync.dma_start(out=y[dc * 128:(dc + 1) * 128, :],
                                           in_=y32[:]), "STY", 16)

                @block.sync
                def _(sync):
                    sync.wait_ge(sem.h["STY"], sem.cur("STY"))
                    sync.wait_ge(sem.h["STR"], sem.cur("STR"))
    lower_extended_insts(nc)
    return nc


_BUILD_CACHE = {}


def _get_nc(tap=None):
    if tap not in _BUILD_CACHE:
        _BUILD_CACHE[tap] = build(tap)
    return _BUILD_CACHE[tap]


def _stage_inputs(kw):
    x = np.asarray(kw["L0_attn_x"])
    B = x.shape[0]
    xt = np.ascontiguousarray(x.transpose(0, 2, 1))
    c = _host_constants()
    base = {("c_" + k): v for k, v in c.items()}
    for name, key in [("bq", "L0_attn_bq"), ("bk", "L0_attn_bk"),
                      ("bv", "L0_attn_bv"), ("bg", "L0_attn_bg"),
                      ("bo", "L0_attn_bo"), ("gamma", "L0_attn_gamma"),
                      ("beta", "L0_attn_beta"), ("mu_r", "L0_ffn_mu_r"),
                      ("mu_k", "L0_ffn_mu_k")]:
        base["p_" + name] = _col8(np.asarray(kw[key]))
    base["p_bd"] = np.ascontiguousarray(np.asarray(kw["L0_attn_bd"])[None, :])
    base["w_dT"] = np.ascontiguousarray(
        np.asarray(kw["L0_attn_wd"]).T.reshape(EC, 128, H))
    for m, key in [("q", "L0_attn_wq"), ("k", "L0_attn_wk"), ("v", "L0_attn_wv"),
                   ("g", "L0_attn_wg"), ("o", "L0_attn_wo"), ("r", "L0_ffn_W_r"),
                   ("k2", "L0_ffn_W_k"), ("v2", "L0_ffn_W_v")]:
        base["w_" + m] = _blk_pair(np.ascontiguousarray(np.asarray(kw[key]).T))
    in_maps = []
    for b in range(B):
        d = dict(base)
        d["x_fm"] = xt[b]
        in_maps.append(d)
    return in_maps


def kernel(**kw):
    in_maps = _stage_inputs(kw)
    nc = _get_nc(None)
    res = run_bass_kernel_spmd(nc, in_maps, core_ids=list(range(8)))
    out = np.stack([np.asarray(r["y"]).T for r in res.results], axis=0)
    return out.astype(np.float32)


# version 10
# speedup vs baseline: 1.0327x; 1.0327x over previous
import contextlib
import math

import numpy as np
import concourse.bass as bass
import concourse.mybir as mybir
from concourse.bass_utils import run_bass_kernel_spmd
from concourse.bass_isa import ReduceOp
from concourse.library_config import attn as _attn_lib
from concourse.library_overlay import lower_extended_insts

F32 = mybir.dt.float32
F16 = mybir.dt.float16
ALU = mybir.AluOpType
ACT = mybir.ActivationFunctionType
AX = mybir.AxisListType

T = 2048
E = 1024
H = 16
HD = 64
EC = 8
NPAIR = 8
KC = 16
QT = 512
NQT = T // QT
LN2_ = math.log(2.0)
WQ, WK, WV, WG = 0, 1, 2, 3
SERIAL_ATTN = False


def _host_constants():
    t = np.arange(T, dtype=np.float64)
    j = np.arange(HD)
    inv = 1.0 / (10000.0 ** ((2.0 * (j % 32)) / HD))
    ang = t[None, :] * inv[:, None]
    cos = np.cos(ang)
    sin = np.sin(ang)
    sgn = np.where(j % 2 == 0, -1.0, 1.0)
    sin_s = sin * sgn[:, None]
    cos2 = np.concatenate([cos, cos], axis=0).astype(np.float32)
    sin2 = np.concatenate([sin_s, sin_s], axis=0).astype(np.float32)
    ii = np.arange(128)
    U = (ii[:, None] > ii[None, :]).astype(np.float32)
    L = (ii[:, None] <= ii[None, :]).astype(np.float32)
    kramp = (ii[:, None] + 128.0 * np.arange(KC)[None, :]).astype(np.float32)
    qramp = np.arange(QT, dtype=np.float32)[None, :]
    ident = np.eye(128, dtype=np.float32)
    ones_col = np.ones((128, 1), dtype=np.float32)
    return dict(cos2=cos2, sin2=sin2, U=U, L=L, kramp=kramp, qramp=qramp,
                ident=ident, ones_col=ones_col)


def _blk_pair(wt):
    return np.ascontiguousarray(wt.reshape(EC, 128, EC, 128).transpose(2, 1, 0, 3))


def _col8(v):
    return np.ascontiguousarray(v.reshape(8, 128).T)


class _Sems:
    def __init__(self, nc, stack, names):
        self.h = {}
        self.n = {}
        for name in names:
            self.h[name] = stack.enter_context(nc.semaphore(name))
            self.n[name] = 0

    def inc(self, ins, name, k=1):
        ins.then_inc(self.h[name], k)
        self.n[name] += k
        return self.n[name]

    def cur(self, name):
        return self.n[name]


def build(tap=None):
    nc = bass.Bass("TRN2", target_bir_lowering=False,
                   detect_race_conditions=False)

    x_fm = nc.dram_tensor("x_fm", [E, T], F32, kind="ExternalInput").ap()
    consts = {
        "cos2": [128, T], "sin2": [128, T], "U": [128, 128], "L": [128, 128],
        "kramp": [128, KC], "qramp": [1, QT], "ident": [128, 128],
        "ones_col": [128, 1],
    }
    cst = {k: nc.dram_tensor("c_" + k, sh, F32, kind="ExternalInput").ap()
           for k, sh in consts.items()}
    pnames = ["bq", "bk", "bv", "bg", "bo", "gamma", "beta", "mu_r", "mu_k"]
    par = {k: nc.dram_tensor("p_" + k, [128, 8], F32, kind="ExternalInput").ap()
           for k in pnames}
    bd = nc.dram_tensor("p_bd", [1, H], F32, kind="ExternalInput").ap()
    wdT = nc.dram_tensor("w_dT", [EC, 128, H], F32, kind="ExternalInput").ap()
    wblk = {m: nc.dram_tensor("w_" + m, [8, 128, EC, 128], F32,
                              kind="ExternalInput").ap()
            for m in ["q", "k", "v", "g", "o", "r", "k2", "v2"]}

    def sp16(name, tapped):
        kind = "ExternalOutput" if tapped else "Internal"
        return nc.dram_tensor(name, [2, EC, 128, T], F16, kind=kind).ap()

    h_sp = sp16("h_sp", tap in ("h", "all", "allq"))
    gon_sp = sp16("gon_sp", tap in ("gon", "all", "allq"))
    x1_sp = sp16("x1_sp", tap in ("x1", "all", "allq"))
    r_sp = nc.dram_tensor("r_sp", [EC, 128, T], F32,
                          kind="ExternalOutput" if tap in ("r", "all", "allq")
                          else "Internal").ap()
    if tap == "qk":
        dbg_qk = nc.dram_tensor("dbg_qk", [4, 128, T], F16,
                                kind="ExternalOutput").ap()
    if tap == "og":
        dbg_og = nc.dram_tensor("dbg_og", [2, 64, T], F32,
                                kind="ExternalOutput").ap()
    if tap in ("pb", "pb0"):
        dbg_pb = nc.dram_tensor("dbg_pb", [2, 128, KC * QT], F32,
                                kind="ExternalOutput").ap()
    if tap == "cs0":
        dbg_cs = nc.dram_tensor("dbg_cs", [66, QT], F32,
                                kind="ExternalOutput").ap()
    if tap in ("ld", "hm"):
        dbg_ld = nc.dram_tensor("dbg_ld", [4, 1, H], F32,
                                kind="ExternalOutput").ap()
        dbg_hm = nc.dram_tensor("dbg_hm", [128, EC], F32,
                                kind="ExternalOutput").ap()
        dbg_hs = nc.dram_tensor("dbg_hs", [2, 128, EC], F32,
                                kind="ExternalOutput").ap()
    y = nc.dram_tensor("y", [E, T], F32, kind="ExternalOutput").ap()
    if tap == "all":
        dbg_b = nc.dram_tensor("dbg_b", [NPAIR, 4, 2], F32,
                               kind="ExternalOutput").ap()
        dbg_ogall = nc.dram_tensor("dbg_ogall", [NPAIR, 2, 64, T], F32,
                                   kind="ExternalOutput").ap()
        dbg_gate = nc.dram_tensor("dbg_gate", [NPAIR, 2, 64, T], F32,
                                  kind="ExternalOutput").ap()
        dbg_r2 = nc.dram_tensor("dbg_r2", [NPAIR, 1, 16], F32,
                                kind="ExternalOutput").ap()

    TAP_PAIRS = 0 if tap in ("ld", "hm") else (1 if tap in ("qk", "og", "pb", "pb0", "cs0") else NPAIR)
    tap_now = [None]

    with contextlib.ExitStack() as st:
        sem = _Sems(nc, st, ["LD", "TE", "SC", "VE", "GP",
                             "LX0", "LX1", "LCS", "LW0", "LW1", "LH0", "LH1",
                             "STH", "STG", "STX", "STR", "STY"])
        sb = lambda name, shape, dt=F32: st.enter_context(
            nc.sbuf_tensor(name, shape, dt))
        block = st.enter_context(nc.Block())

        c_U = sb("sb_U", [128, 128])
        c_L = sb("sb_L", [128, 128])
        c_kramp = sb("sb_kramp", [128, KC])
        c_qramp = sb("sb_qramp", [1, QT])
        c_ident = sb("sb_ident", [128, 128])
        c_ones = sb("sb_ones", [128, 1])
        ones16 = sb("sb_ones16", [128, 1], F16)
        p_sb = {k: sb("p" + k, [128, 8]) for k in pnames}
        p_bd = sb("pbd", [1, H])
        pdrow = sb("pdrow", [1, H])
        ldrow = sb("ldrow", [1, H])
        ldnrow = sb("ldnrow", [1, H])
        hrow = sb("hrow", [1, H])
        ldb = sb("ldb", [128, H])
        ldks = sb("ldks", [128, H * KC])
        eks = sb("eks", [128, H * KC])
        hsum = sb("hsum", [128, EC])
        hsum2 = sb("hsum2", [128, EC])
        hmean = sb("hmean", [128, EC])
        browq = sb("browq", [1, 2])
        browk = sb("browk", [1, 2])
        bbc = sb("bbc", [128, 2])
        ldnb = sb("ldnb", [128, 2])
        bfh = sb("bfh", [128, 2 * KC])
        gnb = sb("gnb", [64, 4])
        gnc = sb("gnc", [64, 4])
        r2t = sb("r2t", [1, 16])
        lqrow = sb("lqrow", [1, 8])
        ctb = sb("ctb", [128, 2])
        flsh = sb("flsh", [8, 1024])

        tok = {}
        prev = {}
        stB = None

        def vsync(vector, ins):
            t = sem.inc(ins, "VE", 1)
            vector.wait_ge(sem.h["VE"], t)
            return t

        def vpub(vector, ins):
            t = sem.inc(ins, "VE", 1)
            vector.wait_ge(sem.h["VE"], t)
            return sem.inc(vector.tensor_copy(flsh[:, 512:1024],
                                              flsh[:, 0:512]), "VE", 1)

        def spub(scalar, ins):
            t = sem.inc(ins, "SC", 1)
            scalar.wait_ge(sem.h["SC"], t)
            return sem.inc(scalar.activation(flsh[:, 0:512],
                                             flsh[:, 512:1024], ACT.Identity),
                           "SC", 1)

        with nc.sbuf_tensor("xb0", [128, T], F32) as xb0, \
             nc.sbuf_tensor("xb1", [128, T], F32) as xb1, \
             nc.sbuf_tensor("sq0", [128, T], F32) as sq0, \
             nc.sbuf_tensor("sq1", [128, T], F32) as sq1, \
             nc.sbuf_tensor("m1row", [1, T], F32) as m1row, \
             nc.sbuf_tensor("rstd1row", [1, T], F32) as rstd1row, \
             nc.sbuf_tensor("m1b", [128, T], F32) as m1b, \
             nc.sbuf_tensor("rstd1b", [128, T], F32) as rstd1b, \
             nc.sbuf_tensor("hhib", [128, T], F16) as hhib, \
             nc.sbuf_tensor("hlob", [128, T], F16) as hlob, \
             nc.sbuf_tensor("wdTs", [128, EC * H], F32) as w_dT, \
             nc.psum_tensor("psS", [1, T], F32) as psS, \
             nc.psum_tensor("psQ", [1, T], F32) as psQ:

            xb = [xb0, xb1]
            sq = [sq0, sq1]
            t1b = sq0
            h32b = sq1
            vrow = psS[0:1, T // 2:T]

            @block.sync
            def _(sync):
                for k, tile in [("U", c_U), ("L", c_L), ("kramp", c_kramp),
                                ("qramp", c_qramp), ("ident", c_ident),
                                ("ones_col", c_ones)]:
                    sem.inc(sync.dma_start(out=tile[:], in_=cst[k]), "LD", 16)
                for k in pnames:
                    sem.inc(sync.dma_start(out=p_sb[k][:], in_=par[k]), "LD", 16)
                sem.inc(sync.dma_start(out=p_bd[:], in_=bd), "LD", 16)
                for ec in range(EC):
                    sem.inc(sync.dma_start(out=w_dT[:, ec * H:(ec + 1) * H],
                                           in_=wdT[ec]), "LD", 16)
                tok["consts"] = sem.cur("LD")

            @block.gpsimd
            def _(gpsimd):
                gpsimd.load_library(_attn_lib)
                sem.inc(gpsimd.engine_nop(), "GP", 1)

            @block.vector
            def _(vector):
                vector.wait_ge(sem.h["LD"], tok["consts"])
                vector.memset(ones16[:], 1.0)
                vector.memset(flsh[:], 0.0)
                vector.memset(ctb[:, 0:1], 1e-5)
                sem.inc(vector.memset(ctb[:, 1:2], -5.0 * LN2_), "VE", 1)

            for ec in range(EC):
                @block.sync
                def _(sync, ec=ec):
                    if ec >= 2:
                        sync.wait_ge(sem.h["TE"], tok[("st_te", ec - 2)])
                    tok[("x0", ec)] = sem.inc(
                        sync.dma_start(out=xb[ec % 2][:],
                                       in_=x_fm[ec * 128:(ec + 1) * 128, :]),
                        "LX%d" % (ec % 2), 16)

                @block.scalar
                def _(scalar, ec=ec):
                    scalar.wait_ge(sem.h["LX%d" % (ec % 2)], tok[("x0", ec)])
                    if ec >= 2:
                        scalar.wait_ge(sem.h["TE"], tok[("st_te", ec - 2)])
                    tok[("sq", ec)] = sem.inc(
                        scalar.activation(sq[ec % 2][:], xb[ec % 2][:], ACT.Square),
                        "SC", 1)

                @block.tensor
                def _(tensor, ec=ec):
                    tensor.wait_ge(sem.h["LX%d" % (ec % 2)], tok[("x0", ec)])
                    if ec == 0:
                        tensor.wait_ge(sem.h["LD"], tok["consts"])
                    for qt in range(NQT):
                        tensor.matmul(psS[0:1, qt * QT:(qt + 1) * QT], c_ones[:],
                                      xb[ec % 2][:, qt * QT:(qt + 1) * QT],
                                      start=(ec == 0), stop=(ec == EC - 1))
                    tensor.wait_ge(sem.h["SC"], tok[("sq", ec)])
                    ins = None
                    for qt in range(NQT):
                        ins = tensor.matmul(psQ[0:1, qt * QT:(qt + 1) * QT],
                                            c_ones[:],
                                            sq[ec % 2][:, qt * QT:(qt + 1) * QT],
                                            start=(ec == 0), stop=(ec == EC - 1))
                    tok[("st_te", ec)] = sem.inc(ins, "TE", 1)

            @block.vector
            def _(vector):
                vector.wait_ge(sem.h["TE"], tok[("st_te", EC - 1)])
                vector.tensor_scalar(m1row[:], psS[0:1, :], 1.0 / E, None, ALU.mult)
                vector.tensor_tensor(rstd1row[:], m1row[:], m1row[:], ALU.mult)
                tok["var"] = sem.inc(
                    vector.scalar_tensor_tensor(rstd1row[:], psQ[0:1, :], 1.0 / E,
                                                rstd1row[:], ALU.mult,
                                                ALU.subtract), "VE", 1)

            @block.scalar
            def _(scalar):
                scalar.wait_ge(sem.h["VE"], tok["var"])
                tok["sqrt"] = sem.inc(
                    scalar.activation(rstd1row[:], rstd1row[:], ACT.Sqrt,
                                      bias=ctb[0:1, 0:1]), "SC", 1)

            @block.vector
            def _(vector):
                vector.wait_ge(sem.h["SC"], tok["sqrt"])
                tok["rstd"] = vpub(
                    vector, vector.reciprocal(rstd1row[:], rstd1row[:]))

            @block.gpsimd
            def _(gpsimd):
                gpsimd.wait_ge(sem.h["VE"], tok["rstd"])
                gpsimd.partition_broadcast(m1b[:], m1row[:], 128)
                tok["bcast"] = sem.inc(
                    gpsimd.partition_broadcast(rstd1b[:], rstd1row[:], 128), "GP", 1)

            for ec in range(EC):
                @block.sync
                def _(sync, ec=ec):
                    if ec >= 2:
                        sync.wait_ge(sem.h["VE"], tok[("h32", ec - 2)])
                    else:
                        sync.wait_ge(sem.h["SC"], tok[("sq", EC - 1)])
                        sync.wait_ge(sem.h["TE"], tok[("st_te", EC - 1)])
                    tok[("x1", ec)] = sem.inc(
                        sync.dma_start(out=xb[ec % 2][:],
                                       in_=x_fm[ec * 128:(ec + 1) * 128, :]),
                        "LX%d" % (ec % 2), 16)

                @block.vector
                def _(vector, ec=ec):
                    vector.wait_ge(sem.h["LX%d" % (ec % 2)], tok[("x1", ec)])
                    if ec == 0:
                        vector.wait_ge(sem.h["GP"], tok["bcast"])
                    if ec >= 1:
                        vector.wait_ge(sem.h["SC"], tok[("hhi", ec - 1)])
                    vector.tensor_tensor(t1b[:], xb[ec % 2][:], m1b[:], ALU.subtract)
                    tok[("h32", ec)] = sem.inc(
                        vector.tensor_tensor(h32b[:], t1b[:], rstd1b[:], ALU.mult),
                        "VE", 1)

                @block.scalar
                def _(scalar, ec=ec):
                    scalar.wait_ge(sem.h["VE"], tok[("h32", ec)])
                    if ec >= 1:
                        scalar.wait_ge(sem.h["STH"], tok[("hsp", ec - 1)])
                    tok[("hhi", ec)] = sem.inc(
                        scalar.activation(hhib[:], h32b[:], ACT.Copy), "SC", 1)

                @block.vector
                def _(vector, ec=ec):
                    vector.wait_ge(sem.h["SC"], tok[("hhi", ec)])
                    vector.scalar_tensor_tensor(
                        hlob[:], h32b[:], 1.0, hhib[:], ALU.mult, ALU.subtract)
                    vsync(vector, vector.reduce_sum(hsum[:, ec:ec + 1], hhib[:],
                                                    AX.X))
                    tok[("hlo", ec)] = vsync(
                        vector, vector.reduce_sum(hsum2[:, ec:ec + 1], hlob[:],
                                                  AX.X))

                @block.sync
                def _(sync, ec=ec):
                    sync.wait_ge(sem.h["VE"], tok[("hlo", ec)])
                    sem.inc(sync.dma_start(out=h_sp[0, ec], in_=hhib[:]), "STH", 16)
                    tok[("hsp", ec)] = sem.inc(
                        sync.dma_start(out=h_sp[1, ec], in_=hlob[:]), "STH", 16)

            @block.vector
            def _(vector):
                vector.wait_ge(sem.h["VE"], tok[("hlo", EC - 1)])
                tok["hmean"] = vpub(
                    vector,
                    vector.tensor_tensor(hmean[:], hsum[:], hsum2[:], ALU.add))

            if tap == "hm":
                @block.sync
                def _(sync):
                    sync.wait_ge(sem.h["VE"], tok["hmean"])
                    sem.inc(sync.dma_start(out=dbg_hm, in_=hmean[:]), "STY", 16)
                    sem.inc(sync.dma_start(out=dbg_hs[0], in_=hsum[:]), "STY", 16)
                    sem.inc(sync.dma_start(out=dbg_hs[1], in_=hsum2[:]), "STY", 16)
                    sync.wait_ge(sem.h["STY"], sem.cur("STY"))
                    sync.wait_ge(sem.h["STH"], sem.cur("STH"))

            @block.tensor
            def _(tensor):
                tensor.wait_ge(sem.h["VE"], tok["hmean"])
                ins = None
                for ec in range(EC):
                    ins = tensor.matmul(psS[0:1, 0:H], hmean[:, ec:ec + 1],
                                        w_dT[:, ec * H:(ec + 1) * H],
                                        start=(ec == 0), stop=(ec == EC - 1),
                                        skip_group_check=True)
                tok["pdmm"] = sem.inc(ins, "TE", 1)

            @block.vector
            def _(vector):
                vector.wait_ge(sem.h["TE"], tok["pdmm"])
                tok["pdl"] = vpub(
                    vector,
                    vector.scalar_tensor_tensor(pdrow[:], psS[0:1, 0:H], 1.0 / T,
                                                p_bd[:], ALU.mult, ALU.add))

            @block.scalar
            def _(scalar):
                scalar.wait_ge(sem.h["VE"], tok["pdl"])
                ts = sem.inc(scalar.activation(pdrow[:], pdrow[:], ACT.Sigmoid),
                             "SC", 1)
                scalar.wait_ge(sem.h["SC"], ts)
                tok["pd"] = spub(
                    scalar,
                    scalar.activation(hrow[:], pdrow[:], ACT.Exp,
                                      bias=ctb[0:1, 1:2], scale=-8.0 * LN2_))

            @block.vector
            def _(vector):
                vector.wait_ge(sem.h["SC"], tok["pd"])
                tok["w1"] = vpub(
                    vector,
                    vector.tensor_scalar(ldnrow[:], hrow[:], -1.0, 1.0, ALU.mult,
                                         ALU.add))

            @block.scalar
            def _(scalar):
                scalar.wait_ge(sem.h["VE"], tok["w1"])
                tok["ld"] = spub(
                    scalar, scalar.activation(ldrow[:], ldnrow[:], ACT.Ln))

            @block.vector
            def _(vector):
                vector.wait_ge(sem.h["SC"], tok["ld"])
                tok["ldn"] = sem.inc(
                    vector.tensor_scalar(ldnrow[:], ldrow[:], -1.0, None, ALU.mult),
                    "VE", 1)

            @block.gpsimd
            def _(gpsimd):
                gpsimd.wait_ge(sem.h["VE"], tok["ldn"])
                ins = None
                for h in range(H):
                    ins = gpsimd.partition_broadcast(ldb[:, h:h + 1],
                                                     ldrow[0:1, h:h + 1], 128)
                tok["ldb"] = sem.inc(ins, "GP", 1)

            @block.vector
            def _(vector):
                vector.wait_ge(sem.h["GP"], tok["ldb"])
                ins = None
                for h in range(H):
                    ins = vector.tensor_scalar(ldks[:, h * KC:(h + 1) * KC],
                                               c_kramp[:], ldb[:, h:h + 1],
                                               None, ALU.mult)
                tok["ldks"] = sem.inc(ins, "VE", 1)

            @block.scalar
            def _(scalar):
                scalar.wait_ge(sem.h["VE"], tok["ldks"])
                tok["eks"] = sem.inc(
                    scalar.activation(eks[:], ldks[:], ACT.Exp, scale=-1.0), "SC", 1)

        if tap == "h":
            @block.sync
            def _(sync):
                sync.wait_ge(sem.h["STH"], sem.cur("STH"))
                sync.wait_ge(sem.h["SC"], sem.cur("SC"))
            lower_extended_insts(nc)
            return nc

        if tap == "hm":
            lower_extended_insts(nc)
            return nc

        if tap == "ld":
            @block.sync
            def _(sync):
                sync.wait_ge(sem.h["SC"], tok["eks"])
                sem.inc(sync.dma_start(out=dbg_ld[0], in_=pdrow[:]), "STY", 16)
                sem.inc(sync.dma_start(out=dbg_ld[1], in_=hrow[:]), "STY", 16)
                sem.inc(sync.dma_start(out=dbg_ld[2], in_=ldrow[:]), "STY", 16)
                sem.inc(sync.dma_start(out=dbg_ld[3], in_=ldnrow[:]), "STY", 16)
                sem.inc(sync.dma_start(out=dbg_hm, in_=hmean[:]), "STY", 16)
                sem.inc(sync.dma_start(out=dbg_hs[0], in_=hsum[:]), "STY", 16)
                sem.inc(sync.dma_start(out=dbg_hs[1], in_=hsum2[:]), "STY", 16)
                sync.wait_ge(sem.h["STY"], sem.cur("STY"))
                sync.wait_ge(sem.h["STH"], sem.cur("STH"))
            lower_extended_insts(nc)
            return nc

        stB = st.enter_context(contextlib.ExitStack())
        sbB = lambda name, shape, dt=F32: stB.enter_context(
            nc.sbuf_tensor(name, shape, dt))
        arena = sbB("arena", [128, 2 * KC * QT])
        wst_ = sbB("wst0", [128, EC * 128])
        wst = [wst_, wst_]
        whi = [sbB(f"whi{i}", [128, EC * 128], F16) for i in range(4)]
        wlo = [sbB(f"wlo{i}", [128, EC * 128], F16) for i in range(4)]
        hstr = [[sbB(f"hstr{i}_{d}", [128, T], F16) for d in range(2)]
                for i in range(2)]
        qhi = sbB("qhi", [128, T], F16)
        qlo = sbB("qlo", [128, T], F16)
        khi = sbB("khi", [128, T], F16)
        klo = sbB("klo", [128, T], F16)
        vtm = [sbB(f"vtm{hh}", [128, KC * 65]) for hh in range(2)]
        vsc = [sbB(f"vsc{hh}", [128, KC * 65]) for hh in range(2)]
        gateh = [sbB(f"gateh{hh}", [64, T]) for hh in range(2)]
        ogt = [sbB(f"ogt{hh}", [64, T]) for hh in range(2)]
        stripL = sbB("stripL", [128, 2 * 2 * 128])
        stripU = sbB("stripU", [128, 2 * 2 * 128])
        csc1 = sbB("csc1", [65, QT])
        linvb = sbB("linvb", [64, QT])
        dbb1v = sbB("dbb1", [65, QT])
        dbb1 = [dbb1v, dbb1v]
        r1t = sbB("r1t", [1, QT])
        sqsc = sbB("sqsc", [128, T], F16)

        cosv = arena[:, 0:T]
        sinv = arena[:, T:2 * T]
        pbuf = [arena[:, 0:KC * QT], arena[:, KC * QT:2 * KC * QT]]
        shuf_mask = [i + 1 if i % 2 == 0 else i - 1 for i in range(32)]

        def wsl(buf, ec):
            return buf[:, ec * 128:(ec + 1) * 128]

        def _mm(tensor, dst, lhsT, rhs, key, state, stop=False):
            first = key not in state
            state[key] = True
            return tensor.matmul(dst, lhsT, rhs, start=first, stop=stop,
                                 skip_group_check=True)

        def emit_wchain(pp):
            wm = [("q", ("hu1", pp - 1, EC - 1)), ("k", ("hu1", pp - 1, EC - 1)),
                  ("v", ("hu2", pp - 1, EC - 1)), ("g", ("hu2", pp - 1, EC - 1))]
            for i, (m, conflict) in enumerate(wm):
                @block.sync
                def _(sync, pp=pp, i=i, m=m):
                    guard = ("wsp", pp, i - 1) if i >= 1 else \
                        ("wsp", pp - 1, 3)
                    if guard in tok:
                        sync.wait_ge(sem.h["VE"], tok[guard])
                    tok[("w32", pp, i)] = sem.inc(
                        sync.dma_start(out=wst[i % 2][:], in_=wblk[m][pp]),
                        "LW%d" % (i % 2), 16)

                @block.scalar
                def _(scalar, pp=pp, i=i, conflict=conflict):
                    scalar.wait_ge(sem.h["LW%d" % (i % 2)], tok[("w32", pp, i)])
                    if pp > 0:
                        scalar.wait_ge(sem.h["TE"], tok[conflict])
                    tok[("whi", pp, i)] = sem.inc(
                        scalar.activation(whi[i][:], wst[i % 2][:], ACT.Copy),
                        "SC", 1)

                @block.vector
                def _(vector, pp=pp, i=i):
                    vector.wait_ge(sem.h["SC"], tok[("whi", pp, i)])
                    tok[("wsp", pp, i)] = sem.inc(
                        vector.scalar_tensor_tensor(
                            wlo[i][:], wst[i % 2][:], 1.0, whi[i][:],
                            ALU.mult, ALU.subtract), "VE", 1)

        emit_wchain(0)

        for p in range(TAP_PAIRS):
            with nc.psum_tensor(f"psA_{p}", [128, T], F32) as psA, \
                 nc.psum_tensor(f"psB_{p}", [128, T], F32) as psB, \
                 nc.sbuf_tensor(f"q32_{p}", [128, T], F32) as q32:

                s1 = arena[:, 2 * T:3 * T]

                def wbuf_hi(i):
                    return whi[i]

                def wbuf_lo(i):
                    return wlo[i]

                @block.gpsimd
                def _(gpsimd, p=p):
                    if p > 0:
                        gpsimd.wait_ge(sem.h["TE"], prev[("pvdone", p - 1)])
                    sem.inc(gpsimd.dma_start(out=cosv, in_=cst["cos2"]),
                            "LCS", 16)
                    tok[("cs", p)] = sem.inc(
                        gpsimd.dma_start(out=sinv, in_=cst["sin2"]), "LCS", 16)

                for ec in range(EC):
                    @block.sync
                    def _(sync, p=p, ec=ec):
                        if ec >= 2:
                            sync.wait_ge(sem.h["TE"], tok[("hu1", p, ec - 2)])
                        elif p > 0:
                            sync.wait_ge(sem.h["TE"],
                                         prev[("hu2", p - 1, EC - 2 + ec)])
                        else:
                            sync.wait_ge(sem.h["STH"], tok[("hsp", EC - 1)])
                        sem.inc(sync.dma_start(out=hstr[ec % 2][0][:],
                                               in_=h_sp[0, ec]),
                                "LH%d" % (ec % 2), 16)
                        tok[("hl1", p, ec)] = sem.inc(
                            sync.dma_start(out=hstr[ec % 2][1][:],
                                           in_=h_sp[1, ec]),
                            "LH%d" % (ec % 2), 16)

                    @block.tensor
                    def _(tensor, p=p, ec=ec):
                        tensor.wait_ge(sem.h["LH%d" % (ec % 2)],
                                       tok[("hl1", p, ec)])
                        if ec == 0:
                            if p > 0:
                                tensor.wait_ge(sem.h["SC"],
                                               prev[("expdone", p - 1)])
                                tensor.wait_ge(sem.h["VE"],
                                               prev[("combdone", p - 1)])
                            tensor.wait_ge(sem.h["VE"], tok[("wsp", p, WK)])
                        ins = None
                        for i, ps in ((WQ, psA), (WK, psB)):
                            first = (ec == 0)
                            for qt in range(NQT):
                                sl = slice(qt * QT, (qt + 1) * QT)
                                tensor.matmul(ps[:, sl], wsl(wbuf_hi(i), ec),
                                              hstr[ec % 2][0][:, sl],
                                              start=first, stop=False)
                                tensor.matmul(ps[:, sl], wsl(wbuf_hi(i), ec),
                                              hstr[ec % 2][1][:, sl],
                                              start=False, stop=False)
                                ins = tensor.matmul(ps[:, sl], wsl(wbuf_lo(i), ec),
                                                    hstr[ec % 2][0][:, sl],
                                                    start=False,
                                                    stop=(ec == EC - 1))
                        tok[("hu1", p, ec)] = sem.inc(ins, "TE", 1)
                tok[("qkmm", p)] = tok[("hu1", p, EC - 1)]

                @block.scalar
                def _(scalar, p=p):
                    scalar.wait_ge(sem.h["TE"], tok[("qkmm", p)])
                    if p > 0:
                        scalar.wait_ge(sem.h["VE"], prev[("rope", p - 1)])
                    tok[("qev", p)] = sem.inc(
                        scalar.activation(q32[:], psA[:], ACT.Identity,
                                          bias=p_sb["bq"][:, p:p + 1]), "SC", 1)

                def rope_ops(vector, srct, dhi, dlo):
                    vector.stream_shuffle(s1[:], srct[:], shuf_mask)
                    vector.tensor_tensor(s1[:], s1[:], sinv, ALU.mult)
                    vector.tensor_tensor(srct[:], srct[:], cosv, ALU.mult)
                    vector.tensor_tensor(srct[:], srct[:], s1[:], ALU.add)
                    vector.tensor_copy(dhi[:], srct[:])
                    return vector.scalar_tensor_tensor(dlo[:], srct[:], 1.0,
                                                       dhi[:], ALU.mult,
                                                       ALU.subtract)

                @block.vector
                def _(vector, p=p):
                    vector.wait_ge(sem.h["SC"], tok[("qev", p)])
                    vector.wait_ge(sem.h["LCS"], tok[("cs", p)])
                    if p > 0:
                        vector.wait_ge(sem.h["TE"], prev[("scdone", p - 1)])
                    tok[("ropeq", p)] = sem.inc(rope_ops(vector, q32, qhi, qlo),
                                                "VE", 1)

                @block.scalar
                def _(scalar, p=p):
                    scalar.wait_ge(sem.h["VE"], tok[("ropeq", p)])
                    tok[("kev", p)] = sem.inc(
                        scalar.activation(q32[:], psB[:], ACT.Identity,
                                          bias=p_sb["bk"][:, p:p + 1]), "SC", 1)

                @block.vector
                def _(vector, p=p):
                    vector.wait_ge(sem.h["SC"], tok[("kev", p)])
                    tok[("rope", p)] = sem.inc(rope_ops(vector, q32, khi, klo),
                                               "VE", 1)
                    prev[("rope", p)] = tok[("rope", p)]

                for ec in range(EC):
                    @block.sync
                    def _(sync, p=p, ec=ec):
                        if ec >= 2:
                            sync.wait_ge(sem.h["TE"], tok[("hu2", p, ec - 2)])
                        else:
                            sync.wait_ge(sem.h["TE"], tok[("hu1", p, EC - 2 + ec)])
                        sem.inc(sync.dma_start(out=hstr[ec % 2][0][:],
                                               in_=h_sp[0, ec]),
                                "LH%d" % (ec % 2), 16)
                        tok[("hl2", p, ec)] = sem.inc(
                            sync.dma_start(out=hstr[ec % 2][1][:],
                                           in_=h_sp[1, ec]),
                            "LH%d" % (ec % 2), 16)

                    @block.tensor
                    def _(tensor, p=p, ec=ec):
                        tensor.wait_ge(sem.h["LH%d" % (ec % 2)],
                                       tok[("hl2", p, ec)])
                        if ec == 0:
                            tensor.wait_ge(sem.h["SC"], tok[("kev", p)])
                            tensor.wait_ge(sem.h["VE"], tok[("wsp", p, WV)])
                            tensor.wait_ge(sem.h["VE"], tok[("wsp", p, WG)])
                        ins = None
                        for i, ps in ((WV, psA), (WG, psB)):
                            first = (ec == 0)
                            for qt in range(NQT):
                                sl = slice(qt * QT, (qt + 1) * QT)
                                tensor.matmul(ps[:, sl], wsl(wbuf_hi(i), ec),
                                              hstr[ec % 2][0][:, sl],
                                              start=first, stop=False)
                                tensor.matmul(ps[:, sl], wsl(wbuf_hi(i), ec),
                                              hstr[ec % 2][1][:, sl],
                                              start=False, stop=False)
                                ins = tensor.matmul(ps[:, sl], wsl(wbuf_lo(i), ec),
                                                    hstr[ec % 2][0][:, sl],
                                                    start=False,
                                                    stop=(ec == EC - 1))
                        tok[("hu2", p, ec)] = sem.inc(ins, "TE", 1)
                        prev[("hu2", p, ec)] = tok[("hu2", p, ec)]
                tok[("vgmm", p)] = tok[("hu2", p, EC - 1)]
                prev[("projdone", p)] = tok[("vgmm", p)]

                @block.scalar
                def _(scalar, p=p):
                    scalar.wait_ge(sem.h["VE"], tok[("rope", p)])
                    if p > 0:
                        scalar.wait_ge(sem.h["VE"], prev[("gsq", p - 1)])
                    tok[("sqq", p)] = sem.inc(
                        scalar.activation(sqsc[:], qhi[:], ACT.Square), "SC", 1)

                @block.scalar
                def _(scalar, p=p):
                    scalar.wait_ge(sem.h["TE"], tok[("vgmm", p)])
                    scalar.wait_ge(sem.h["VE"], tok[("rope", p)])
                    tok[("vev", p)] = sem.inc(
                        scalar.activation(q32[:], psA[:], ACT.Identity,
                                          bias=p_sb["bv"][:, p:p + 1]), "SC", 1)
                    if p > 0:
                        scalar.wait_ge(sem.h["VE"], prev[("gonve", p - 1)])
                    scalar.activation(gateh[0][:], psB[0:64, :], ACT.Silu,
                                      bias=p_sb["bg"][0:64, p:p + 1])
                    tok[("gev", p)] = sem.inc(
                        scalar.activation(gateh[1][:], psB[64:128, :], ACT.Silu,
                                          bias=p_sb["bg"][64:128, p:p + 1]),
                        "SC", 1)

                if p + 1 < TAP_PAIRS:
                    emit_wchain(p + 1)

                @block.tensor
                def _(tensor, p=p):
                    tensor.wait_ge(sem.h["SC"], tok[("vev", p)])
                    if p > 0:
                        tensor.wait_ge(sem.h["VE"], prev[("vscd", p - 1)])
                    ins = None
                    for c in range(KC):
                        ins = tensor.matmul(psA[:, c * 128:(c + 1) * 128],
                                            q32[:, c * 128:(c + 1) * 128],
                                            c_ident[:], start=(c % 4 == 0),
                                            stop=(c % 4 == 3),
                                            skip_group_check=True)
                    tok[("vt", p)] = sem.inc(ins, "TE", 1)

                @block.vector
                def _(vector, p=p):
                    vector.wait_ge(sem.h["TE"], tok[("vt", p)])
                    ins = None
                    for c in range(KC):
                        src = psA[:, c * 128:(c + 1) * 128]
                        vector.tensor_copy(vtm[0][:, c * 65:c * 65 + 64],
                                           src[:, 0:64])
                        ins = vector.tensor_copy(vtm[1][:, c * 65:c * 65 + 64],
                                                 src[:, 64:128])
                    tok[("vtcall", p)] = sem.inc(ins, "VE", 1)

                @block.vector
                def _(vector, p=p):
                    vector.memset(vtm[0][:, 64::65], 1.0)
                    vector.memset(vtm[1][:, 64::65], 1.0)
                    vector.wait_ge(sem.h["SC"], tok["eks"])
                    ins = None
                    for hh in range(2):
                        h = 2 * p + hh
                        for c in range(KC):
                            ins = vector.tensor_scalar(
                                vsc[hh][:, c * 65:(c + 1) * 65],
                                vtm[hh][:, c * 65:(c + 1) * 65],
                                eks[:, h * KC + c:h * KC + c + 1], None, ALU.mult)
                    tok[("vscd", p)] = sem.inc(ins, "VE", 1)
                    prev[("vscd", p)] = tok[("vscd", p)]

                @block.tensor
                def _(tensor, p=p):
                    tensor.wait_ge(sem.h["SC"], tok[("sqq", p)])
                    tensor.wait_ge(sem.h["SC"], tok[("gev", p)])
                    tensor.wait_ge(sem.h["VE"], tok[("vtcall", p)])
                    for qt in range(NQT):
                        sl = slice(qt * QT, (qt + 1) * QT)
                        tensor.matmul(psB[0:1, sl], ones16[0:64, 0:1],
                                      sqsc[0:64, sl], start=True, stop=True,
                                      skip_group_check=True)
                        ins = tensor.matmul(psA[0:1, sl], ones16[64:128, 0:1],
                                            sqsc[64:128, sl], start=True,
                                            stop=True, tile_position=(64, 0),
                                            skip_group_check=True)
                    tok[("nrmq", p)] = sem.inc(ins, "TE", 1)

                @block.vector
                def _(vector, p=p):
                    vector.wait_ge(sem.h["TE"], tok[("nrmq", p)])
                    vsync(vector, vector.tensor_reduce(browq[0:1, 0:1],
                                                       psB[0:1, :], AX.X, ALU.max))
                    tok[("nrq", p)] = vsync(
                        vector, vector.tensor_reduce(browq[0:1, 1:2], psA[0:1, :],
                                                     AX.X, ALU.max))

                @block.scalar
                def _(scalar, p=p):
                    scalar.wait_ge(sem.h["TE"], tok[("nrmq", p)])
                    tok[("sqk", p)] = sem.inc(
                        scalar.activation(sqsc[:], khi[:], ACT.Square), "SC", 1)

                @block.tensor
                def _(tensor, p=p):
                    tensor.wait_ge(sem.h["VE"], tok[("nrq", p)])
                    tensor.wait_ge(sem.h["SC"], tok[("sqk", p)])
                    for qt in range(NQT):
                        sl = slice(qt * QT, (qt + 1) * QT)
                        tensor.matmul(psB[0:1, sl], ones16[0:64, 0:1],
                                      sqsc[0:64, sl], start=True, stop=True,
                                      skip_group_check=True)
                        ins = tensor.matmul(psA[0:1, sl], ones16[64:128, 0:1],
                                            sqsc[64:128, sl], start=True,
                                            stop=True, tile_position=(64, 0),
                                            skip_group_check=True)
                    tok[("nrmk", p)] = sem.inc(ins, "TE", 1)

                @block.vector
                def _(vector, p=p):
                    vector.wait_ge(sem.h["TE"], tok[("nrmk", p)])
                    vsync(vector, vector.tensor_reduce(browk[0:1, 0:1],
                                                       psB[0:1, :], AX.X, ALU.max))
                    vsync(vector, vector.tensor_reduce(browk[0:1, 1:2],
                                                       psA[0:1, :], AX.X, ALU.max))
                    tok[("nred", p)] = vpub(
                        vector,
                        vector.tensor_tensor(browq[:], browq[:], browk[:],
                                             ALU.mult))

                @block.scalar
                def _(scalar, p=p):
                    scalar.wait_ge(sem.h["VE"], tok[("nred", p)])
                    tok[("brow", p)] = spub(
                        scalar, scalar.activation(browq[:], browq[:], ACT.Sqrt,
                                                  scale=0.0629))

                if tap == "all":
                    @block.sync
                    def _(sync, p=p):
                        sync.wait_ge(sem.h["SC"], tok[("brow", p)])
                        sem.inc(sync.dma_start(out=dbg_b[p, 0:1, :],
                                               in_=browq[:]), "STY", 16)
                        sem.inc(sync.dma_start(out=dbg_b[p, 1:2, :],
                                               in_=browk[:]), "STY", 16)

                @block.gpsimd
                def _(gpsimd, p=p):
                    gpsimd.wait_ge(sem.h["SC"], tok[("brow", p)])
                    if tap == "all":
                        gpsimd.wait_ge(sem.h["STY"], sem.cur("STY"))
                    gpsimd.partition_broadcast(bbc[:, 0:1], browq[0:1, 0:1], 128)
                    tok[("bbc", p)] = sem.inc(
                        gpsimd.partition_broadcast(bbc[:, 1:2], browq[0:1, 1:2],
                                                   128), "GP", 1)

                @block.vector
                def _(vector, p=p):
                    vector.wait_ge(sem.h["GP"], tok[("bbc", p)])
                    vector.wait_ge(sem.h["VE"], tok["ldks"])
                    ins = None
                    for hh in range(2):
                        h = 2 * p + hh
                        vector.tensor_scalar(bfh[:, hh * KC:(hh + 1) * KC],
                                             ldks[:, h * KC:(h + 1) * KC],
                                             bbc[:, hh:hh + 1], None, ALU.subtract)
                        vector.tensor_scalar(ldnb[:, hh:hh + 1],
                                             ldb[:, h:h + 1], -1.0, None,
                                             ALU.mult)
                        ins = vector.tensor_scalar(lqrow[0:1, hh * 4:hh * 4 + 4],
                                                   c_kramp[0:1, 0:KC:4],
                                                   ldnrow[0:1, h:h + 1], None,
                                                   ALU.mult)
                    tok[("bfh", p)] = vpub(vector, ins)

            with nc.psum_tensor(f"scp0_{p}", [128, 2 * QT], F32) as scp0, \
                 nc.psum_tensor(f"scp1_{p}", [128, 2 * QT], F32) as scp1, \
                 nc.psum_tensor(f"opA_{p}", [65, QT], F32) as opA, \
                 nc.psum_tensor(f"ofA_{p}", [65, QT], F32) as ofA, \
                 nc.psum_tensor(f"opB_{p}", [65, QT], F32) as opB, \
                 nc.psum_tensor(f"ofB_{p}", [65, QT], F32) as ofB, \
                 nc.sbuf_tensor(f"on32_{p}", [64, T], F32) as on32, \
                 nc.sbuf_tensor(f"gonh_{p}", [64, T], F16) as gonhi, \
                 nc.sbuf_tensor(f"gonl_{p}", [64, T], F16) as gonlo, \
                 nc.sbuf_tensor(f"gsum_{p}", [64, 4], F32) as gsum:

                scp = [scp0, scp1]
                ophh = [opA, opB]
                ofhh = [ofA, ofB]

                for qt in range(1 if tap == "pb0" else NQT):
                    pv_state = {}
                    for c in range(KC):
                        @block.tensor
                        def _(tensor, p=p, qt=qt, c=c):
                            g = qt * KC + c
                            if g >= 2:
                                gg = g - 2
                                tensor.wait_ge(
                                    sem.h["SC"],
                                    tok[("exp", p, gg // KC, gg % KC)])
                            if g == 0:
                                tensor.wait_ge(sem.h["VE"], tok[("rope", p)])
                                tensor.wait_ge(sem.h["VE"], tok[("nred", p)])
                            gsl = slice(qt * QT, (qt + 1) * QT)
                            par = c % 2
                            ksl = slice(c * 128, (c + 1) * 128)
                            ins = None
                            for hh in range(2):
                                ph = slice(hh * 64, hh * 64 + 64)
                                tp = (hh * 64, 0)
                                dst = scp[par][:, hh * QT:hh * QT + QT]
                                tensor.matmul(dst, khi[ph, ksl], qhi[ph, gsl],
                                              start=True, stop=False,
                                              tile_position=tp,
                                              skip_group_check=True)
                                tensor.matmul(dst, khi[ph, ksl], qlo[ph, gsl],
                                              start=False, stop=False,
                                              tile_position=tp,
                                              skip_group_check=True)
                                ins = tensor.matmul(dst, klo[ph, ksl],
                                                    qhi[ph, gsl],
                                                    start=False, stop=True,
                                                    tile_position=tp,
                                                    skip_group_check=True)
                            tok[("sc", p, qt, c)] = sem.inc(ins, "TE", 1)

                        @block.scalar
                        def _(scalar, p=p, qt=qt, c=c):
                            scalar.wait_ge(sem.h["TE"], tok[("sc", p, qt, c)])
                            if qt == 0 and c == 0:
                                scalar.wait_ge(sem.h["VE"], tok[("bfh", p)])
                                if p > 0:
                                    scalar.wait_ge(sem.h["TE"],
                                                   prev[("pvdone", p - 1)])
                            par = c % 2
                            ins = None
                            for hh in range(2):
                                src = scp[par][:, hh * QT:hh * QT + QT]
                                dst = pbuf[hh][:, c * QT:(c + 1) * QT]
                                ins = scalar.activation(
                                    dst, src, ACT.Exp, scale=0.125,
                                    bias=bfh[:, hh * KC + c:hh * KC + c + 1])
                            tok[("exp", p, qt, c)] = sem.inc(ins, "SC", 1)

                        def emit_strip(c, p=p, qt=qt):
                            @block.vector
                            def _(vector, p=p, qt=qt, c=c):
                                j = c - 4 * qt
                                vector.wait_ge(sem.h["SC"],
                                               tok[("exp", p, qt, c)])
                                if j >= 2:
                                    vector.wait_ge(sem.h["TE"],
                                                   tok[("pvc", p, qt, c - 2)])
                                ins = None
                                for hh in range(2):
                                    pb = pbuf[hh][:, c * QT + j * 128:
                                                  c * QT + j * 128 + 128]
                                    ssl = slice((j % 2) * 256 + hh * 128,
                                                (j % 2) * 256 + hh * 128 + 128)
                                    vector.tensor_tensor(stripL[:, ssl], pb,
                                                         c_L[:], ALU.mult)
                                    ins = vector.tensor_tensor(stripU[:, ssl], pb,
                                                               c_U[:], ALU.mult)
                                tok[("strip", p, qt, c)] = sem.inc(ins, "VE", 1)

                        if c // 4 == qt and not SERIAL_ATTN:
                            emit_strip(c)

                        def emit_pv(cc, p=p, qt=qt, pv_state=pv_state):
                            @block.tensor
                            def _(tensor, p=p, qt=qt, cc=cc):
                                tensor.wait_ge(sem.h["SC"],
                                               tok[("exp", p, qt, cc)])
                                if cc == 0:
                                    tensor.wait_ge(sem.h["VE"], tok[("vscd", p)])
                                    if qt > 0:
                                        tensor.wait_ge(sem.h["VE"],
                                                       tok[("comb", p, qt - 1)])
                                strad = (cc // 4 == qt)
                                if strad:
                                    tensor.wait_ge(sem.h["VE"],
                                                   tok[("strip", p, qt, cc)])
                                ins = None
                                for hh in range(2):
                                    vt = vtm[hh][:, cc * 65:(cc + 1) * 65]
                                    vs = vsc[hh][:, cc * 65:(cc + 1) * 65]
                                    pb = pbuf[hh][:, cc * QT:(cc + 1) * QT]
                                    op = ophh[hh]
                                    of = ofhh[hh]
                                    if cc < 4 * qt:
                                        ins = _mm(tensor, op[:, :], vs, pb,
                                                  ("op", hh), pv_state)
                                    elif cc >= 4 * (qt + 1):
                                        ins = _mm(tensor, of[:, :], vt, pb,
                                                  ("of", hh), pv_state,
                                                  stop=(cc == KC - 1))
                                    else:
                                        j = cc - 4 * qt
                                        w_f = 128 * j
                                        w_p0 = w_f + 128
                                        if w_f > 0:
                                            _mm(tensor, of[:, 0:w_f], vt,
                                                pb[:, 0:w_f], ("of", hh), pv_state)
                                        if QT - w_p0 > 0:
                                            _mm(tensor, op[:, w_p0:QT], vs,
                                                pb[:, w_p0:QT], ("op", hh),
                                                pv_state)
                                        ssl = slice((j % 2) * 256 + hh * 128,
                                                    (j % 2) * 256 + hh * 128 + 128)
                                        _mm(tensor, op[:, w_f:w_p0], vs,
                                            stripL[:, ssl], ("op", hh), pv_state,
                                            stop=True)
                                        ins = _mm(tensor, of[:, w_f:w_p0], vt,
                                                  stripU[:, ssl], ("of", hh),
                                                  pv_state, stop=(cc == KC - 1))
                                tok[("pvc", p, qt, cc)] = sem.inc(ins, "TE", 1)
                                if cc == KC - 1:
                                    tok[("pv", p, qt)] = tok[("pvc", p, qt, cc)]

                        if not SERIAL_ATTN:
                            if c >= 1:
                                emit_pv(c - 1)
                            if c == KC - 1:
                                emit_pv(c)
                    if SERIAL_ATTN:
                        for c in range(KC):
                            if c // 4 == qt:
                                emit_strip(c)
                            emit_pv(c)

                    for hh in range(2):
                        @block.vector
                        def _(vector, p=p, qt=qt, hh=hh):
                            gsl = slice(qt * QT, (qt + 1) * QT)
                            vector.wait_ge(sem.h["VE"], tok[("bfh", p)])
                            tok[("dr", p, qt, hh)] = sem.inc(
                                vector.tensor_scalar(
                                    r1t[:], c_qramp[:],
                                    ldnrow[0:1, 2 * p + hh:2 * p + hh + 1],
                                    lqrow[0:1, hh * 4 + qt:hh * 4 + qt + 1],
                                    ALU.mult, ALU.add), "VE", 1)

                        @block.scalar
                        def _(scalar, p=p, qt=qt, hh=hh):
                            scalar.wait_ge(sem.h["VE"], tok[("dr", p, qt, hh)])
                            tok[("dre", p, qt, hh)] = sem.inc(
                                scalar.activation(r1t[:], r1t[:], ACT.Exp),
                                "SC", 1)

                        @block.gpsimd
                        def _(gpsimd, p=p, qt=qt, hh=hh):
                            gpsimd.wait_ge(sem.h["SC"], tok[("dre", p, qt, hh)])
                            if hh == 1:
                                gpsimd.wait_ge(sem.h["VE"],
                                               tok[("linv", p, qt, 0)])
                            tok[("dbb", p, qt, hh)] = sem.inc(
                                gpsimd.partition_broadcast(dbb1[hh][:], r1t[:],
                                                           65), "GP", 1)

                        @block.vector
                        def _(vector, p=p, qt=qt, hh=hh):
                            vector.wait_ge(sem.h["TE"], tok[("pv", p, qt)])
                            vector.wait_ge(sem.h["GP"], tok[("dbb", p, qt, hh)])
                            vector.tensor_tensor(csc1[:], ofhh[hh][:],
                                                 dbb1[hh][:], ALU.mult)
                            vector.scalar_tensor_tensor(csc1[:], ophh[hh][:],
                                                        1.0, csc1[:], ALU.mult,
                                                        ALU.add)
                            tok[("linv", p, qt, hh)] = vpub(
                                vector, vector.reciprocal(r1t[:], csc1[64:65, :]))

                        if tap == "cs0" and qt == 0 and hh == 0:
                            @block.sync
                            def _(sync, p=p, qt=qt, hh=hh):
                                sync.wait_ge(sem.h["VE"], tok[("linv", p, qt, hh)])
                                sem.inc(sync.dma_start(out=dbg_cs[0:65, :],
                                                       in_=csc1[:]), "STY", 16)
                                sem.inc(sync.dma_start(out=dbg_cs[65:66, :],
                                                       in_=r1t[:]), "STY", 16)

                        @block.gpsimd
                        def _(gpsimd, p=p, qt=qt, hh=hh):
                            gpsimd.wait_ge(sem.h["VE"], tok[("linv", p, qt, hh)])
                            if tap == "cs0" and qt == 0 and hh == 0:
                                gpsimd.wait_ge(sem.h["STY"], sem.cur("STY"))
                            tok[("lbc", p, qt, hh)] = sem.inc(
                                gpsimd.partition_broadcast(linvb[:], r1t[:], 64),
                                "GP", 1)

                        @block.vector
                        def _(vector, p=p, qt=qt, hh=hh):
                            gsl = slice(qt * QT, (qt + 1) * QT)
                            vector.wait_ge(sem.h["GP"], tok[("lbc", p, qt, hh)])
                            tok[("comb", p, qt)] = sem.inc(
                                vector.tensor_tensor(ogt[hh][:, gsl],
                                                     csc1[0:64, :], linvb[:],
                                                     ALU.mult), "VE", 1)

                qq = 0 if tap == "pb0" else NQT - 1
                prev[("pvdone", p)] = tok[("pv", p, qq)]
                prev[("scdone", p)] = tok[("sc", p, qq, KC - 1)]
                prev[("combdone", p)] = tok[("comb", p, qq)]
                prev[("expdone", p)] = tok[("exp", p, qq, KC - 1)]

                if tap in ("pb", "pb0"):
                    @block.sync
                    def _(sync, p=p):
                        qq = 0 if tap == "pb0" else NQT - 1
                        sync.wait_ge(sem.h["SC"], tok[("exp", p, qq, KC - 1)])
                        sem.inc(sync.dma_start(out=dbg_pb[0], in_=pbuf[0]),
                                "STY", 16)
                        sem.inc(sync.dma_start(out=dbg_pb[1], in_=pbuf[1]),
                                "STY", 16)
                        sync.wait_ge(sem.h["STY"], sem.cur("STY"))
                    tap_now[0] = tap
                    break

                if tap in ("og", "cs0"):
                    @block.sync
                    def _(sync, p=p):
                        sync.wait_ge(sem.h["VE"], tok[("comb", p, NQT - 1)])
                        if tap == "og":
                            sem.inc(sync.dma_start(out=dbg_og[0], in_=ogt[0][:]),
                                    "STY", 16)
                            sem.inc(sync.dma_start(out=dbg_og[1], in_=ogt[1][:]),
                                    "STY", 16)
                        sync.wait_ge(sem.h["STY"], sem.cur("STY"))
                    tap_now[0] = tap
                    break

                if tap == "all":
                    @block.sync
                    def _(sync, p=p):
                        sync.wait_ge(sem.h["VE"], tok[("comb", p, NQT - 1)])
                        for hh in range(2):
                            sem.inc(sync.dma_start(out=dbg_ogall[p, hh],
                                                   in_=ogt[hh][:]), "STY", 16)
                            sem.inc(sync.dma_start(out=dbg_gate[p, hh],
                                                   in_=gateh[hh][:]), "STY", 16)

                for hh in range(2):
                    @block.vector
                    def _(vector, p=p, hh=hh):
                        vector.wait_ge(sem.h["VE"], tok[("comb", p, NQT - 1)])
                        tok[("gsum", p, hh)] = vsync(
                            vector, vector.reduce_sum(gsum[:, hh:hh + 1],
                                                      ogt[hh][:], AX.X))
                        vector.tensor_tensor(sqsc[0:64, :], ogt[hh][:],
                                             ogt[hh][:], ALU.mult)
                        tok[("gsq", p, hh)] = (vpub if hh == 1 else vsync)(
                            vector, vector.reduce_sum(gsum[:, 2 + hh:3 + hh],
                                                      sqsc[0:64, :], AX.X))
                prev[("gsq", p)] = tok[("gsq", p, 1)]

                @block.gpsimd
                def _(gpsimd, p=p):
                    gpsimd.wait_ge(sem.h["VE"], tok[("gsq", p, 1)])
                    gpsimd.wait_ge(sem.h["VE"], tok[("gsq", p, 0)])
                    tok[("gmm", p)] = sem.inc(
                        gpsimd.partition_all_reduce(gsum[:], gsum[:], 64,
                                                    ReduceOp.add), "GP", 1)

                @block.vector
                def _(vector, p=p):
                    vector.wait_ge(sem.h["GP"], tok[("gmm", p)])
                    NTOT = float(T * 64)
                    vsync(vector,
                          vector.tensor_scalar(r2t[0:1, 0:4], gsum[0:1, 0:4],
                                               1.0 / NTOT, None, ALU.mult))
                    vsync(vector,
                          vector.tensor_tensor(r2t[0:1, 4:6], r2t[0:1, 0:2],
                                               r2t[0:1, 0:2], ALU.mult))
                    tok[("gvar", p)] = vpub(
                        vector, vector.tensor_tensor(r2t[0:1, 2:4],
                                                     r2t[0:1, 2:4],
                                                     r2t[0:1, 4:6],
                                                     ALU.subtract))


                @block.scalar
                def _(scalar, p=p):
                    scalar.wait_ge(sem.h["VE"], tok[("gvar", p)])
                    tok[("gsqrt", p)] = spub(
                        scalar,
                        scalar.activation(r2t[0:1, 2:4], r2t[0:1, 2:4], ACT.Sqrt,
                                          bias=ctb[0:1, 0:1]))

                @block.vector
                def _(vector, p=p):
                    vector.wait_ge(sem.h["SC"], tok[("gsqrt", p)])
                    tok[("grstd", p)] = vpub(
                        vector, vector.reciprocal(r2t[0:1, 6:8], r2t[0:1, 2:4]))

                if tap == "all":
                    @block.sync
                    def _(sync, p=p):
                        sync.wait_ge(sem.h["VE"], tok[("grstd", p)])
                        sem.inc(sync.dma_start(out=dbg_r2[p], in_=r2t[:]),
                                "STY", 16)

                @block.gpsimd
                def _(gpsimd, p=p):
                    gpsimd.wait_ge(sem.h["VE"], tok[("grstd", p)])
                    ins = None
                    for hh in range(2):
                        gpsimd.partition_broadcast(gnb[:, hh:hh + 1],
                                                   r2t[0:1, hh:hh + 1], 64)
                        ins = gpsimd.partition_broadcast(gnb[:, 2 + hh:3 + hh],
                                                         r2t[0:1, 6 + hh:7 + hh],
                                                         64)
                    tok[("gnb", p)] = sem.inc(ins, "GP", 1)

                for hh in range(2):
                    @block.vector
                    def _(vector, p=p, hh=hh):
                        vector.wait_ge(sem.h["GP"], tok[("gnb", p)])
                        if hh == 0:
                            vector.wait_ge(sem.h["SC"], tok[("gev", p)])
                            if p > 0:
                                vector.wait_ge(sem.h["STG"],
                                               prev[("gonst", p - 1)])
                        else:
                            vector.wait_ge(sem.h["STG"], tok[("gonst", p, 0)])
                        ph = slice(hh * 64, hh * 64 + 64)
                        vsync(vector, vector.tensor_copy(
                            gnc[:, 0:1], p_sb["gamma"][ph, p:p + 1]))
                        vsync(vector, vector.tensor_copy(
                            gnc[:, 1:2], p_sb["beta"][ph, p:p + 1]))
                        vsync(vector, vector.tensor_tensor(
                            gnc[:, 2:3], gnc[:, 0:1],
                            gnb[:, 2 + hh:3 + hh], ALU.mult))
                        vsync(vector, vector.scalar_tensor_tensor(
                            gnc[:, 3:4], gnb[:, hh:hh + 1], 1.0,
                            gnc[:, 2:3], ALU.mult, ALU.mult))
                        vsync(vector, vector.tensor_tensor(
                            gnc[:, 3:4], gnc[:, 3:4],
                            gnc[:, 1:2], ALU.subtract))
                        vector.tensor_scalar(on32[:], ogt[hh][:], gnc[:, 2:3],
                                             gnc[:, 3:4], ALU.mult, ALU.subtract)
                        vector.tensor_tensor(on32[:], on32[:], gateh[hh][:],
                                             ALU.mult)
                        vector.tensor_copy(gonhi[:], on32[:])
                        tok[("gon", p, hh)] = sem.inc(
                            vector.scalar_tensor_tensor(gonlo[:], on32[:], 1.0,
                                                        gonhi[:], ALU.mult,
                                                        ALU.subtract), "VE", 1)

                    @block.gpsimd
                    def _(gpsimd, p=p, hh=hh):
                        gpsimd.wait_ge(sem.h["VE"], tok[("gon", p, hh)])
                        ph = slice(hh * 64, hh * 64 + 64)
                        sem.inc(gpsimd.dma_start(out=gon_sp[0, p, ph],
                                                 in_=gonhi[:]), "STG", 16)
                        tok[("gonst", p, hh)] = sem.inc(
                            gpsimd.dma_start(out=gon_sp[1, p, ph], in_=gonlo[:]),
                            "STG", 16)
                if tap == "all":
                    @block.sync
                    def _(sync, p=p):
                        sync.wait_ge(sem.h["STY"], sem.cur("STY"))
                prev[("gonve", p)] = tok[("gon", p, 1)]
                prev[("gonst", p)] = tok[("gonst", p, 1)]

        if tap == "qk":
            @block.sync
            def _(sync):
                sync.wait_ge(sem.h["VE"], tok[("rope", 0)])
                sem.inc(sync.dma_start(out=dbg_qk[0], in_=qhi[:]), "STY", 16)
                sem.inc(sync.dma_start(out=dbg_qk[1], in_=qlo[:]), "STY", 16)
                sem.inc(sync.dma_start(out=dbg_qk[2], in_=khi[:]), "STY", 16)
                sem.inc(sync.dma_start(out=dbg_qk[3], in_=klo[:]), "STY", 16)
                sync.wait_ge(sem.h["STY"], sem.cur("STY"))
            lower_extended_insts(nc)
            return nc

        if tap in ("og", "gon", "pb", "pb0", "cs0"):
            @block.sync
            def _(sync):
                for s in ["STH", "STG", "STY"]:
                    sync.wait_ge(sem.h[s], sem.cur(s))
                sync.wait_ge(sem.h["SC"], sem.cur("SC"))
                sync.wait_ge(sem.h["VE"], sem.cur("VE"))
            lower_extended_insts(nc)
            return nc

        stB.close()

        _k5 = "ExternalOutput" if tap in ("all", "allq") else "Internal"
        xr_sp = nc.dram_tensor("xr_sp", [2, EC, 128, T], F16, kind=_k5).ap()
        xk_sp = nc.dram_tensor("xk_sp", [2, EC, 128, T], F16, kind=_k5).ap()

        with contextlib.ExitStack() as s3:
            sb3 = lambda name, shape, dt=F32: s3.enter_context(
                nc.sbuf_tensor(name, shape, dt))
            gstr = [[sb3(f"gstr{i}_{d}", [128, T], F16) for d in range(2)]
                    for i in range(2)]
            w3 = [sb3(f"w3st{i}", [128, EC * 128]) for i in range(2)]
            w3hi = [sb3(f"w3hi{i}", [128, EC * 128], F16) for i in range(2)]
            w3lo = [sb3(f"w3lo{i}", [128, EC * 128], F16) for i in range(2)]
            xb3 = [sb3(f"xb3_{i}", [128, T]) for i in range(2)]
            x1c = sb3("x1c", [128, T])
            x1hi = sb3("x1hi", [128, T], F16)
            x1lo = sb3("x1lo", [128, T], F16)
            psO = s3.enter_context(nc.psum_tensor("psO", [128, T], F32))
            psO2 = s3.enter_context(nc.psum_tensor("psO2", [128, T], F32))
            psOs = [psO, psO2]

            @block.sync
            def _(sync):
                sync.wait_ge(sem.h["STG"], sem.cur("STG"))
                sync.wait_ge(sem.h["VE"], sem.cur("VE"))
                sync.wait_ge(sem.h["SC"], sem.cur("SC"))
                sync.wait_ge(sem.h["TE"], sem.cur("TE"))
                sync.wait_ge(sem.h["GP"], sem.cur("GP"))
                sem.inc(sync.dma_start(out=w3[0][:], in_=wblk["o"][0]),
                        "LW0", 16)
                tok[("w3", 0)] = sem.cur("LW0")

            def wchain(gi, nxt_src, fmm_key):
                sl = gi % 2

                @block.scalar
                def _(scalar, gi=gi, sl=sl):
                    scalar.wait_ge(sem.h["LW%d" % sl], tok[("w3", gi)])
                    if gi >= 2:
                        scalar.wait_ge(sem.h["TE"], tok[(fmm_key, gi - 2)])
                    tok[("w3hi", gi)] = sem.inc(
                        scalar.activation(w3hi[sl][:], w3[sl][:], ACT.Copy),
                        "SC", 1)

                @block.vector
                def _(vector, gi=gi, sl=sl):
                    vector.wait_ge(sem.h["SC"], tok[("w3hi", gi)])
                    tok[("w3sp", gi)] = sem.inc(
                        vector.scalar_tensor_tensor(
                            w3lo[sl][:], w3[sl][:], 1.0, w3hi[sl][:],
                            ALU.mult, ALU.subtract), "VE", 1)

                @block.sync
                def _(sync, gi=gi, nxt_src=nxt_src):
                    if nxt_src is not None:
                        osl = (gi + 1) % 2
                        if gi >= 1:
                            sync.wait_ge(sem.h["SC"], tok[("w3hi", gi)])
                        sem.inc(sync.dma_start(out=w3[osl][:], in_=nxt_src),
                                "LW%d" % osl, 16)
                        tok[("w3", gi + 1)] = sem.cur("LW%d" % osl)

            for dc in range(EC):
                wchain(dc, wblk["o"][dc + 1] if dc < EC - 1 else None, "p3mm")

                for ec in range(EC):
                    @block.sync
                    def _(sync, dc=dc, ec=ec):
                        gi = dc * EC + ec
                        if gi >= 2:
                            sync.wait_ge(sem.h["TE"], tok[("p3u", gi - 2)])
                        sem.inc(sync.dma_start(out=gstr[ec % 2][0][:],
                                               in_=gon_sp[0, ec]),
                                "LH%d" % (ec % 2), 16)
                        tok[("g3", gi)] = sem.inc(
                            sync.dma_start(out=gstr[ec % 2][1][:],
                                           in_=gon_sp[1, ec]),
                            "LH%d" % (ec % 2), 16)

                    @block.tensor
                    def _(tensor, dc=dc, ec=ec):
                        gi = dc * EC + ec
                        tensor.wait_ge(sem.h["LH%d" % (ec % 2)], tok[("g3", gi)])
                        if ec == 0:
                            tensor.wait_ge(sem.h["VE"], tok[("w3sp", dc)])
                            if dc >= 2:
                                tensor.wait_ge(sem.h["VE"],
                                               tok[("x1ev", dc - 2)])
                        ps = psOs[dc % 2]
                        whi_ = w3hi[dc % 2]
                        wlo_ = w3lo[dc % 2]
                        ins = None
                        first = (ec == 0)
                        for qt in range(NQT):
                            sl = slice(qt * QT, (qt + 1) * QT)
                            tensor.matmul(ps[:, sl], wsl(whi_, ec),
                                          gstr[ec % 2][0][:, sl],
                                          start=first, stop=False)
                            tensor.matmul(ps[:, sl], wsl(whi_, ec),
                                          gstr[ec % 2][1][:, sl],
                                          start=False, stop=False)
                            ins = tensor.matmul(ps[:, sl], wsl(wlo_, ec),
                                                gstr[ec % 2][0][:, sl],
                                                start=False,
                                                stop=(ec == EC - 1))
                        tok[("p3u", gi)] = sem.inc(ins, "TE", 1)
                tok[("p3mm", dc)] = tok[("p3u", dc * EC + EC - 1)]

                @block.sync
                def _(sync, dc=dc):
                    if dc >= 2:
                        sync.wait_ge(sem.h["VE"], tok[("x1c", dc - 2)])
                    tok[("x3", dc)] = sem.inc(
                        sync.dma_start(out=xb3[dc % 2][:],
                                       in_=x_fm[dc * 128:(dc + 1) * 128, :]),
                        "LX%d" % (dc % 2), 16)

                @block.vector
                def _(vector, dc=dc):
                    vector.wait_ge(sem.h["TE"], tok[("p3mm", dc)])
                    vector.wait_ge(sem.h["LX%d" % (dc % 2)], tok[("x3", dc)])
                    if dc >= 1:
                        vector.wait_ge(sem.h["STX"], tok[("x1sp", dc - 1)])
                    vector.tensor_scalar(x1c[:], psOs[dc % 2][:],
                                         p_sb["bo"][:, dc:dc + 1], None, ALU.add)
                    tok[("x1c", dc)] = sem.inc(
                        vector.tensor_tensor(x1c[:], x1c[:], xb3[dc % 2][:],
                                             ALU.add), "VE", 1)

                @block.scalar
                def _(scalar, dc=dc):
                    scalar.wait_ge(sem.h["VE"], tok[("x1c", dc)])
                    tok[("x1hi", dc)] = sem.inc(
                        scalar.activation(x1hi[:], x1c[:], ACT.Copy), "SC", 1)

                @block.vector
                def _(vector, dc=dc):
                    vector.wait_ge(sem.h["SC"], tok[("x1hi", dc)])
                    tok[("x1ev", dc)] = sem.inc(
                        vector.scalar_tensor_tensor(
                            x1lo[:], x1c[:], 1.0, x1hi[:], ALU.mult,
                            ALU.subtract), "VE", 1)

                @block.sync
                def _(sync, dc=dc):
                    sync.wait_ge(sem.h["VE"], tok[("x1ev", dc)])
                    sem.inc(sync.dma_start(out=x1_sp[0, dc], in_=x1hi[:]),
                            "STX", 16)
                    tok[("x1sp", dc)] = sem.inc(
                        sync.dma_start(out=x1_sp[1, dc], in_=x1lo[:]), "STX", 16)

        if tap == "x1":
            @block.sync
            def _(sync):
                sync.wait_ge(sem.h["STX"], sem.cur("STX"))
            lower_extended_insts(nc)
            return nc

        with contextlib.ExitStack() as s4:
            sb4 = lambda name, shape, dt=F32: s4.enter_context(
                nc.sbuf_tensor(name, shape, dt))
            xs4 = [[sb4(f"xs4_{i}_{d}", [128, T], F16) for d in range(2)]
                   for i in range(2)]
            t4 = [sb4(f"t4_{i}", [128, T]) for i in range(2)]
            sq4 = sb4("sq4", [128, T])
            m2b_ = sb4("m2b", [128, T])
            rstd2b_ = sb4("rstd2b", [128, T])

            with nc.sbuf_tensor("m2row", [1, T], F32) as m2row, \
                 nc.sbuf_tensor("rstd2row", [1, T], F32) as rstd2row, \
                 nc.psum_tensor("psS2", [1, T], F32) as psS2, \
                 nc.psum_tensor("psQ2", [1, T], F32) as psQ2:

                for ec in range(EC):
                    @block.sync
                    def _(sync, ec=ec):
                        if ec >= 2:
                            sync.wait_ge(sem.h["TE"], tok[("s4te", ec - 2)])
                        if ec == 0:
                            sync.wait_ge(sem.h["STX"], tok[("x1sp", EC - 1)])
                        sem.inc(sync.dma_start(out=xs4[ec % 2][0][:],
                                               in_=x1_sp[0, ec]),
                                "LH%d" % (ec % 2), 16)
                        tok[("x4", ec)] = sem.inc(
                            sync.dma_start(out=xs4[ec % 2][1][:],
                                           in_=x1_sp[1, ec]),
                            "LH%d" % (ec % 2), 16)

                    @block.vector
                    def _(vector, ec=ec):
                        vector.wait_ge(sem.h["LH%d" % (ec % 2)], tok[("x4", ec)])
                        if ec >= 2:
                            vector.wait_ge(sem.h["TE"], tok[("s4te", ec - 2)])
                        tok[("t4", ec)] = sem.inc(
                            vector.tensor_tensor(t4[ec % 2][:], xs4[ec % 2][0][:],
                                                 xs4[ec % 2][1][:], ALU.add),
                            "VE", 1)

                    @block.scalar
                    def _(scalar, ec=ec):
                        scalar.wait_ge(sem.h["VE"], tok[("t4", ec)])
                        if ec >= 1:
                            scalar.wait_ge(sem.h["TE"], tok[("s4te", ec - 1)])
                        tok[("sq4", ec)] = sem.inc(
                            scalar.activation(sq4[:], t4[ec % 2][:], ACT.Square),
                            "SC", 1)

                    @block.tensor
                    def _(tensor, ec=ec):
                        tensor.wait_ge(sem.h["VE"], tok[("t4", ec)])
                        for qt in range(NQT):
                            tensor.matmul(psS2[0:1, qt * QT:(qt + 1) * QT],
                                          c_ones[:],
                                          t4[ec % 2][:, qt * QT:(qt + 1) * QT],
                                          start=(ec == 0), stop=(ec == EC - 1))
                        tensor.wait_ge(sem.h["SC"], tok[("sq4", ec)])
                        ins = None
                        for qt in range(NQT):
                            ins = tensor.matmul(
                                psQ2[0:1, qt * QT:(qt + 1) * QT], c_ones[:],
                                sq4[:, qt * QT:(qt + 1) * QT],
                                start=(ec == 0), stop=(ec == EC - 1))
                        tok[("s4te", ec)] = sem.inc(ins, "TE", 1)

                @block.vector
                def _(vector):
                    vector.wait_ge(sem.h["TE"], tok[("s4te", EC - 1)])
                    vector.tensor_scalar(m2row[:], psS2[0:1, :], 1.0 / E, None,
                                         ALU.mult)
                    vector.tensor_tensor(rstd2row[:], m2row[:], m2row[:],
                                         ALU.mult)
                    tok["var2"] = sem.inc(
                        vector.scalar_tensor_tensor(rstd2row[:], psQ2[0:1, :],
                                                    1.0 / E, rstd2row[:],
                                                    ALU.mult, ALU.subtract),
                        "VE", 1)

                @block.scalar
                def _(scalar):
                    scalar.wait_ge(sem.h["VE"], tok["var2"])
                    tok["sqrt2"] = sem.inc(
                        scalar.activation(rstd2row[:], rstd2row[:], ACT.Sqrt,
                                          bias=ctb[0:1, 0:1]), "SC", 1)

                @block.vector
                def _(vector):
                    vector.wait_ge(sem.h["SC"], tok["sqrt2"])
                    tok["rstd2"] = vpub(
                        vector, vector.reciprocal(rstd2row[:], rstd2row[:]))

                @block.gpsimd
                def _(gpsimd):
                    gpsimd.wait_ge(sem.h["VE"], tok["rstd2"])
                    gpsimd.partition_broadcast(m2b_[:], m2row[:], 128)
                    tok["bcast2"] = sem.inc(
                        gpsimd.partition_broadcast(rstd2b_[:], rstd2row[:], 128),
                        "GP", 1)

            with nc.sbuf_tensor("g32", [128, T + 1], F32) as g32, \
                 nc.sbuf_tensor("xm32", [128, T], F32) as xm32, \
                 nc.sbuf_tensor("xphi", [128, T], F16) as xphi, \
                 nc.sbuf_tensor("xplo", [128, T], F16) as xplo, \
                 nc.sbuf_tensor("mur2", [128, 8], F32) as mur2, \
                 nc.sbuf_tensor("muk2", [128, 8], F32) as muk2:

                @block.vector
                def _(vector):
                    vector.wait_ge(sem.h["GP"], tok["bcast2"])
                    vector.tensor_scalar(mur2[:], p_sb["mu_r"][:], -1.0, 1.0,
                                         ALU.mult, ALU.add)
                    vector.memset(g32[:, 0:1], 0.0)
                    tok["mu2"] = vpub(
                        vector,
                        vector.tensor_scalar(muk2[:], p_sb["mu_k"][:], -1.0, 1.0,
                                             ALU.mult, ALU.add))

                for ec in range(EC):
                    @block.sync
                    def _(sync, ec=ec):
                        if ec >= 2:
                            sync.wait_ge(sem.h["VE"], tok[("gdone", ec - 2)])
                        sem.inc(sync.dma_start(out=xs4[ec % 2][0][:],
                                               in_=x1_sp[0, ec]),
                                "LH%d" % (ec % 2), 16)
                        tok[("x5", ec)] = sem.inc(
                            sync.dma_start(out=xs4[ec % 2][1][:],
                                           in_=x1_sp[1, ec]),
                            "LH%d" % (ec % 2), 16)

                    for which, mucol, mu2col, spt in (
                            ("xr", p_sb["mu_r"], mur2, xr_sp),
                            ("xk", p_sb["mu_k"], muk2, xk_sp)):
                        @block.vector
                        def _(vector, ec=ec, which=which, mucol=mucol,
                              mu2col=mu2col):
                            if which == "xk":
                                vector.wait_ge(sem.h["STH"],
                                               tok[("xrsp", ec)])
                            if which == "xr":
                                vector.wait_ge(sem.h["LH%d" % (ec % 2)],
                                               tok[("x5", ec)])
                                if ec == 0:
                                    vector.wait_ge(sem.h["VE"], tok["mu2"])
                                if ec >= 1:
                                    vector.wait_ge(sem.h["STH"],
                                                   tok[("xksp", ec - 1)])
                                vector.tensor_tensor(t4[ec % 2][:],
                                                     xs4[ec % 2][0][:],
                                                     xs4[ec % 2][1][:], ALU.add)
                                vector.tensor_tensor(t4[ec % 2][:],
                                                     t4[ec % 2][:], m2b_[:],
                                                     ALU.subtract)
                                vector.tensor_tensor(g32[:, 1:T + 1],
                                                     t4[ec % 2][:], rstd2b_[:],
                                                     ALU.mult)
                            vector.tensor_scalar(xm32[:], g32[:, 1:T + 1],
                                                 mucol[:, ec:ec + 1], None,
                                                 ALU.mult)
                            vector.scalar_tensor_tensor(xm32[:], g32[:, 0:T],
                                                        mu2col[:, ec:ec + 1],
                                                        xm32[:], ALU.mult,
                                                        ALU.add)
                            vector.tensor_copy(xphi[:], xm32[:])
                            tok[(which, ec)] = sem.inc(
                                vector.scalar_tensor_tensor(
                                    xplo[:], xm32[:], 1.0, xphi[:], ALU.mult,
                                    ALU.subtract), "VE", 1)
                            if which == "xk":
                                tok[("gdone", ec)] = tok[("xk", ec)]

                        @block.sync
                        def _(sync, ec=ec, which=which, spt=spt):
                            sync.wait_ge(sem.h["VE"], tok[(which, ec)])
                            sem.inc(sync.dma_start(out=spt[0, ec], in_=xphi[:]),
                                    "STH", 16)
                            tok[(which + "sp", ec)] = sem.inc(
                                sync.dma_start(out=spt[1, ec], in_=xplo[:]),
                                "STH", 16)

            with nc.sbuf_tensor("w6a", [128, EC * 128], F32) as w6a, \
                 nc.sbuf_tensor("w6b", [128, EC * 128], F32) as w6b, \
                 nc.sbuf_tensor("w6ha", [128, EC * 128], F16) as w6ha, \
                 nc.sbuf_tensor("w6hb", [128, EC * 128], F16) as w6hb, \
                 nc.sbuf_tensor("w6la", [128, EC * 128], F16) as w6la, \
                 nc.sbuf_tensor("w6lb", [128, EC * 128], F16) as w6lb, \
                 nc.sbuf_tensor("kahi", [128, EC * T], F16) as _kh, \
                 nc.sbuf_tensor("kalo", [128, EC * T], F16) as _kl, \
                 nc.sbuf_tensor("r32a", [128, T], F32) as r32a, \
                 nc.sbuf_tensor("r32b", [128, T], F32) as r32c, \
                 nc.sbuf_tensor("xs6a0", [128, T], F16) as xs6a0, \
                 nc.sbuf_tensor("xs6a1", [128, T], F16) as xs6a1, \
                 nc.sbuf_tensor("xs6b0", [128, T], F16) as xs6b0, \
                 nc.sbuf_tensor("xs6b1", [128, T], F16) as xs6b1, \
                 nc.sbuf_tensor("s6", [128, T], F32) as s6, \
                 nc.sbuf_tensor("y32", [128, T], F32) as y32, \
                 nc.psum_tensor("psF", [128, T], F32) as psF, \
                 nc.psum_tensor("psF2", [128, T], F32) as psF2:

                def kchunk_hi(dc):
                    return _kh[:, dc * T:(dc + 1) * T]

                def kchunk_lo(dc):
                    return _kl[:, dc * T:(dc + 1) * T]

                psFs = [psF, psF2]
                w6 = [w6a, w6b]
                w6h = [w6ha, w6hb]
                w6l = [w6la, w6lb]
                r32s = [r32a, r32c]

                FSRC = ([("r", d) for d in range(EC)]
                        + [("k2", d) for d in range(EC)]
                        + [("v2", d) for d in range(EC)])

                @block.sync
                def _(sync):
                    sync.wait_ge(sem.h["VE"], tok[("gdone", EC - 1)])
                    sem.inc(sync.dma_start(out=w6[0][:], in_=wblk["r"][0]),
                            "LW0", 16)
                    tok[("w6", 0)] = sem.cur("LW0")

                def w6chain(gi):
                    sl = gi % 2

                    @block.scalar
                    def _(scalar, gi=gi, sl=sl):
                        scalar.wait_ge(sem.h["LW%d" % sl], tok[("w6", gi)])
                        if gi >= 2:
                            scalar.wait_ge(sem.h["TE"], tok[("fmm", gi - 2)])
                        tok[("w6hi", gi)] = sem.inc(
                            scalar.activation(w6h[sl][:], w6[sl][:], ACT.Copy),
                            "SC", 1)

                    @block.vector
                    def _(vector, gi=gi, sl=sl):
                        vector.wait_ge(sem.h["SC"], tok[("w6hi", gi)])
                        tok[("w6sp", gi)] = sem.inc(
                            vector.scalar_tensor_tensor(
                                w6l[sl][:], w6[sl][:], 1.0, w6h[sl][:],
                                ALU.mult, ALU.subtract), "VE", 1)

                    @block.sync
                    def _(sync, gi=gi):
                        if gi + 1 < len(FSRC):
                            osl = (gi + 1) % 2
                            sync.wait_ge(sem.h["SC"], tok[("w6hi", gi)])
                            m, d = FSRC[gi + 1]
                            sem.inc(sync.dma_start(out=w6[osl][:],
                                                   in_=wblk[m][d]),
                                    "LW%d" % osl, 16)
                            tok[("w6", gi + 1)] = sem.cur("LW%d" % osl)

                xs6 = [[xs6a0, xs6a1], [xs6b0, xs6b1]]

                def fmm_streamed(gi, spt, bufs):
                    for ec in range(EC):
                        @block.sync
                        def _(sync, gi=gi, ec=ec, spt=spt, bufs=bufs):
                            si = gi * EC + ec
                            if si >= 2:
                                sync.wait_ge(sem.h["TE"], tok[("f6u", si - 2)])
                            if gi < EC:
                                sync.wait_ge(sem.h["STH"], tok[("xrsp", ec)])
                            elif gi < 2 * EC:
                                sync.wait_ge(sem.h["STH"], tok[("xksp", ec)])
                            sem.inc(sync.dma_start(out=bufs[ec % 2][0][:],
                                                   in_=spt[0, ec]),
                                    "LH%d" % (ec % 2), 16)
                            tok[("f6l", si)] = sem.inc(
                                sync.dma_start(out=bufs[ec % 2][1][:],
                                               in_=spt[1, ec]),
                                "LH%d" % (ec % 2), 16)

                        @block.tensor
                        def _(tensor, gi=gi, ec=ec, bufs=bufs):
                            si = gi * EC + ec
                            tensor.wait_ge(sem.h["LH%d" % (ec % 2)],
                                           tok[("f6l", si)])
                            if ec == 0:
                                tensor.wait_ge(sem.h["VE"], tok[("w6sp", gi)])
                                if gi >= 2:
                                    tensor.wait_ge(sem.h["VE"],
                                                   tok[("fev", gi - 2)])
                            ps = psFs[gi % 2]
                            whi_ = w6h[gi % 2]
                            wlo_ = w6l[gi % 2]
                            first = (ec == 0)
                            ins = None
                            for qt in range(NQT):
                                sl = slice(qt * QT, (qt + 1) * QT)
                                tensor.matmul(ps[:, sl], wsl(whi_, ec),
                                              bufs[ec % 2][0][:, sl],
                                              start=first, stop=False)
                                tensor.matmul(ps[:, sl], wsl(whi_, ec),
                                              bufs[ec % 2][1][:, sl],
                                              start=False, stop=False)
                                ins = tensor.matmul(ps[:, sl], wsl(wlo_, ec),
                                                    bufs[ec % 2][0][:, sl],
                                                    start=False,
                                                    stop=(ec == EC - 1))
                            tok[("f6u", si)] = sem.inc(ins, "TE", 1)
                    tok[("fmm", gi)] = tok[("f6u", gi * EC + EC - 1)]

                for dc in range(EC):
                    gi = dc
                    w6chain(gi)
                    fmm_streamed(gi, xr_sp, xs6)

                    @block.scalar
                    def _(scalar, dc=dc):
                        scalar.wait_ge(sem.h["TE"], tok[("fmm", dc)])
                        if dc >= 2:
                            scalar.wait_ge(sem.h["STR"], tok[("rsp", dc - 2)])
                        tok[("rev", dc)] = sem.inc(
                            scalar.activation(r32s[dc % 2][:], psFs[dc % 2][:],
                                              ACT.Sigmoid), "SC", 1)

                    @block.vector
                    def _(vector, dc=dc):
                        vector.wait_ge(sem.h["SC"], tok[("rev", dc)])
                        tok[("fev", dc)] = sem.inc(
                            vector.tensor_copy(s6[:, 0:1],
                                               r32s[dc % 2][:, 0:1]), "VE", 1)

                    @block.sync
                    def _(sync, dc=dc):
                        sync.wait_ge(sem.h["SC"], tok[("rev", dc)])
                        tok[("rsp", dc)] = sem.inc(
                            sync.dma_start(out=r_sp[dc], in_=r32s[dc % 2][:]),
                            "STR", 16)

                for dc in range(EC):
                    gi = EC + dc
                    w6chain(gi)
                    fmm_streamed(gi, xk_sp, xs4)

                    @block.scalar
                    def _(scalar, dc=dc, gi=gi):
                        scalar.wait_ge(sem.h["TE"], tok[("fmm", gi)])
                        tok[("kev", dc)] = sem.inc(
                            scalar.activation(s6[:], psFs[gi % 2][:], ACT.Relu),
                            "SC", 1)

                    @block.vector
                    def _(vector, dc=dc, gi=gi):
                        vector.wait_ge(sem.h["SC"], tok[("kev", dc)])
                        vector.tensor_tensor(y32[:], s6[:], s6[:], ALU.mult)
                        vector.tensor_copy(kchunk_hi(dc), y32[:])
                        tok[("kact", dc)] = sem.inc(
                            vector.scalar_tensor_tensor(kchunk_lo(dc), y32[:],
                                                        1.0, kchunk_hi(dc),
                                                        ALU.mult, ALU.subtract),
                            "VE", 1)
                        tok[("fev", gi)] = tok[("kact", dc)]

                for dc in range(EC):
                    gi = 2 * EC + dc
                    w6chain(gi)

                    @block.tensor
                    def _(tensor, dc=dc, gi=gi):
                        tensor.wait_ge(sem.h["VE"], tok[("w6sp", gi)])
                        tensor.wait_ge(sem.h["VE"], tok[("fev", gi - 2)])
                        tensor.wait_ge(sem.h["VE"], tok[("kact", EC - 1)])
                        ps = psFs[gi % 2]
                        whi_ = w6h[gi % 2]
                        wlo_ = w6l[gi % 2]
                        ins = None
                        for ec in range(EC):
                            first = (ec == 0)
                            for qt in range(NQT):
                                sl = slice(qt * QT, (qt + 1) * QT)
                                tensor.matmul(ps[:, sl], wsl(whi_, ec),
                                              kchunk_hi(ec)[:, sl],
                                              start=first, stop=False)
                                tensor.matmul(ps[:, sl], wsl(whi_, ec),
                                              kchunk_lo(ec)[:, sl],
                                              start=False, stop=False)
                                ins = tensor.matmul(ps[:, sl], wsl(wlo_, ec),
                                                    kchunk_hi(ec)[:, sl],
                                                    start=False,
                                                    stop=(ec == EC - 1))
                        tok[("fmm", gi)] = sem.inc(ins, "TE", 1)

                    @block.sync
                    def _(sync, dc=dc):
                        if dc >= 2:
                            sync.wait_ge(sem.h["VE"], tok[("yev", dc - 2)])
                        sem.inc(sync.dma_start(out=r32s[dc % 2][:],
                                               in_=r_sp[dc]),
                                "LX%d" % (dc % 2), 16)
                        tok[("r6", dc)] = sem.cur("LX%d" % (dc % 2))
                        sem.inc(sync.dma_start(out=xs4[dc % 2][0][:],
                                               in_=x1_sp[0, dc]),
                                "LH%d" % (dc % 2), 16)
                        tok[("x6", dc)] = sem.inc(
                            sync.dma_start(out=xs4[dc % 2][1][:],
                                           in_=x1_sp[1, dc]),
                            "LH%d" % (dc % 2), 16)

                    @block.vector
                    def _(vector, dc=dc, gi=gi):
                        vector.wait_ge(sem.h["TE"], tok[("fmm", gi)])
                        vector.wait_ge(sem.h["LX%d" % (dc % 2)], tok[("r6", dc)])
                        vector.wait_ge(sem.h["LH%d" % (dc % 2)], tok[("x6", dc)])
                        if dc >= 1:
                            vector.wait_ge(sem.h["STY"], tok[("ysp", dc - 1)])
                        vector.tensor_tensor(s6[:], xs4[dc % 2][0][:],
                                             xs4[dc % 2][1][:], ALU.add)
                        vector.scalar_tensor_tensor(y32[:], psFs[gi % 2][:], 1.0,
                                                    r32s[dc % 2][:], ALU.mult,
                                                    ALU.mult)
                        tok[("yev", dc)] = sem.inc(
                            vector.tensor_tensor(y32[:], y32[:], s6[:], ALU.add),
                            "VE", 1)
                        tok[("fev", gi)] = tok[("yev", dc)]

                    @block.sync
                    def _(sync, dc=dc):
                        sync.wait_ge(sem.h["VE"], tok[("yev", dc)])
                        tok[("ysp", dc)] = sem.inc(
                            sync.dma_start(out=y[dc * 128:(dc + 1) * 128, :],
                                           in_=y32[:]), "STY", 16)

                @block.sync
                def _(sync):
                    sync.wait_ge(sem.h["STY"], sem.cur("STY"))
                    sync.wait_ge(sem.h["STR"], sem.cur("STR"))
    lower_extended_insts(nc)
    return nc


_BUILD_CACHE = {}


def _get_nc(tap=None):
    if tap not in _BUILD_CACHE:
        _BUILD_CACHE[tap] = build(tap)
    return _BUILD_CACHE[tap]


def _stage_inputs(kw):
    x = np.asarray(kw["L0_attn_x"])
    B = x.shape[0]
    xt = np.ascontiguousarray(x.transpose(0, 2, 1))
    c = _host_constants()
    base = {("c_" + k): v for k, v in c.items()}
    for name, key in [("bq", "L0_attn_bq"), ("bk", "L0_attn_bk"),
                      ("bv", "L0_attn_bv"), ("bg", "L0_attn_bg"),
                      ("bo", "L0_attn_bo"), ("gamma", "L0_attn_gamma"),
                      ("beta", "L0_attn_beta"), ("mu_r", "L0_ffn_mu_r"),
                      ("mu_k", "L0_ffn_mu_k")]:
        base["p_" + name] = _col8(np.asarray(kw[key]))
    base["p_bd"] = np.ascontiguousarray(np.asarray(kw["L0_attn_bd"])[None, :])
    base["w_dT"] = np.ascontiguousarray(
        np.asarray(kw["L0_attn_wd"]).T.reshape(EC, 128, H))
    for m, key in [("q", "L0_attn_wq"), ("k", "L0_attn_wk"), ("v", "L0_attn_wv"),
                   ("g", "L0_attn_wg"), ("o", "L0_attn_wo"), ("r", "L0_ffn_W_r"),
                   ("k2", "L0_ffn_W_k"), ("v2", "L0_ffn_W_v")]:
        base["w_" + m] = _blk_pair(np.ascontiguousarray(np.asarray(kw[key]).T))
    in_maps = []
    for b in range(B):
        d = dict(base)
        d["x_fm"] = xt[b]
        in_maps.append(d)
    return in_maps


def kernel(**kw):
    in_maps = _stage_inputs(kw)
    nc = _get_nc(None)
    res = run_bass_kernel_spmd(nc, in_maps, core_ids=list(range(8)))
    out = np.stack([np.asarray(r["y"]).T for r in res.results], axis=0)
    return out.astype(np.float32)


# version 11
# speedup vs baseline: 1.0940x; 1.0594x over previous
import contextlib
import math

import numpy as np
import concourse.bass as bass
import concourse.mybir as mybir
from concourse.bass_utils import run_bass_kernel_spmd
from concourse.bass_isa import ReduceOp
from concourse.library_config import attn as _attn_lib
from concourse.library_overlay import lower_extended_insts

F32 = mybir.dt.float32
F16 = mybir.dt.float16
ALU = mybir.AluOpType
ACT = mybir.ActivationFunctionType
AX = mybir.AxisListType

T = 2048
E = 1024
H = 16
HD = 64
EC = 8
NPAIR = 8
KC = 16
QT = 512
NQT = T // QT
LN2_ = math.log(2.0)
WQ, WK, WV, WG = 0, 1, 2, 3
SERIAL_ATTN = False


def _host_constants():
    t = np.arange(T, dtype=np.float64)
    j = np.arange(HD)
    inv = 1.0 / (10000.0 ** ((2.0 * (j % 32)) / HD))
    ang = t[None, :] * inv[:, None]
    cos = np.cos(ang)
    sin = np.sin(ang)
    sgn = np.where(j % 2 == 0, -1.0, 1.0)
    sin_s = sin * sgn[:, None]
    cos2 = np.concatenate([cos, cos], axis=0).astype(np.float32)
    sin2 = np.concatenate([sin_s, sin_s], axis=0).astype(np.float32)
    ii = np.arange(128)
    U = (ii[:, None] > ii[None, :]).astype(np.float32)
    L = (ii[:, None] <= ii[None, :]).astype(np.float32)
    kramp = (ii[:, None] + 128.0 * np.arange(KC)[None, :]).astype(np.float32)
    qramp = np.arange(QT, dtype=np.float32)[None, :]
    ident = np.eye(128, dtype=np.float32)
    ones_col = np.ones((128, 1), dtype=np.float32)
    return dict(cos2=cos2, sin2=sin2, U=U, L=L, kramp=kramp, qramp=qramp,
                ident=ident, ones_col=ones_col)


def _blk_pair(wt):
    return np.ascontiguousarray(wt.reshape(EC, 128, EC, 128).transpose(2, 1, 0, 3))


def _col8(v):
    return np.ascontiguousarray(v.reshape(8, 128).T)


class _Sems:
    def __init__(self, nc, stack, names):
        self.h = {}
        self.n = {}
        for name in names:
            self.h[name] = stack.enter_context(nc.semaphore(name))
            self.n[name] = 0

    def inc(self, ins, name, k=1):
        ins.then_inc(self.h[name], k)
        self.n[name] += k
        return self.n[name]

    def cur(self, name):
        return self.n[name]


def build(tap=None):
    nc = bass.Bass("TRN2", target_bir_lowering=False,
                   detect_race_conditions=False)

    x_fm = nc.dram_tensor("x_fm", [E, T], F32, kind="ExternalInput").ap()
    consts = {
        "cos2": [128, T], "sin2": [128, T], "U": [128, 128], "L": [128, 128],
        "kramp": [128, KC], "qramp": [1, QT], "ident": [128, 128],
        "ones_col": [128, 1],
    }
    cst = {k: nc.dram_tensor("c_" + k, sh, F32, kind="ExternalInput").ap()
           for k, sh in consts.items()}
    pnames = ["bq", "bk", "bv", "bg", "bo", "gamma", "beta", "mu_r", "mu_k"]
    par = {k: nc.dram_tensor("p_" + k, [128, 8], F32, kind="ExternalInput").ap()
           for k in pnames}
    bd = nc.dram_tensor("p_bd", [1, H], F32, kind="ExternalInput").ap()
    wdT = nc.dram_tensor("w_dT", [EC, 128, H], F32, kind="ExternalInput").ap()
    wblk = {m: nc.dram_tensor("w_" + m, [8, 128, EC, 128], F32,
                              kind="ExternalInput").ap()
            for m in ["q", "k", "v", "g", "o", "r", "k2", "v2"]}

    def sp16(name, tapped):
        kind = "ExternalOutput" if tapped else "Internal"
        return nc.dram_tensor(name, [2, EC, 128, T], F16, kind=kind).ap()

    h_sp = sp16("h_sp", tap in ("h", "all", "allq"))
    gon_sp = sp16("gon_sp", tap in ("gon", "all", "allq"))
    x1_sp = sp16("x1_sp", tap in ("x1", "all", "allq"))
    r_sp = nc.dram_tensor("r_sp", [EC, 128, T], F32,
                          kind="ExternalOutput" if tap in ("r", "all", "allq")
                          else "Internal").ap()
    if tap == "qk":
        dbg_qk = nc.dram_tensor("dbg_qk", [4, 128, T], F16,
                                kind="ExternalOutput").ap()
    if tap == "og":
        dbg_og = nc.dram_tensor("dbg_og", [2, 64, T], F32,
                                kind="ExternalOutput").ap()
    if tap in ("pb", "pb0"):
        dbg_pb = nc.dram_tensor("dbg_pb", [2, 128, KC * QT], F32,
                                kind="ExternalOutput").ap()
    if tap == "cs0":
        dbg_cs = nc.dram_tensor("dbg_cs", [66, QT], F32,
                                kind="ExternalOutput").ap()
    if tap in ("ld", "hm"):
        dbg_ld = nc.dram_tensor("dbg_ld", [4, 1, H], F32,
                                kind="ExternalOutput").ap()
        dbg_hm = nc.dram_tensor("dbg_hm", [128, EC], F32,
                                kind="ExternalOutput").ap()
        dbg_hs = nc.dram_tensor("dbg_hs", [2, 128, EC], F32,
                                kind="ExternalOutput").ap()
    y = nc.dram_tensor("y", [E, T], F32, kind="ExternalOutput").ap()
    if tap == "all":
        dbg_b = nc.dram_tensor("dbg_b", [NPAIR, 4, 2], F32,
                               kind="ExternalOutput").ap()
        dbg_ogall = nc.dram_tensor("dbg_ogall", [NPAIR, 2, 64, T], F32,
                                   kind="ExternalOutput").ap()
        dbg_gate = nc.dram_tensor("dbg_gate", [NPAIR, 2, 64, T], F32,
                                  kind="ExternalOutput").ap()
        dbg_r2 = nc.dram_tensor("dbg_r2", [NPAIR, 1, 16], F32,
                                kind="ExternalOutput").ap()

    TAP_PAIRS = 0 if tap in ("ld", "hm") else (1 if tap in ("qk", "og", "pb", "pb0", "cs0") else NPAIR)
    tap_now = [None]

    with contextlib.ExitStack() as st:
        sem = _Sems(nc, st, ["LD", "TE", "SC", "VE", "GP",
                             "LX0", "LX1", "LCS", "LW0", "LW1", "LH0", "LH1",
                             "STH", "STG", "STX", "STR", "STY"])
        sb = lambda name, shape, dt=F32: st.enter_context(
            nc.sbuf_tensor(name, shape, dt))
        block = st.enter_context(nc.Block())

        c_U = sb("sb_U", [128, 128])
        c_L = sb("sb_L", [128, 128])
        c_kramp = sb("sb_kramp", [128, KC])
        c_qramp = sb("sb_qramp", [1, QT])
        c_ident = sb("sb_ident", [128, 128])
        c_ones = sb("sb_ones", [128, 1])
        ones16 = sb("sb_ones16", [128, 1], F16)
        p_sb = {k: sb("p" + k, [128, 8]) for k in pnames}
        p_bd = sb("pbd", [1, H])
        pdrow = sb("pdrow", [1, H])
        ldrow = sb("ldrow", [1, H])
        ldnrow = sb("ldnrow", [1, H])
        hrow = sb("hrow", [1, H])
        ldb = sb("ldb", [128, H])
        ldks = sb("ldks", [128, H * KC])
        eks = sb("eks", [128, H * KC])
        hsum = sb("hsum", [128, EC])
        hsum2 = sb("hsum2", [128, EC])
        hmean = sb("hmean", [128, EC])
        browq = sb("browq", [1, 2])
        browk = sb("browk", [1, 2])
        bbc = sb("bbc", [128, 2])
        ldnb = sb("ldnb", [128, 2])
        bfh = sb("bfh", [128, 2 * KC])
        gnb = sb("gnb", [64, 4])
        gnc = sb("gnc", [64, 4])
        r2t = sb("r2t", [1, 16])
        lqrow = sb("lqrow", [1, 8])
        ctb = sb("ctb", [128, 2])
        flsh = sb("flsh", [8, 1024])

        tok = {}
        prev = {}
        stB = None

        def vsync(vector, ins):
            t = sem.inc(ins, "VE", 1)
            vector.wait_ge(sem.h["VE"], t)
            return t

        def vpub(vector, ins):
            t = sem.inc(ins, "VE", 1)
            vector.wait_ge(sem.h["VE"], t)
            return sem.inc(vector.tensor_copy(flsh[:, 512:1024],
                                              flsh[:, 0:512]), "VE", 1)

        def spub(scalar, ins):
            t = sem.inc(ins, "SC", 1)
            scalar.wait_ge(sem.h["SC"], t)
            return sem.inc(scalar.activation(flsh[:, 0:512],
                                             flsh[:, 512:1024], ACT.Identity),
                           "SC", 1)

        with nc.sbuf_tensor("xb0", [128, T], F32) as xb0, \
             nc.sbuf_tensor("xb1", [128, T], F32) as xb1, \
             nc.sbuf_tensor("sq0", [128, T], F32) as sq0, \
             nc.sbuf_tensor("sq1", [128, T], F32) as sq1, \
             nc.sbuf_tensor("m1row", [1, T], F32) as m1row, \
             nc.sbuf_tensor("rstd1row", [1, T], F32) as rstd1row, \
             nc.sbuf_tensor("m1b", [128, T], F32) as m1b, \
             nc.sbuf_tensor("rstd1b", [128, T], F32) as rstd1b, \
             nc.sbuf_tensor("hhib", [128, T], F16) as hhib, \
             nc.sbuf_tensor("hlob", [128, T], F16) as hlob, \
             nc.sbuf_tensor("wdTs", [128, EC * H], F32) as w_dT, \
             nc.psum_tensor("psS", [1, T], F32) as psS, \
             nc.psum_tensor("psQ", [1, T], F32) as psQ:

            xb = [xb0, xb1]
            sq = [sq0, sq1]
            t1b = sq0
            h32b = sq1
            vrow = psS[0:1, T // 2:T]

            @block.sync
            def _(sync):
                for k, tile in [("U", c_U), ("L", c_L), ("kramp", c_kramp),
                                ("qramp", c_qramp), ("ident", c_ident),
                                ("ones_col", c_ones)]:
                    sem.inc(sync.dma_start(out=tile[:], in_=cst[k]), "LD", 16)
                for k in pnames:
                    sem.inc(sync.dma_start(out=p_sb[k][:], in_=par[k]), "LD", 16)
                sem.inc(sync.dma_start(out=p_bd[:], in_=bd), "LD", 16)
                for ec in range(EC):
                    sem.inc(sync.dma_start(out=w_dT[:, ec * H:(ec + 1) * H],
                                           in_=wdT[ec]), "LD", 16)
                tok["consts"] = sem.cur("LD")

            @block.gpsimd
            def _(gpsimd):
                gpsimd.load_library(_attn_lib)
                sem.inc(gpsimd.engine_nop(), "GP", 1)

            @block.vector
            def _(vector):
                vector.wait_ge(sem.h["LD"], tok["consts"])
                vector.memset(ones16[:], 1.0)
                vector.memset(flsh[:], 0.0)
                vector.memset(ctb[:, 0:1], 1e-5)
                sem.inc(vector.memset(ctb[:, 1:2], -5.0 * LN2_), "VE", 1)

            for ec in range(EC):
                @block.sync
                def _(sync, ec=ec):
                    if ec >= 2:
                        sync.wait_ge(sem.h["TE"], tok[("st_te", ec - 2)])
                    tok[("x0", ec)] = sem.inc(
                        sync.dma_start(out=xb[ec % 2][:],
                                       in_=x_fm[ec * 128:(ec + 1) * 128, :]),
                        "LX%d" % (ec % 2), 16)

                @block.scalar
                def _(scalar, ec=ec):
                    scalar.wait_ge(sem.h["LX%d" % (ec % 2)], tok[("x0", ec)])
                    if ec >= 2:
                        scalar.wait_ge(sem.h["TE"], tok[("st_te", ec - 2)])
                    tok[("sq", ec)] = sem.inc(
                        scalar.activation(sq[ec % 2][:], xb[ec % 2][:], ACT.Square),
                        "SC", 1)

                @block.tensor
                def _(tensor, ec=ec):
                    tensor.wait_ge(sem.h["LX%d" % (ec % 2)], tok[("x0", ec)])
                    if ec == 0:
                        tensor.wait_ge(sem.h["LD"], tok["consts"])
                    for qt in range(NQT):
                        tensor.matmul(psS[0:1, qt * QT:(qt + 1) * QT], c_ones[:],
                                      xb[ec % 2][:, qt * QT:(qt + 1) * QT],
                                      start=(ec == 0), stop=(ec == EC - 1))
                    tensor.wait_ge(sem.h["SC"], tok[("sq", ec)])
                    ins = None
                    for qt in range(NQT):
                        ins = tensor.matmul(psQ[0:1, qt * QT:(qt + 1) * QT],
                                            c_ones[:],
                                            sq[ec % 2][:, qt * QT:(qt + 1) * QT],
                                            start=(ec == 0), stop=(ec == EC - 1))
                    tok[("st_te", ec)] = sem.inc(ins, "TE", 1)

            @block.vector
            def _(vector):
                vector.wait_ge(sem.h["TE"], tok[("st_te", EC - 1)])
                vector.tensor_scalar(m1row[:], psS[0:1, :], 1.0 / E, None, ALU.mult)
                vector.tensor_tensor(rstd1row[:], m1row[:], m1row[:], ALU.mult)
                tok["var"] = sem.inc(
                    vector.scalar_tensor_tensor(rstd1row[:], psQ[0:1, :], 1.0 / E,
                                                rstd1row[:], ALU.mult,
                                                ALU.subtract), "VE", 1)

            @block.scalar
            def _(scalar):
                scalar.wait_ge(sem.h["VE"], tok["var"])
                tok["sqrt"] = sem.inc(
                    scalar.activation(rstd1row[:], rstd1row[:], ACT.Sqrt,
                                      bias=ctb[0:1, 0:1]), "SC", 1)

            @block.vector
            def _(vector):
                vector.wait_ge(sem.h["SC"], tok["sqrt"])
                tok["rstd"] = vpub(
                    vector, vector.reciprocal(rstd1row[:], rstd1row[:]))

            @block.gpsimd
            def _(gpsimd):
                gpsimd.wait_ge(sem.h["VE"], tok["rstd"])
                gpsimd.partition_broadcast(m1b[:], m1row[:], 128)
                tok["bcast"] = sem.inc(
                    gpsimd.partition_broadcast(rstd1b[:], rstd1row[:], 128), "GP", 1)

            for ec in range(EC):
                @block.sync
                def _(sync, ec=ec):
                    if ec >= 2:
                        sync.wait_ge(sem.h["VE"], tok[("h32", ec - 2)])
                    else:
                        sync.wait_ge(sem.h["SC"], tok[("sq", EC - 1)])
                        sync.wait_ge(sem.h["TE"], tok[("st_te", EC - 1)])
                    tok[("x1", ec)] = sem.inc(
                        sync.dma_start(out=xb[ec % 2][:],
                                       in_=x_fm[ec * 128:(ec + 1) * 128, :]),
                        "LX%d" % (ec % 2), 16)

                @block.vector
                def _(vector, ec=ec):
                    vector.wait_ge(sem.h["LX%d" % (ec % 2)], tok[("x1", ec)])
                    if ec == 0:
                        vector.wait_ge(sem.h["GP"], tok["bcast"])
                    if ec >= 1:
                        vector.wait_ge(sem.h["SC"], tok[("hhi", ec - 1)])
                    vector.tensor_tensor(t1b[:], xb[ec % 2][:], m1b[:], ALU.subtract)
                    tok[("h32", ec)] = sem.inc(
                        vector.tensor_tensor(h32b[:], t1b[:], rstd1b[:], ALU.mult),
                        "VE", 1)

                @block.scalar
                def _(scalar, ec=ec):
                    scalar.wait_ge(sem.h["VE"], tok[("h32", ec)])
                    if ec >= 1:
                        scalar.wait_ge(sem.h["STH"], tok[("hsp", ec - 1)])
                    tok[("hhi", ec)] = sem.inc(
                        scalar.activation(hhib[:], h32b[:], ACT.Copy), "SC", 1)

                @block.vector
                def _(vector, ec=ec):
                    vector.wait_ge(sem.h["SC"], tok[("hhi", ec)])
                    vector.scalar_tensor_tensor(
                        hlob[:], h32b[:], 1.0, hhib[:], ALU.mult, ALU.subtract)
                    vsync(vector, vector.reduce_sum(hsum[:, ec:ec + 1], hhib[:],
                                                    AX.X))
                    tok[("hlo", ec)] = vsync(
                        vector, vector.reduce_sum(hsum2[:, ec:ec + 1], hlob[:],
                                                  AX.X))

                @block.sync
                def _(sync, ec=ec):
                    sync.wait_ge(sem.h["VE"], tok[("hlo", ec)])
                    sem.inc(sync.dma_start(out=h_sp[0, ec], in_=hhib[:]), "STH", 16)
                    tok[("hsp", ec)] = sem.inc(
                        sync.dma_start(out=h_sp[1, ec], in_=hlob[:]), "STH", 16)

            @block.vector
            def _(vector):
                vector.wait_ge(sem.h["VE"], tok[("hlo", EC - 1)])
                tok["hmean"] = vpub(
                    vector,
                    vector.tensor_tensor(hmean[:], hsum[:], hsum2[:], ALU.add))

            if tap == "hm":
                @block.sync
                def _(sync):
                    sync.wait_ge(sem.h["VE"], tok["hmean"])
                    sem.inc(sync.dma_start(out=dbg_hm, in_=hmean[:]), "STY", 16)
                    sem.inc(sync.dma_start(out=dbg_hs[0], in_=hsum[:]), "STY", 16)
                    sem.inc(sync.dma_start(out=dbg_hs[1], in_=hsum2[:]), "STY", 16)
                    sync.wait_ge(sem.h["STY"], sem.cur("STY"))
                    sync.wait_ge(sem.h["STH"], sem.cur("STH"))

            @block.tensor
            def _(tensor):
                tensor.wait_ge(sem.h["VE"], tok["hmean"])
                ins = None
                for ec in range(EC):
                    ins = tensor.matmul(psS[0:1, 0:H], hmean[:, ec:ec + 1],
                                        w_dT[:, ec * H:(ec + 1) * H],
                                        start=(ec == 0), stop=(ec == EC - 1),
                                        skip_group_check=True)
                tok["pdmm"] = sem.inc(ins, "TE", 1)

            @block.vector
            def _(vector):
                vector.wait_ge(sem.h["TE"], tok["pdmm"])
                tok["pdl"] = vpub(
                    vector,
                    vector.scalar_tensor_tensor(pdrow[:], psS[0:1, 0:H], 1.0 / T,
                                                p_bd[:], ALU.mult, ALU.add))

            @block.scalar
            def _(scalar):
                scalar.wait_ge(sem.h["VE"], tok["pdl"])
                ts = sem.inc(scalar.activation(pdrow[:], pdrow[:], ACT.Sigmoid),
                             "SC", 1)
                scalar.wait_ge(sem.h["SC"], ts)
                tok["pd"] = spub(
                    scalar,
                    scalar.activation(hrow[:], pdrow[:], ACT.Exp,
                                      bias=ctb[0:1, 1:2], scale=-8.0 * LN2_))

            @block.vector
            def _(vector):
                vector.wait_ge(sem.h["SC"], tok["pd"])
                tok["w1"] = vpub(
                    vector,
                    vector.tensor_scalar(ldnrow[:], hrow[:], -1.0, 1.0, ALU.mult,
                                         ALU.add))

            @block.scalar
            def _(scalar):
                scalar.wait_ge(sem.h["VE"], tok["w1"])
                tok["ld"] = spub(
                    scalar, scalar.activation(ldrow[:], ldnrow[:], ACT.Ln))

            @block.vector
            def _(vector):
                vector.wait_ge(sem.h["SC"], tok["ld"])
                tok["ldn"] = sem.inc(
                    vector.tensor_scalar(ldnrow[:], ldrow[:], -1.0, None, ALU.mult),
                    "VE", 1)

            @block.gpsimd
            def _(gpsimd):
                gpsimd.wait_ge(sem.h["VE"], tok["ldn"])
                ins = None
                for h in range(H):
                    ins = gpsimd.partition_broadcast(ldb[:, h:h + 1],
                                                     ldrow[0:1, h:h + 1], 128)
                tok["ldb"] = sem.inc(ins, "GP", 1)

            @block.vector
            def _(vector):
                vector.wait_ge(sem.h["GP"], tok["ldb"])
                ins = None
                for h in range(H):
                    ins = vector.tensor_scalar(ldks[:, h * KC:(h + 1) * KC],
                                               c_kramp[:], ldb[:, h:h + 1],
                                               None, ALU.mult)
                tok["ldks"] = sem.inc(ins, "VE", 1)

            @block.scalar
            def _(scalar):
                scalar.wait_ge(sem.h["VE"], tok["ldks"])
                tok["eks"] = sem.inc(
                    scalar.activation(eks[:], ldks[:], ACT.Exp, scale=-1.0), "SC", 1)

        if tap == "h":
            @block.sync
            def _(sync):
                sync.wait_ge(sem.h["STH"], sem.cur("STH"))
                sync.wait_ge(sem.h["SC"], sem.cur("SC"))
            lower_extended_insts(nc)
            return nc

        if tap == "hm":
            lower_extended_insts(nc)
            return nc

        if tap == "ld":
            @block.sync
            def _(sync):
                sync.wait_ge(sem.h["SC"], tok["eks"])
                sem.inc(sync.dma_start(out=dbg_ld[0], in_=pdrow[:]), "STY", 16)
                sem.inc(sync.dma_start(out=dbg_ld[1], in_=hrow[:]), "STY", 16)
                sem.inc(sync.dma_start(out=dbg_ld[2], in_=ldrow[:]), "STY", 16)
                sem.inc(sync.dma_start(out=dbg_ld[3], in_=ldnrow[:]), "STY", 16)
                sem.inc(sync.dma_start(out=dbg_hm, in_=hmean[:]), "STY", 16)
                sem.inc(sync.dma_start(out=dbg_hs[0], in_=hsum[:]), "STY", 16)
                sem.inc(sync.dma_start(out=dbg_hs[1], in_=hsum2[:]), "STY", 16)
                sync.wait_ge(sem.h["STY"], sem.cur("STY"))
                sync.wait_ge(sem.h["STH"], sem.cur("STH"))
            lower_extended_insts(nc)
            return nc

        stB = st.enter_context(contextlib.ExitStack())
        sbB = lambda name, shape, dt=F32: stB.enter_context(
            nc.sbuf_tensor(name, shape, dt))
        arena = sbB("arena", [128, 2 * KC * QT])
        wst_ = sbB("wst0", [128, EC * 128])
        wst = [wst_, wst_]
        whi = [sbB(f"whi{i}", [128, EC * 128], F16) for i in range(4)]
        wlo = [sbB(f"wlo{i}", [128, EC * 128], F16) for i in range(4)]
        hstr = [[sbB(f"hstr{i}_{d}", [128, T], F16) for d in range(2)]
                for i in range(2)]
        qhi = sbB("qhi", [128, T], F16)
        qlo = sbB("qlo", [128, T], F16)
        khi = sbB("khi", [128, T], F16)
        klo = sbB("klo", [128, T], F16)
        vtm = [sbB(f"vtm{hh}", [128, KC * 65]) for hh in range(2)]
        vsc = [sbB(f"vsc{hh}", [128, KC * 65]) for hh in range(2)]
        gateh = [sbB(f"gateh{hh}", [64, T]) for hh in range(2)]
        ogt = [sbB(f"ogt{hh}", [64, T]) for hh in range(2)]
        stripL = sbB("stripL", [128, 2 * 2 * 128])
        stripU = sbB("stripU", [128, 2 * 2 * 128])
        csc1 = sbB("csc1", [65, QT])
        linvb = sbB("linvb", [64, QT])
        dbb1v = sbB("dbb1", [65, QT])
        dbb1 = [dbb1v, dbb1v]
        r1t = sbB("r1t", [1, QT])
        sqsc = sbB("sqsc", [128, T], F16)

        cosv = arena[:, 0:T]
        sinv = arena[:, T:2 * T]
        pbuf = [arena[:, 0:KC * QT], arena[:, KC * QT:2 * KC * QT]]
        shuf_mask = [i + 1 if i % 2 == 0 else i - 1 for i in range(32)]

        def wsl(buf, ec):
            return buf[:, ec * 128:(ec + 1) * 128]

        def _mm(tensor, dst, lhsT, rhs, key, state, stop=False):
            first = key not in state
            state[key] = True
            return tensor.matmul(dst, lhsT, rhs, start=first, stop=stop,
                                 skip_group_check=True)

        def emit_wchain(pp):
            wm = [("q", ("hu1", pp - 1, EC - 1)), ("k", ("hu1", pp - 1, EC - 1)),
                  ("v", ("hu2", pp - 1, EC - 1)), ("g", ("hu2", pp - 1, EC - 1))]
            for i, (m, conflict) in enumerate(wm):
                @block.sync
                def _(sync, pp=pp, i=i, m=m):
                    guard = ("wsp", pp, i - 1) if i >= 1 else \
                        ("wsp", pp - 1, 3)
                    if guard in tok:
                        sync.wait_ge(sem.h["VE"], tok[guard])
                    tok[("w32", pp, i)] = sem.inc(
                        sync.dma_start(out=wst[i % 2][:], in_=wblk[m][pp]),
                        "LW%d" % (i % 2), 16)

                @block.scalar
                def _(scalar, pp=pp, i=i, conflict=conflict):
                    scalar.wait_ge(sem.h["LW%d" % (i % 2)], tok[("w32", pp, i)])
                    if pp > 0:
                        scalar.wait_ge(sem.h["TE"], tok[conflict])
                    tok[("whi", pp, i)] = sem.inc(
                        scalar.activation(whi[i][:], wst[i % 2][:], ACT.Copy),
                        "SC", 1)

                @block.vector
                def _(vector, pp=pp, i=i):
                    vector.wait_ge(sem.h["SC"], tok[("whi", pp, i)])
                    tok[("wsp", pp, i)] = sem.inc(
                        vector.scalar_tensor_tensor(
                            wlo[i][:], wst[i % 2][:], 1.0, whi[i][:],
                            ALU.mult, ALU.subtract), "VE", 1)

        emit_wchain(0)

        for p in range(TAP_PAIRS):
            with nc.psum_tensor(f"psA_{p}", [128, T], F32) as psA, \
                 nc.psum_tensor(f"psB_{p}", [128, T], F32) as psB, \
                 nc.sbuf_tensor(f"q32_{p}", [128, T], F32) as q32:

                s1 = arena[:, 2 * T:3 * T]

                def wbuf_hi(i):
                    return whi[i]

                def wbuf_lo(i):
                    return wlo[i]

                @block.gpsimd
                def _(gpsimd, p=p):
                    if p > 0:
                        gpsimd.wait_ge(sem.h["TE"], prev[("pvdone", p - 1)])
                    sem.inc(gpsimd.dma_start(out=cosv, in_=cst["cos2"]),
                            "LCS", 16)
                    tok[("cs", p)] = sem.inc(
                        gpsimd.dma_start(out=sinv, in_=cst["sin2"]), "LCS", 16)

                for ec in range(EC):
                    @block.sync
                    def _(sync, p=p, ec=ec):
                        if ec >= 2:
                            sync.wait_ge(sem.h["TE"], tok[("hu1", p, ec - 2)])
                        elif p > 0:
                            sync.wait_ge(sem.h["TE"],
                                         prev[("hu2", p - 1, EC - 2 + ec)])
                        else:
                            sync.wait_ge(sem.h["STH"], tok[("hsp", EC - 1)])
                        sem.inc(sync.dma_start(out=hstr[ec % 2][0][:],
                                               in_=h_sp[0, ec]),
                                "LH%d" % (ec % 2), 16)
                        tok[("hl1", p, ec)] = sem.inc(
                            sync.dma_start(out=hstr[ec % 2][1][:],
                                           in_=h_sp[1, ec]),
                            "LH%d" % (ec % 2), 16)

                    @block.tensor
                    def _(tensor, p=p, ec=ec):
                        tensor.wait_ge(sem.h["LH%d" % (ec % 2)],
                                       tok[("hl1", p, ec)])
                        if ec == 0:
                            if p > 0:
                                tensor.wait_ge(sem.h["SC"],
                                               prev[("expdone", p - 1)])
                                tensor.wait_ge(sem.h["VE"],
                                               prev[("combdone", p - 1)])
                            tensor.wait_ge(sem.h["VE"], tok[("wsp", p, WK)])
                        ins = None
                        for i, ps in ((WQ, psA), (WK, psB)):
                            first = (ec == 0)
                            for qt in range(NQT):
                                sl = slice(qt * QT, (qt + 1) * QT)
                                tensor.matmul(ps[:, sl], wsl(wbuf_hi(i), ec),
                                              hstr[ec % 2][0][:, sl],
                                              start=first, stop=False)
                                tensor.matmul(ps[:, sl], wsl(wbuf_hi(i), ec),
                                              hstr[ec % 2][1][:, sl],
                                              start=False, stop=False)
                                ins = tensor.matmul(ps[:, sl], wsl(wbuf_lo(i), ec),
                                                    hstr[ec % 2][0][:, sl],
                                                    start=False,
                                                    stop=(ec == EC - 1))
                        tok[("hu1", p, ec)] = sem.inc(ins, "TE", 1)
                tok[("qkmm", p)] = tok[("hu1", p, EC - 1)]

                @block.scalar
                def _(scalar, p=p):
                    scalar.wait_ge(sem.h["TE"], tok[("qkmm", p)])
                    if p > 0:
                        scalar.wait_ge(sem.h["VE"], prev[("rope", p - 1)])
                    tok[("qev", p)] = sem.inc(
                        scalar.activation(q32[:], psA[:], ACT.Identity,
                                          bias=p_sb["bq"][:, p:p + 1]), "SC", 1)

                def rope_ops(vector, srct, dhi, dlo):
                    vector.stream_shuffle(s1[:], srct[:], shuf_mask)
                    vector.tensor_tensor(s1[:], s1[:], sinv, ALU.mult)
                    vector.tensor_tensor(srct[:], srct[:], cosv, ALU.mult)
                    vector.tensor_tensor(srct[:], srct[:], s1[:], ALU.add)
                    vector.tensor_copy(dhi[:], srct[:])
                    return vector.scalar_tensor_tensor(dlo[:], srct[:], 1.0,
                                                       dhi[:], ALU.mult,
                                                       ALU.subtract)

                @block.vector
                def _(vector, p=p):
                    vector.wait_ge(sem.h["SC"], tok[("qev", p)])
                    vector.wait_ge(sem.h["LCS"], tok[("cs", p)])
                    if p > 0:
                        vector.wait_ge(sem.h["TE"], prev[("scdone", p - 1)])
                    tok[("ropeq", p)] = sem.inc(rope_ops(vector, q32, qhi, qlo),
                                                "VE", 1)

                @block.scalar
                def _(scalar, p=p):
                    scalar.wait_ge(sem.h["VE"], tok[("ropeq", p)])
                    tok[("kev", p)] = sem.inc(
                        scalar.activation(q32[:], psB[:], ACT.Identity,
                                          bias=p_sb["bk"][:, p:p + 1]), "SC", 1)

                @block.vector
                def _(vector, p=p):
                    vector.wait_ge(sem.h["SC"], tok[("kev", p)])
                    tok[("rope", p)] = sem.inc(rope_ops(vector, q32, khi, klo),
                                               "VE", 1)
                    prev[("rope", p)] = tok[("rope", p)]

                for ec in range(EC):
                    @block.sync
                    def _(sync, p=p, ec=ec):
                        if ec >= 2:
                            sync.wait_ge(sem.h["TE"], tok[("hu2", p, ec - 2)])
                        else:
                            sync.wait_ge(sem.h["TE"], tok[("hu1", p, EC - 2 + ec)])
                        sem.inc(sync.dma_start(out=hstr[ec % 2][0][:],
                                               in_=h_sp[0, ec]),
                                "LH%d" % (ec % 2), 16)
                        tok[("hl2", p, ec)] = sem.inc(
                            sync.dma_start(out=hstr[ec % 2][1][:],
                                           in_=h_sp[1, ec]),
                            "LH%d" % (ec % 2), 16)

                    @block.tensor
                    def _(tensor, p=p, ec=ec):
                        tensor.wait_ge(sem.h["LH%d" % (ec % 2)],
                                       tok[("hl2", p, ec)])
                        if ec == 0:
                            tensor.wait_ge(sem.h["SC"], tok[("kev", p)])
                            tensor.wait_ge(sem.h["VE"], tok[("wsp", p, WV)])
                            tensor.wait_ge(sem.h["VE"], tok[("wsp", p, WG)])
                        ins = None
                        for i, ps in ((WV, psA), (WG, psB)):
                            first = (ec == 0)
                            for qt in range(NQT):
                                sl = slice(qt * QT, (qt + 1) * QT)
                                tensor.matmul(ps[:, sl], wsl(wbuf_hi(i), ec),
                                              hstr[ec % 2][0][:, sl],
                                              start=first, stop=False)
                                tensor.matmul(ps[:, sl], wsl(wbuf_hi(i), ec),
                                              hstr[ec % 2][1][:, sl],
                                              start=False, stop=False)
                                ins = tensor.matmul(ps[:, sl], wsl(wbuf_lo(i), ec),
                                                    hstr[ec % 2][0][:, sl],
                                                    start=False,
                                                    stop=(ec == EC - 1))
                        tok[("hu2", p, ec)] = sem.inc(ins, "TE", 1)
                        prev[("hu2", p, ec)] = tok[("hu2", p, ec)]
                tok[("vgmm", p)] = tok[("hu2", p, EC - 1)]
                prev[("projdone", p)] = tok[("vgmm", p)]

                @block.scalar
                def _(scalar, p=p):
                    scalar.wait_ge(sem.h["VE"], tok[("rope", p)])
                    if p > 0:
                        scalar.wait_ge(sem.h["VE"], prev[("gsq", p - 1)])
                    tok[("sqq", p)] = sem.inc(
                        scalar.activation(sqsc[:], qhi[:], ACT.Square), "SC", 1)

                @block.scalar
                def _(scalar, p=p):
                    scalar.wait_ge(sem.h["TE"], tok[("vgmm", p)])
                    scalar.wait_ge(sem.h["VE"], tok[("rope", p)])
                    tok[("vev", p)] = sem.inc(
                        scalar.activation(q32[:], psA[:], ACT.Identity,
                                          bias=p_sb["bv"][:, p:p + 1]), "SC", 1)
                    if p > 0:
                        scalar.wait_ge(sem.h["VE"], prev[("gonve", p - 1)])
                    scalar.activation(gateh[0][:], psB[0:64, :], ACT.Silu,
                                      bias=p_sb["bg"][0:64, p:p + 1])
                    tok[("gev", p)] = sem.inc(
                        scalar.activation(gateh[1][:], psB[64:128, :], ACT.Silu,
                                          bias=p_sb["bg"][64:128, p:p + 1]),
                        "SC", 1)

                if p + 1 < TAP_PAIRS:
                    emit_wchain(p + 1)

                @block.tensor
                def _(tensor, p=p):
                    tensor.wait_ge(sem.h["SC"], tok[("vev", p)])
                    if p > 0:
                        tensor.wait_ge(sem.h["VE"], prev[("vscd", p - 1)])
                    ins = None
                    for c in range(KC):
                        ins = tensor.matmul(psA[:, c * 128:(c + 1) * 128],
                                            q32[:, c * 128:(c + 1) * 128],
                                            c_ident[:], start=(c % 4 == 0),
                                            stop=(c % 4 == 3),
                                            skip_group_check=True)
                    tok[("vt", p)] = sem.inc(ins, "TE", 1)

                @block.vector
                def _(vector, p=p):
                    vector.wait_ge(sem.h["TE"], tok[("vt", p)])
                    ins = None
                    for c in range(KC):
                        src = psA[:, c * 128:(c + 1) * 128]
                        vector.tensor_copy(vtm[0][:, c * 65:c * 65 + 64],
                                           src[:, 0:64])
                        ins = vector.tensor_copy(vtm[1][:, c * 65:c * 65 + 64],
                                                 src[:, 64:128])
                    tok[("vtcall", p)] = sem.inc(ins, "VE", 1)

                @block.vector
                def _(vector, p=p):
                    vector.memset(vtm[0][:, 64::65], 1.0)
                    vector.memset(vtm[1][:, 64::65], 1.0)
                    vector.wait_ge(sem.h["SC"], tok["eks"])
                    ins = None
                    for hh in range(2):
                        h = 2 * p + hh
                        for c in range(KC):
                            ins = vector.tensor_scalar(
                                vsc[hh][:, c * 65:(c + 1) * 65],
                                vtm[hh][:, c * 65:(c + 1) * 65],
                                eks[:, h * KC + c:h * KC + c + 1], None, ALU.mult)
                    tok[("vscd", p)] = sem.inc(ins, "VE", 1)
                    prev[("vscd", p)] = tok[("vscd", p)]

                @block.tensor
                def _(tensor, p=p):
                    tensor.wait_ge(sem.h["SC"], tok[("sqq", p)])
                    tensor.wait_ge(sem.h["SC"], tok[("gev", p)])
                    tensor.wait_ge(sem.h["VE"], tok[("vtcall", p)])
                    for qt in range(NQT):
                        sl = slice(qt * QT, (qt + 1) * QT)
                        tensor.matmul(psB[0:1, sl], ones16[0:64, 0:1],
                                      sqsc[0:64, sl], start=True, stop=True,
                                      skip_group_check=True)
                        ins = tensor.matmul(psA[0:1, sl], ones16[64:128, 0:1],
                                            sqsc[64:128, sl], start=True,
                                            stop=True, tile_position=(64, 0),
                                            skip_group_check=True)
                    tok[("nrmq", p)] = sem.inc(ins, "TE", 1)

                @block.vector
                def _(vector, p=p):
                    vector.wait_ge(sem.h["TE"], tok[("nrmq", p)])
                    vsync(vector, vector.tensor_reduce(browq[0:1, 0:1],
                                                       psB[0:1, :], AX.X, ALU.max))
                    tok[("nrq", p)] = vsync(
                        vector, vector.tensor_reduce(browq[0:1, 1:2], psA[0:1, :],
                                                     AX.X, ALU.max))

                @block.scalar
                def _(scalar, p=p):
                    scalar.wait_ge(sem.h["TE"], tok[("nrmq", p)])
                    tok[("sqk", p)] = sem.inc(
                        scalar.activation(sqsc[:], khi[:], ACT.Square), "SC", 1)

                @block.tensor
                def _(tensor, p=p):
                    tensor.wait_ge(sem.h["VE"], tok[("nrq", p)])
                    tensor.wait_ge(sem.h["SC"], tok[("sqk", p)])
                    for qt in range(NQT):
                        sl = slice(qt * QT, (qt + 1) * QT)
                        tensor.matmul(psB[0:1, sl], ones16[0:64, 0:1],
                                      sqsc[0:64, sl], start=True, stop=True,
                                      skip_group_check=True)
                        ins = tensor.matmul(psA[0:1, sl], ones16[64:128, 0:1],
                                            sqsc[64:128, sl], start=True,
                                            stop=True, tile_position=(64, 0),
                                            skip_group_check=True)
                    tok[("nrmk", p)] = sem.inc(ins, "TE", 1)

                @block.vector
                def _(vector, p=p):
                    vector.wait_ge(sem.h["TE"], tok[("nrmk", p)])
                    vsync(vector, vector.tensor_reduce(browk[0:1, 0:1],
                                                       psB[0:1, :], AX.X, ALU.max))
                    vsync(vector, vector.tensor_reduce(browk[0:1, 1:2],
                                                       psA[0:1, :], AX.X, ALU.max))
                    tok[("nred", p)] = vpub(
                        vector,
                        vector.tensor_tensor(browq[:], browq[:], browk[:],
                                             ALU.mult))

                @block.scalar
                def _(scalar, p=p):
                    scalar.wait_ge(sem.h["VE"], tok[("nred", p)])
                    tok[("brow", p)] = spub(
                        scalar, scalar.activation(browq[:], browq[:], ACT.Sqrt,
                                                  scale=0.0629))

                if tap == "all":
                    @block.sync
                    def _(sync, p=p):
                        sync.wait_ge(sem.h["SC"], tok[("brow", p)])
                        sem.inc(sync.dma_start(out=dbg_b[p, 0:1, :],
                                               in_=browq[:]), "STY", 16)
                        sem.inc(sync.dma_start(out=dbg_b[p, 1:2, :],
                                               in_=browk[:]), "STY", 16)

                @block.gpsimd
                def _(gpsimd, p=p):
                    gpsimd.wait_ge(sem.h["SC"], tok[("brow", p)])
                    if tap == "all":
                        gpsimd.wait_ge(sem.h["STY"], sem.cur("STY"))
                    gpsimd.partition_broadcast(bbc[:, 0:1], browq[0:1, 0:1], 128)
                    tok[("bbc", p)] = sem.inc(
                        gpsimd.partition_broadcast(bbc[:, 1:2], browq[0:1, 1:2],
                                                   128), "GP", 1)

                @block.vector
                def _(vector, p=p):
                    vector.wait_ge(sem.h["GP"], tok[("bbc", p)])
                    vector.wait_ge(sem.h["VE"], tok["ldks"])
                    ins = None
                    for hh in range(2):
                        h = 2 * p + hh
                        vector.tensor_scalar(bfh[:, hh * KC:(hh + 1) * KC],
                                             ldks[:, h * KC:(h + 1) * KC],
                                             bbc[:, hh:hh + 1], None, ALU.subtract)
                        vector.tensor_scalar(ldnb[:, hh:hh + 1],
                                             ldb[:, h:h + 1], -1.0, None,
                                             ALU.mult)
                        ins = vector.tensor_scalar(lqrow[0:1, hh * 4:hh * 4 + 4],
                                                   c_kramp[0:1, 0:KC:4],
                                                   ldnrow[0:1, h:h + 1], None,
                                                   ALU.mult)
                    tok[("bfh", p)] = vpub(vector, ins)

            with nc.psum_tensor(f"scp0_{p}", [128, 2 * QT], F32) as scp0, \
                 nc.psum_tensor(f"scp1_{p}", [128, 2 * QT], F32) as scp1, \
                 nc.psum_tensor(f"opA_{p}", [65, QT], F32) as opA, \
                 nc.psum_tensor(f"ofA_{p}", [65, QT], F32) as ofA, \
                 nc.psum_tensor(f"opB_{p}", [65, QT], F32) as opB, \
                 nc.psum_tensor(f"ofB_{p}", [65, QT], F32) as ofB, \
                 nc.sbuf_tensor(f"on32_{p}", [64, T], F32) as on32, \
                 nc.sbuf_tensor(f"gonh_{p}", [64, T], F16) as gonhi, \
                 nc.sbuf_tensor(f"gonl_{p}", [64, T], F16) as gonlo, \
                 nc.sbuf_tensor(f"gsum_{p}", [64, 4], F32) as gsum:

                scp = [scp0, scp1]
                ophh = [opA, opB]
                ofhh = [ofA, ofB]

                for qt in range(1 if tap == "pb0" else NQT):
                    pv_state = {}
                    for c in range(KC):
                        @block.tensor
                        def _(tensor, p=p, qt=qt, c=c):
                            g = qt * KC + c
                            if g >= 2:
                                gg = g - 2
                                tensor.wait_ge(
                                    sem.h["SC"],
                                    tok[("exp", p, gg // KC, gg % KC)])
                            if g == 0:
                                tensor.wait_ge(sem.h["VE"], tok[("rope", p)])
                                tensor.wait_ge(sem.h["VE"], tok[("nred", p)])
                            gsl = slice(qt * QT, (qt + 1) * QT)
                            par = c % 2
                            ksl = slice(c * 128, (c + 1) * 128)
                            ins = None
                            for hh in range(2):
                                ph = slice(hh * 64, hh * 64 + 64)
                                tp = (hh * 64, 0)
                                dst = scp[par][:, hh * QT:hh * QT + QT]
                                tensor.matmul(dst, khi[ph, ksl], qhi[ph, gsl],
                                              start=True, stop=False,
                                              tile_position=tp,
                                              skip_group_check=True)
                                tensor.matmul(dst, khi[ph, ksl], qlo[ph, gsl],
                                              start=False, stop=False,
                                              tile_position=tp,
                                              skip_group_check=True)
                                ins = tensor.matmul(dst, klo[ph, ksl],
                                                    qhi[ph, gsl],
                                                    start=False, stop=True,
                                                    tile_position=tp,
                                                    skip_group_check=True)
                            tok[("sc", p, qt, c)] = sem.inc(ins, "TE", 1)

                        @block.scalar
                        def _(scalar, p=p, qt=qt, c=c):
                            scalar.wait_ge(sem.h["TE"], tok[("sc", p, qt, c)])
                            if qt == 0 and c == 0:
                                scalar.wait_ge(sem.h["VE"], tok[("bfh", p)])
                                if p > 0:
                                    scalar.wait_ge(sem.h["TE"],
                                                   prev[("pvdone", p - 1)])
                            par = c % 2
                            ins = None
                            for hh in range(2):
                                src = scp[par][:, hh * QT:hh * QT + QT]
                                dst = pbuf[hh][:, c * QT:(c + 1) * QT]
                                ins = scalar.activation(
                                    dst, src, ACT.Exp, scale=0.125,
                                    bias=bfh[:, hh * KC + c:hh * KC + c + 1])
                            tok[("exp", p, qt, c)] = sem.inc(ins, "SC", 1)

                        def emit_strip(c, p=p, qt=qt):
                            @block.vector
                            def _(vector, p=p, qt=qt, c=c):
                                j = c - 4 * qt
                                vector.wait_ge(sem.h["SC"],
                                               tok[("exp", p, qt, c)])
                                if j >= 2:
                                    vector.wait_ge(sem.h["TE"],
                                                   tok[("pvc", p, qt, c - 2)])
                                ins = None
                                for hh in range(2):
                                    pb = pbuf[hh][:, c * QT + j * 128:
                                                  c * QT + j * 128 + 128]
                                    ssl = slice((j % 2) * 256 + hh * 128,
                                                (j % 2) * 256 + hh * 128 + 128)
                                    vector.tensor_tensor(stripL[:, ssl], pb,
                                                         c_L[:], ALU.mult)
                                    ins = vector.tensor_tensor(stripU[:, ssl], pb,
                                                               c_U[:], ALU.mult)
                                tok[("strip", p, qt, c)] = sem.inc(ins, "VE", 1)

                        if c // 4 == qt and not SERIAL_ATTN:
                            emit_strip(c)

                        def emit_pv(cc, p=p, qt=qt, pv_state=pv_state):
                            @block.tensor
                            def _(tensor, p=p, qt=qt, cc=cc):
                                tensor.wait_ge(sem.h["SC"],
                                               tok[("exp", p, qt, cc)])
                                if cc == 0:
                                    tensor.wait_ge(sem.h["VE"], tok[("vscd", p)])
                                    if qt > 0:
                                        tensor.wait_ge(sem.h["VE"],
                                                       tok[("comb", p, qt - 1)])
                                strad = (cc // 4 == qt)
                                if strad:
                                    tensor.wait_ge(sem.h["VE"],
                                                   tok[("strip", p, qt, cc)])
                                ins = None
                                for hh in range(2):
                                    vt = vtm[hh][:, cc * 65:(cc + 1) * 65]
                                    vs = vsc[hh][:, cc * 65:(cc + 1) * 65]
                                    pb = pbuf[hh][:, cc * QT:(cc + 1) * QT]
                                    op = ophh[hh]
                                    of = ofhh[hh]
                                    if cc < 4 * qt:
                                        ins = _mm(tensor, op[:, :], vs, pb,
                                                  ("op", hh), pv_state)
                                    elif cc >= 4 * (qt + 1):
                                        ins = _mm(tensor, of[:, :], vt, pb,
                                                  ("of", hh), pv_state,
                                                  stop=(cc == KC - 1))
                                    else:
                                        j = cc - 4 * qt
                                        w_f = 128 * j
                                        w_p0 = w_f + 128
                                        if w_f > 0:
                                            _mm(tensor, of[:, 0:w_f], vt,
                                                pb[:, 0:w_f], ("of", hh), pv_state)
                                        if QT - w_p0 > 0:
                                            _mm(tensor, op[:, w_p0:QT], vs,
                                                pb[:, w_p0:QT], ("op", hh),
                                                pv_state)
                                        ssl = slice((j % 2) * 256 + hh * 128,
                                                    (j % 2) * 256 + hh * 128 + 128)
                                        _mm(tensor, op[:, w_f:w_p0], vs,
                                            stripL[:, ssl], ("op", hh), pv_state,
                                            stop=True)
                                        ins = _mm(tensor, of[:, w_f:w_p0], vt,
                                                  stripU[:, ssl], ("of", hh),
                                                  pv_state, stop=(cc == KC - 1))
                                tok[("pvc", p, qt, cc)] = sem.inc(ins, "TE", 1)
                                if cc == KC - 1:
                                    tok[("pv", p, qt)] = tok[("pvc", p, qt, cc)]

                        if not SERIAL_ATTN:
                            if c >= 1:
                                emit_pv(c - 1)
                            if c == KC - 1:
                                emit_pv(c)
                    if SERIAL_ATTN:
                        for c in range(KC):
                            if c // 4 == qt:
                                emit_strip(c)
                            emit_pv(c)

                    for hh in range(2):
                        @block.vector
                        def _(vector, p=p, qt=qt, hh=hh):
                            gsl = slice(qt * QT, (qt + 1) * QT)
                            vector.wait_ge(sem.h["VE"], tok[("bfh", p)])
                            tok[("dr", p, qt, hh)] = sem.inc(
                                vector.tensor_scalar(
                                    r1t[:], c_qramp[:],
                                    ldnrow[0:1, 2 * p + hh:2 * p + hh + 1],
                                    lqrow[0:1, hh * 4 + qt:hh * 4 + qt + 1],
                                    ALU.mult, ALU.add), "VE", 1)

                        @block.scalar
                        def _(scalar, p=p, qt=qt, hh=hh):
                            scalar.wait_ge(sem.h["VE"], tok[("dr", p, qt, hh)])
                            tok[("dre", p, qt, hh)] = sem.inc(
                                scalar.activation(r1t[:], r1t[:], ACT.Exp),
                                "SC", 1)

                        @block.gpsimd
                        def _(gpsimd, p=p, qt=qt, hh=hh):
                            gpsimd.wait_ge(sem.h["SC"], tok[("dre", p, qt, hh)])
                            if hh == 1:
                                gpsimd.wait_ge(sem.h["VE"],
                                               tok[("linv", p, qt, 0)])
                            tok[("dbb", p, qt, hh)] = sem.inc(
                                gpsimd.partition_broadcast(dbb1[hh][:], r1t[:],
                                                           65), "GP", 1)

                        @block.vector
                        def _(vector, p=p, qt=qt, hh=hh):
                            vector.wait_ge(sem.h["TE"], tok[("pv", p, qt)])
                            vector.wait_ge(sem.h["GP"], tok[("dbb", p, qt, hh)])
                            vector.tensor_tensor(csc1[:], ofhh[hh][:],
                                                 dbb1[hh][:], ALU.mult)
                            vector.scalar_tensor_tensor(csc1[:], ophh[hh][:],
                                                        1.0, csc1[:], ALU.mult,
                                                        ALU.add)
                            tok[("linv", p, qt, hh)] = vpub(
                                vector, vector.reciprocal(r1t[:], csc1[64:65, :]))

                        if tap == "cs0" and qt == 0 and hh == 0:
                            @block.sync
                            def _(sync, p=p, qt=qt, hh=hh):
                                sync.wait_ge(sem.h["VE"], tok[("linv", p, qt, hh)])
                                sem.inc(sync.dma_start(out=dbg_cs[0:65, :],
                                                       in_=csc1[:]), "STY", 16)
                                sem.inc(sync.dma_start(out=dbg_cs[65:66, :],
                                                       in_=r1t[:]), "STY", 16)

                        @block.gpsimd
                        def _(gpsimd, p=p, qt=qt, hh=hh):
                            gpsimd.wait_ge(sem.h["VE"], tok[("linv", p, qt, hh)])
                            if tap == "cs0" and qt == 0 and hh == 0:
                                gpsimd.wait_ge(sem.h["STY"], sem.cur("STY"))
                            tok[("lbc", p, qt, hh)] = sem.inc(
                                gpsimd.partition_broadcast(linvb[:], r1t[:], 64),
                                "GP", 1)

                        @block.vector
                        def _(vector, p=p, qt=qt, hh=hh):
                            gsl = slice(qt * QT, (qt + 1) * QT)
                            vector.wait_ge(sem.h["GP"], tok[("lbc", p, qt, hh)])
                            tok[("comb", p, qt)] = sem.inc(
                                vector.tensor_tensor(ogt[hh][:, gsl],
                                                     csc1[0:64, :], linvb[:],
                                                     ALU.mult), "VE", 1)

                qq = 0 if tap == "pb0" else NQT - 1
                prev[("pvdone", p)] = tok[("pv", p, qq)]
                prev[("scdone", p)] = tok[("sc", p, qq, KC - 1)]
                prev[("combdone", p)] = tok[("comb", p, qq)]
                prev[("expdone", p)] = tok[("exp", p, qq, KC - 1)]

                if tap in ("pb", "pb0"):
                    @block.sync
                    def _(sync, p=p):
                        qq = 0 if tap == "pb0" else NQT - 1
                        sync.wait_ge(sem.h["SC"], tok[("exp", p, qq, KC - 1)])
                        sem.inc(sync.dma_start(out=dbg_pb[0], in_=pbuf[0]),
                                "STY", 16)
                        sem.inc(sync.dma_start(out=dbg_pb[1], in_=pbuf[1]),
                                "STY", 16)
                        sync.wait_ge(sem.h["STY"], sem.cur("STY"))
                    tap_now[0] = tap
                    break

                if tap in ("og", "cs0"):
                    @block.sync
                    def _(sync, p=p):
                        sync.wait_ge(sem.h["VE"], tok[("comb", p, NQT - 1)])
                        if tap == "og":
                            sem.inc(sync.dma_start(out=dbg_og[0], in_=ogt[0][:]),
                                    "STY", 16)
                            sem.inc(sync.dma_start(out=dbg_og[1], in_=ogt[1][:]),
                                    "STY", 16)
                        sync.wait_ge(sem.h["STY"], sem.cur("STY"))
                    tap_now[0] = tap
                    break

                if tap == "all":
                    @block.sync
                    def _(sync, p=p):
                        sync.wait_ge(sem.h["VE"], tok[("comb", p, NQT - 1)])
                        for hh in range(2):
                            sem.inc(sync.dma_start(out=dbg_ogall[p, hh],
                                                   in_=ogt[hh][:]), "STY", 16)
                            sem.inc(sync.dma_start(out=dbg_gate[p, hh],
                                                   in_=gateh[hh][:]), "STY", 16)

                for hh in range(2):
                    @block.vector
                    def _(vector, p=p, hh=hh):
                        vector.wait_ge(sem.h["VE"], tok[("comb", p, NQT - 1)])
                        tok[("gsum", p, hh)] = vsync(
                            vector, vector.reduce_sum(gsum[:, hh:hh + 1],
                                                      ogt[hh][:], AX.X))
                        vector.tensor_tensor(sqsc[0:64, :], ogt[hh][:],
                                             ogt[hh][:], ALU.mult)
                        tok[("gsq", p, hh)] = (vpub if hh == 1 else vsync)(
                            vector, vector.reduce_sum(gsum[:, 2 + hh:3 + hh],
                                                      sqsc[0:64, :], AX.X))
                prev[("gsq", p)] = tok[("gsq", p, 1)]

                @block.gpsimd
                def _(gpsimd, p=p):
                    gpsimd.wait_ge(sem.h["VE"], tok[("gsq", p, 1)])
                    gpsimd.wait_ge(sem.h["VE"], tok[("gsq", p, 0)])
                    tok[("gmm", p)] = sem.inc(
                        gpsimd.partition_all_reduce(gsum[:], gsum[:], 64,
                                                    ReduceOp.add), "GP", 1)

                @block.vector
                def _(vector, p=p):
                    vector.wait_ge(sem.h["GP"], tok[("gmm", p)])
                    NTOT = float(T * 64)
                    vsync(vector,
                          vector.tensor_scalar(r2t[0:1, 0:4], gsum[0:1, 0:4],
                                               1.0 / NTOT, None, ALU.mult))
                    vsync(vector,
                          vector.tensor_tensor(r2t[0:1, 4:6], r2t[0:1, 0:2],
                                               r2t[0:1, 0:2], ALU.mult))
                    tok[("gvar", p)] = vpub(
                        vector, vector.tensor_tensor(r2t[0:1, 2:4],
                                                     r2t[0:1, 2:4],
                                                     r2t[0:1, 4:6],
                                                     ALU.subtract))


                @block.scalar
                def _(scalar, p=p):
                    scalar.wait_ge(sem.h["VE"], tok[("gvar", p)])
                    tok[("gsqrt", p)] = spub(
                        scalar,
                        scalar.activation(r2t[0:1, 2:4], r2t[0:1, 2:4], ACT.Sqrt,
                                          bias=ctb[0:1, 0:1]))

                @block.vector
                def _(vector, p=p):
                    vector.wait_ge(sem.h["SC"], tok[("gsqrt", p)])
                    tok[("grstd", p)] = vpub(
                        vector, vector.reciprocal(r2t[0:1, 6:8], r2t[0:1, 2:4]))

                if tap == "all":
                    @block.sync
                    def _(sync, p=p):
                        sync.wait_ge(sem.h["VE"], tok[("grstd", p)])
                        sem.inc(sync.dma_start(out=dbg_r2[p], in_=r2t[:]),
                                "STY", 16)

                @block.gpsimd
                def _(gpsimd, p=p):
                    gpsimd.wait_ge(sem.h["VE"], tok[("grstd", p)])
                    ins = None
                    for hh in range(2):
                        gpsimd.partition_broadcast(gnb[:, hh:hh + 1],
                                                   r2t[0:1, hh:hh + 1], 64)
                        ins = gpsimd.partition_broadcast(gnb[:, 2 + hh:3 + hh],
                                                         r2t[0:1, 6 + hh:7 + hh],
                                                         64)
                    tok[("gnb", p)] = sem.inc(ins, "GP", 1)

                for hh in range(2):
                    @block.vector
                    def _(vector, p=p, hh=hh):
                        vector.wait_ge(sem.h["GP"], tok[("gnb", p)])
                        if hh == 0:
                            vector.wait_ge(sem.h["SC"], tok[("gev", p)])
                            if p > 0:
                                vector.wait_ge(sem.h["STG"],
                                               prev[("gonst", p - 1)])
                        else:
                            vector.wait_ge(sem.h["STG"], tok[("gonst", p, 0)])
                        ph = slice(hh * 64, hh * 64 + 64)
                        vsync(vector, vector.tensor_copy(
                            gnc[:, 0:1], p_sb["gamma"][ph, p:p + 1]))
                        vsync(vector, vector.tensor_copy(
                            gnc[:, 1:2], p_sb["beta"][ph, p:p + 1]))
                        vsync(vector, vector.tensor_tensor(
                            gnc[:, 2:3], gnc[:, 0:1],
                            gnb[:, 2 + hh:3 + hh], ALU.mult))
                        vsync(vector, vector.scalar_tensor_tensor(
                            gnc[:, 3:4], gnb[:, hh:hh + 1], 1.0,
                            gnc[:, 2:3], ALU.mult, ALU.mult))
                        vsync(vector, vector.tensor_tensor(
                            gnc[:, 3:4], gnc[:, 3:4],
                            gnc[:, 1:2], ALU.subtract))
                        vector.tensor_scalar(on32[:], ogt[hh][:], gnc[:, 2:3],
                                             gnc[:, 3:4], ALU.mult, ALU.subtract)
                        vector.tensor_tensor(on32[:], on32[:], gateh[hh][:],
                                             ALU.mult)
                        vector.tensor_copy(gonhi[:], on32[:])
                        tok[("gon", p, hh)] = sem.inc(
                            vector.scalar_tensor_tensor(gonlo[:], on32[:], 1.0,
                                                        gonhi[:], ALU.mult,
                                                        ALU.subtract), "VE", 1)

                    @block.gpsimd
                    def _(gpsimd, p=p, hh=hh):
                        gpsimd.wait_ge(sem.h["VE"], tok[("gon", p, hh)])
                        ph = slice(hh * 64, hh * 64 + 64)
                        sem.inc(gpsimd.dma_start(out=gon_sp[0, p, ph],
                                                 in_=gonhi[:]), "STG", 16)
                        tok[("gonst", p, hh)] = sem.inc(
                            gpsimd.dma_start(out=gon_sp[1, p, ph], in_=gonlo[:]),
                            "STG", 16)
                if tap == "all":
                    @block.sync
                    def _(sync, p=p):
                        sync.wait_ge(sem.h["STY"], sem.cur("STY"))
                prev[("gonve", p)] = tok[("gon", p, 1)]
                prev[("gonst", p)] = tok[("gonst", p, 1)]

        if tap == "qk":
            @block.sync
            def _(sync):
                sync.wait_ge(sem.h["VE"], tok[("rope", 0)])
                sem.inc(sync.dma_start(out=dbg_qk[0], in_=qhi[:]), "STY", 16)
                sem.inc(sync.dma_start(out=dbg_qk[1], in_=qlo[:]), "STY", 16)
                sem.inc(sync.dma_start(out=dbg_qk[2], in_=khi[:]), "STY", 16)
                sem.inc(sync.dma_start(out=dbg_qk[3], in_=klo[:]), "STY", 16)
                sync.wait_ge(sem.h["STY"], sem.cur("STY"))
            lower_extended_insts(nc)
            return nc

        if tap in ("og", "gon", "pb", "pb0", "cs0"):
            @block.sync
            def _(sync):
                for s in ["STH", "STG", "STY"]:
                    sync.wait_ge(sem.h[s], sem.cur(s))
                sync.wait_ge(sem.h["SC"], sem.cur("SC"))
                sync.wait_ge(sem.h["VE"], sem.cur("VE"))
            lower_extended_insts(nc)
            return nc

        stB.close()

        _k5 = "ExternalOutput" if tap in ("all", "allq") else "Internal"
        xr_sp = nc.dram_tensor("xr_sp", [2, EC, 128, T], F16, kind=_k5).ap()
        xk_sp = nc.dram_tensor("xk_sp", [2, EC, 128, T], F16, kind=_k5).ap()

        with contextlib.ExitStack() as s3:
            sb3 = lambda name, shape, dt=F32: s3.enter_context(
                nc.sbuf_tensor(name, shape, dt))
            gstr = [[sb3(f"gstr{i}_{d}", [128, T], F16) for d in range(2)]
                    for i in range(2)]
            w3 = [sb3(f"w3st{i}", [128, EC * 128]) for i in range(2)]
            w3hi = [sb3(f"w3hi{i}", [128, EC * 128], F16) for i in range(2)]
            w3lo = [sb3(f"w3lo{i}", [128, EC * 128], F16) for i in range(2)]
            xb3 = [sb3(f"xb3_{i}", [128, T]) for i in range(2)]
            x1c = sb3("x1c", [128, T])
            x1hi = sb3("x1hi", [128, T], F16)
            x1lo = sb3("x1lo", [128, T], F16)
            psO = s3.enter_context(nc.psum_tensor("psO", [128, T], F32))
            psO2 = s3.enter_context(nc.psum_tensor("psO2", [128, T], F32))
            psOs = [psO, psO2]

            @block.sync
            def _(sync):
                sync.wait_ge(sem.h["STG"], sem.cur("STG"))
                sync.wait_ge(sem.h["VE"], sem.cur("VE"))
                sync.wait_ge(sem.h["SC"], sem.cur("SC"))
                sync.wait_ge(sem.h["TE"], sem.cur("TE"))
                sync.wait_ge(sem.h["GP"], sem.cur("GP"))
                sem.inc(sync.dma_start(out=w3[0][:], in_=wblk["o"][0]),
                        "LW0", 16)
                tok[("w3", 0)] = sem.cur("LW0")

            def wchain(gi, nxt_src, fmm_key):
                sl = gi % 2

                @block.scalar
                def _(scalar, gi=gi, sl=sl):
                    scalar.wait_ge(sem.h["LW%d" % sl], tok[("w3", gi)])
                    if gi >= 2:
                        scalar.wait_ge(sem.h["TE"], tok[(fmm_key, gi - 2)])
                    tok[("w3hi", gi)] = sem.inc(
                        scalar.activation(w3hi[sl][:], w3[sl][:], ACT.Copy),
                        "SC", 1)

                @block.vector
                def _(vector, gi=gi, sl=sl):
                    vector.wait_ge(sem.h["SC"], tok[("w3hi", gi)])
                    tok[("w3sp", gi)] = sem.inc(
                        vector.scalar_tensor_tensor(
                            w3lo[sl][:], w3[sl][:], 1.0, w3hi[sl][:],
                            ALU.mult, ALU.subtract), "VE", 1)

                @block.sync
                def _(sync, gi=gi, nxt_src=nxt_src):
                    if nxt_src is not None:
                        osl = (gi + 1) % 2
                        if gi >= 1:
                            sync.wait_ge(sem.h["SC"], tok[("w3hi", gi)])
                        sem.inc(sync.dma_start(out=w3[osl][:], in_=nxt_src),
                                "LW%d" % osl, 16)
                        tok[("w3", gi + 1)] = sem.cur("LW%d" % osl)

            for dc in range(EC):
                wchain(dc, wblk["o"][dc + 1] if dc < EC - 1 else None, "p3mm")

                for ec in range(EC):
                    @block.sync
                    def _(sync, dc=dc, ec=ec):
                        gi = dc * EC + ec
                        if gi >= 2:
                            sync.wait_ge(sem.h["TE"], tok[("p3u", gi - 2)])
                        sem.inc(sync.dma_start(out=gstr[ec % 2][0][:],
                                               in_=gon_sp[0, ec]),
                                "LH%d" % (ec % 2), 16)
                        tok[("g3", gi)] = sem.inc(
                            sync.dma_start(out=gstr[ec % 2][1][:],
                                           in_=gon_sp[1, ec]),
                            "LH%d" % (ec % 2), 16)

                    @block.tensor
                    def _(tensor, dc=dc, ec=ec):
                        gi = dc * EC + ec
                        tensor.wait_ge(sem.h["LH%d" % (ec % 2)], tok[("g3", gi)])
                        if ec == 0:
                            tensor.wait_ge(sem.h["VE"], tok[("w3sp", dc)])
                            if dc >= 2:
                                tensor.wait_ge(sem.h["VE"],
                                               tok[("x1ev", dc - 2)])
                        ps = psOs[dc % 2]
                        whi_ = w3hi[dc % 2]
                        wlo_ = w3lo[dc % 2]
                        ins = None
                        first = (ec == 0)
                        for qt in range(NQT):
                            sl = slice(qt * QT, (qt + 1) * QT)
                            tensor.matmul(ps[:, sl], wsl(whi_, ec),
                                          gstr[ec % 2][0][:, sl],
                                          start=first, stop=False)
                            tensor.matmul(ps[:, sl], wsl(whi_, ec),
                                          gstr[ec % 2][1][:, sl],
                                          start=False, stop=False)
                            ins = tensor.matmul(ps[:, sl], wsl(wlo_, ec),
                                                gstr[ec % 2][0][:, sl],
                                                start=False,
                                                stop=(ec == EC - 1))
                        tok[("p3u", gi)] = sem.inc(ins, "TE", 1)
                tok[("p3mm", dc)] = tok[("p3u", dc * EC + EC - 1)]

                @block.sync
                def _(sync, dc=dc):
                    if dc >= 2:
                        sync.wait_ge(sem.h["VE"], tok[("x1c", dc - 2)])
                    tok[("x3", dc)] = sem.inc(
                        sync.dma_start(out=xb3[dc % 2][:],
                                       in_=x_fm[dc * 128:(dc + 1) * 128, :]),
                        "LX%d" % (dc % 2), 16)

                @block.vector
                def _(vector, dc=dc):
                    vector.wait_ge(sem.h["TE"], tok[("p3mm", dc)])
                    vector.wait_ge(sem.h["LX%d" % (dc % 2)], tok[("x3", dc)])
                    if dc >= 1:
                        vector.wait_ge(sem.h["STX"], tok[("x1sp", dc - 1)])
                    vector.tensor_scalar(x1c[:], psOs[dc % 2][:],
                                         p_sb["bo"][:, dc:dc + 1], None, ALU.add)
                    tok[("x1c", dc)] = sem.inc(
                        vector.tensor_tensor(x1c[:], x1c[:], xb3[dc % 2][:],
                                             ALU.add), "VE", 1)

                @block.scalar
                def _(scalar, dc=dc):
                    scalar.wait_ge(sem.h["VE"], tok[("x1c", dc)])
                    tok[("x1hi", dc)] = sem.inc(
                        scalar.activation(x1hi[:], x1c[:], ACT.Copy), "SC", 1)

                @block.vector
                def _(vector, dc=dc):
                    vector.wait_ge(sem.h["SC"], tok[("x1hi", dc)])
                    tok[("x1ev", dc)] = sem.inc(
                        vector.scalar_tensor_tensor(
                            x1lo[:], x1c[:], 1.0, x1hi[:], ALU.mult,
                            ALU.subtract), "VE", 1)

                @block.sync
                def _(sync, dc=dc):
                    sync.wait_ge(sem.h["VE"], tok[("x1ev", dc)])
                    sem.inc(sync.dma_start(out=x1_sp[0, dc], in_=x1hi[:]),
                            "STX", 16)
                    tok[("x1sp", dc)] = sem.inc(
                        sync.dma_start(out=x1_sp[1, dc], in_=x1lo[:]), "STX", 16)

        if tap == "x1":
            @block.sync
            def _(sync):
                sync.wait_ge(sem.h["STX"], sem.cur("STX"))
            lower_extended_insts(nc)
            return nc

        with contextlib.ExitStack() as s4:
            sb4 = lambda name, shape, dt=F32: s4.enter_context(
                nc.sbuf_tensor(name, shape, dt))
            xs4 = [[sb4(f"xs4_{i}_{d}", [128, T], F16) for d in range(2)]
                   for i in range(2)]
            t4 = [sb4(f"t4_{i}", [128, T]) for i in range(2)]
            sq4 = sb4("sq4", [128, T])
            m2b_ = sb4("m2b", [128, T])
            rstd2b_ = sb4("rstd2b", [128, T])

            with nc.sbuf_tensor("m2row", [1, T], F32) as m2row, \
                 nc.sbuf_tensor("rstd2row", [1, T], F32) as rstd2row, \
                 nc.psum_tensor("psS2", [1, T], F32) as psS2, \
                 nc.psum_tensor("psQ2", [1, T], F32) as psQ2:

                for ec in range(EC):
                    @block.sync
                    def _(sync, ec=ec):
                        if ec >= 2:
                            sync.wait_ge(sem.h["TE"], tok[("s4te", ec - 2)])
                        if ec == 0:
                            sync.wait_ge(sem.h["STX"], tok[("x1sp", EC - 1)])
                        sem.inc(sync.dma_start(out=xs4[ec % 2][0][:],
                                               in_=x1_sp[0, ec]),
                                "LH%d" % (ec % 2), 16)
                        tok[("x4", ec)] = sem.inc(
                            sync.dma_start(out=xs4[ec % 2][1][:],
                                           in_=x1_sp[1, ec]),
                            "LH%d" % (ec % 2), 16)

                    @block.vector
                    def _(vector, ec=ec):
                        vector.wait_ge(sem.h["LH%d" % (ec % 2)], tok[("x4", ec)])
                        if ec >= 2:
                            vector.wait_ge(sem.h["TE"], tok[("s4te", ec - 2)])
                        tok[("t4", ec)] = sem.inc(
                            vector.tensor_tensor(t4[ec % 2][:], xs4[ec % 2][0][:],
                                                 xs4[ec % 2][1][:], ALU.add),
                            "VE", 1)

                    @block.scalar
                    def _(scalar, ec=ec):
                        scalar.wait_ge(sem.h["VE"], tok[("t4", ec)])
                        if ec >= 1:
                            scalar.wait_ge(sem.h["TE"], tok[("s4te", ec - 1)])
                        tok[("sq4", ec)] = sem.inc(
                            scalar.activation(sq4[:], t4[ec % 2][:], ACT.Square),
                            "SC", 1)

                    @block.tensor
                    def _(tensor, ec=ec):
                        tensor.wait_ge(sem.h["VE"], tok[("t4", ec)])
                        for qt in range(NQT):
                            tensor.matmul(psS2[0:1, qt * QT:(qt + 1) * QT],
                                          c_ones[:],
                                          t4[ec % 2][:, qt * QT:(qt + 1) * QT],
                                          start=(ec == 0), stop=(ec == EC - 1))
                        tensor.wait_ge(sem.h["SC"], tok[("sq4", ec)])
                        ins = None
                        for qt in range(NQT):
                            ins = tensor.matmul(
                                psQ2[0:1, qt * QT:(qt + 1) * QT], c_ones[:],
                                sq4[:, qt * QT:(qt + 1) * QT],
                                start=(ec == 0), stop=(ec == EC - 1))
                        tok[("s4te", ec)] = sem.inc(ins, "TE", 1)

                @block.vector
                def _(vector):
                    vector.wait_ge(sem.h["TE"], tok[("s4te", EC - 1)])
                    vector.tensor_scalar(m2row[:], psS2[0:1, :], 1.0 / E, None,
                                         ALU.mult)
                    vector.tensor_tensor(rstd2row[:], m2row[:], m2row[:],
                                         ALU.mult)
                    tok["var2"] = sem.inc(
                        vector.scalar_tensor_tensor(rstd2row[:], psQ2[0:1, :],
                                                    1.0 / E, rstd2row[:],
                                                    ALU.mult, ALU.subtract),
                        "VE", 1)

                @block.scalar
                def _(scalar):
                    scalar.wait_ge(sem.h["VE"], tok["var2"])
                    tok["sqrt2"] = sem.inc(
                        scalar.activation(rstd2row[:], rstd2row[:], ACT.Sqrt,
                                          bias=ctb[0:1, 0:1]), "SC", 1)

                @block.vector
                def _(vector):
                    vector.wait_ge(sem.h["SC"], tok["sqrt2"])
                    tok["rstd2"] = vpub(
                        vector, vector.reciprocal(rstd2row[:], rstd2row[:]))

                @block.gpsimd
                def _(gpsimd):
                    gpsimd.wait_ge(sem.h["VE"], tok["rstd2"])
                    gpsimd.partition_broadcast(m2b_[:], m2row[:], 128)
                    tok["bcast2"] = sem.inc(
                        gpsimd.partition_broadcast(rstd2b_[:], rstd2row[:], 128),
                        "GP", 1)

            with nc.sbuf_tensor("g32", [128, T + 1], F32) as g32, \
                 nc.sbuf_tensor("xm32", [128, T], F32) as xm32, \
                 nc.sbuf_tensor("xphi", [128, T], F16) as xphi, \
                 nc.sbuf_tensor("xplo", [128, T], F16) as xplo, \
                 nc.sbuf_tensor("mur2", [128, 8], F32) as mur2, \
                 nc.sbuf_tensor("muk2", [128, 8], F32) as muk2:

                @block.vector
                def _(vector):
                    vector.wait_ge(sem.h["GP"], tok["bcast2"])
                    vector.tensor_scalar(mur2[:], p_sb["mu_r"][:], -1.0, 1.0,
                                         ALU.mult, ALU.add)
                    vector.memset(g32[:, 0:1], 0.0)
                    tok["mu2"] = vpub(
                        vector,
                        vector.tensor_scalar(muk2[:], p_sb["mu_k"][:], -1.0, 1.0,
                                             ALU.mult, ALU.add))

                for ec in range(EC):
                    @block.sync
                    def _(sync, ec=ec):
                        if ec >= 2:
                            sync.wait_ge(sem.h["VE"], tok[("gdone", ec - 2)])
                        sem.inc(sync.dma_start(out=xs4[ec % 2][0][:],
                                               in_=x1_sp[0, ec]),
                                "LH%d" % (ec % 2), 16)
                        tok[("x5", ec)] = sem.inc(
                            sync.dma_start(out=xs4[ec % 2][1][:],
                                           in_=x1_sp[1, ec]),
                            "LH%d" % (ec % 2), 16)

                    for which, mucol, mu2col, spt in (
                            ("xr", p_sb["mu_r"], mur2, xr_sp),
                            ("xk", p_sb["mu_k"], muk2, xk_sp)):
                        @block.vector
                        def _(vector, ec=ec, which=which, mucol=mucol,
                              mu2col=mu2col):
                            if which == "xk":
                                vector.wait_ge(sem.h["STH"],
                                               tok[("xrsp", ec)])
                            if which == "xr":
                                vector.wait_ge(sem.h["LH%d" % (ec % 2)],
                                               tok[("x5", ec)])
                                if ec == 0:
                                    vector.wait_ge(sem.h["VE"], tok["mu2"])
                                if ec >= 1:
                                    vector.wait_ge(sem.h["STH"],
                                                   tok[("xksp", ec - 1)])
                                vector.tensor_tensor(t4[ec % 2][:],
                                                     xs4[ec % 2][0][:],
                                                     xs4[ec % 2][1][:], ALU.add)
                                vector.tensor_tensor(t4[ec % 2][:],
                                                     t4[ec % 2][:], m2b_[:],
                                                     ALU.subtract)
                                vector.tensor_tensor(g32[:, 1:T + 1],
                                                     t4[ec % 2][:], rstd2b_[:],
                                                     ALU.mult)
                            vector.tensor_scalar(xm32[:], g32[:, 1:T + 1],
                                                 mucol[:, ec:ec + 1], None,
                                                 ALU.mult)
                            vector.scalar_tensor_tensor(xm32[:], g32[:, 0:T],
                                                        mu2col[:, ec:ec + 1],
                                                        xm32[:], ALU.mult,
                                                        ALU.add)
                            vector.tensor_copy(xphi[:], xm32[:])
                            tok[(which, ec)] = sem.inc(
                                vector.scalar_tensor_tensor(
                                    xplo[:], xm32[:], 1.0, xphi[:], ALU.mult,
                                    ALU.subtract), "VE", 1)
                            if which == "xk":
                                tok[("gdone", ec)] = tok[("xk", ec)]

                        @block.sync
                        def _(sync, ec=ec, which=which, spt=spt):
                            sync.wait_ge(sem.h["VE"], tok[(which, ec)])
                            sem.inc(sync.dma_start(out=spt[0, ec], in_=xphi[:]),
                                    "STH", 16)
                            tok[(which + "sp", ec)] = sem.inc(
                                sync.dma_start(out=spt[1, ec], in_=xplo[:]),
                                "STH", 16)

            with nc.sbuf_tensor("w6a", [128, EC * 128], F32) as w6a, \
                 nc.sbuf_tensor("w6b", [128, EC * 128], F32) as w6b, \
                 nc.sbuf_tensor("w6ha", [128, EC * 128], F16) as w6ha, \
                 nc.sbuf_tensor("w6hb", [128, EC * 128], F16) as w6hb, \
                 nc.sbuf_tensor("w6la", [128, EC * 128], F16) as w6la, \
                 nc.sbuf_tensor("w6lb", [128, EC * 128], F16) as w6lb, \
                 nc.sbuf_tensor("kahi", [128, EC * T], F16) as _kh, \
                 nc.sbuf_tensor("kalo", [128, EC * T], F16) as _kl, \
                 nc.sbuf_tensor("r32a", [128, T], F32) as r32a, \
                 nc.sbuf_tensor("r32b", [128, T], F32) as r32c, \
                 nc.sbuf_tensor("xs6a0", [128, T], F16) as xs6a0, \
                 nc.sbuf_tensor("xs6a1", [128, T], F16) as xs6a1, \
                 nc.sbuf_tensor("xs6b0", [128, T], F16) as xs6b0, \
                 nc.sbuf_tensor("xs6b1", [128, T], F16) as xs6b1, \
                 nc.sbuf_tensor("s6", [128, T], F32) as s6, \
                 nc.sbuf_tensor("y32", [128, T], F32) as y32, \
                 nc.psum_tensor("psF", [128, T], F32) as psF, \
                 nc.psum_tensor("psF2", [128, T], F32) as psF2:

                def kchunk_hi(dc):
                    return _kh[:, dc * T:(dc + 1) * T]

                def kchunk_lo(dc):
                    return _kl[:, dc * T:(dc + 1) * T]

                psFs = [psF, psF2]
                w6 = [w6a, w6b]
                w6h = [w6ha, w6hb]
                w6l = [w6la, w6lb]
                r32s = [r32a, r32c]

                FSRC = ([("r", d) for d in range(EC)]
                        + [("k2", d) for d in range(EC)]
                        + [("v2", d) for d in range(EC)])

                @block.sync
                def _(sync):
                    sync.wait_ge(sem.h["VE"], tok[("gdone", EC - 1)])
                    sem.inc(sync.dma_start(out=w6[0][:], in_=wblk["r"][0]),
                            "LW0", 16)
                    tok[("w6", 0)] = sem.cur("LW0")

                def w6chain(gi):
                    sl = gi % 2

                    @block.scalar
                    def _(scalar, gi=gi, sl=sl):
                        scalar.wait_ge(sem.h["LW%d" % sl], tok[("w6", gi)])
                        if gi >= 2:
                            scalar.wait_ge(sem.h["TE"], tok[("fmm", gi - 2)])
                        tok[("w6hi", gi)] = sem.inc(
                            scalar.activation(w6h[sl][:], w6[sl][:], ACT.Copy),
                            "SC", 1)

                    @block.vector
                    def _(vector, gi=gi, sl=sl):
                        vector.wait_ge(sem.h["SC"], tok[("w6hi", gi)])
                        tok[("w6sp", gi)] = sem.inc(
                            vector.scalar_tensor_tensor(
                                w6l[sl][:], w6[sl][:], 1.0, w6h[sl][:],
                                ALU.mult, ALU.subtract), "VE", 1)

                    @block.sync
                    def _(sync, gi=gi):
                        if gi + 1 < len(FSRC):
                            osl = (gi + 1) % 2
                            sync.wait_ge(sem.h["SC"], tok[("w6hi", gi)])
                            if ("w6sp", gi - 1) in tok:
                                sync.wait_ge(sem.h["VE"], tok[("w6sp", gi - 1)])
                            m, d = FSRC[gi + 1]
                            sem.inc(sync.dma_start(out=w6[osl][:],
                                                   in_=wblk[m][d]),
                                    "LW%d" % osl, 16)
                            tok[("w6", gi + 1)] = sem.cur("LW%d" % osl)

                xs6 = [[xs6a0, xs6a1], [xs6b0, xs6b1]]

                def fmm2_streamed(base, dp, spt, bufs):
                    for ec in range(EC):
                        @block.sync
                        def _(sync, base=base, dp=dp, ec=ec, spt=spt,
                              bufs=bufs):
                            if ec >= 2:
                                sync.wait_ge(sem.h["TE"],
                                             tok[("f6u2", base, dp, ec - 2)])
                            elif dp > 0:
                                sync.wait_ge(
                                    sem.h["TE"],
                                    tok[("f6u2", base, dp - 1, EC - 2 + ec)])
                            if base == 0:
                                sync.wait_ge(sem.h["STH"], tok[("xrsp", ec)])
                            else:
                                sync.wait_ge(sem.h["STH"], tok[("xksp", ec)])
                            sem.inc(sync.dma_start(out=bufs[ec % 2][0][:],
                                                   in_=spt[0, ec]),
                                    "LH%d" % (ec % 2), 16)
                            tok[("f6l2", base, dp, ec)] = sem.inc(
                                sync.dma_start(out=bufs[ec % 2][1][:],
                                               in_=spt[1, ec]),
                                "LH%d" % (ec % 2), 16)

                        @block.tensor
                        def _(tensor, base=base, dp=dp, ec=ec, bufs=bufs):
                            g0, g1 = base + 2 * dp, base + 2 * dp + 1
                            tensor.wait_ge(sem.h["LH%d" % (ec % 2)],
                                           tok[("f6l2", base, dp, ec)])
                            if ec == 0:
                                tensor.wait_ge(sem.h["VE"], tok[("w6sp", g0)])
                                tensor.wait_ge(sem.h["VE"], tok[("w6sp", g1)])
                                for gg in (g0 - 2, g1 - 2):
                                    if ("fev", gg) in tok:
                                        tensor.wait_ge(sem.h["VE"],
                                                       tok[("fev", gg)])
                            first = (ec == 0)
                            ins = None
                            for g in (g0, g1):
                                ps = psFs[g % 2]
                                whi_ = w6h[g % 2]
                                wlo_ = w6l[g % 2]
                                for qt in range(NQT):
                                    sl = slice(qt * QT, (qt + 1) * QT)
                                    tensor.matmul(ps[:, sl], wsl(whi_, ec),
                                                  bufs[ec % 2][0][:, sl],
                                                  start=first, stop=False)
                                    tensor.matmul(ps[:, sl], wsl(whi_, ec),
                                                  bufs[ec % 2][1][:, sl],
                                                  start=False, stop=False)
                                    ins = tensor.matmul(ps[:, sl],
                                                        wsl(wlo_, ec),
                                                        bufs[ec % 2][0][:, sl],
                                                        start=False,
                                                        stop=(ec == EC - 1))
                            tok[("f6u2", base, dp, ec)] = sem.inc(ins, "TE", 1)
                    last = tok[("f6u2", base, dp, EC - 1)]
                    tok[("fmm", base + 2 * dp)] = last
                    tok[("fmm", base + 2 * dp + 1)] = last

                def fmm_streamed(gi, spt, bufs):
                    for ec in range(EC):
                        @block.sync
                        def _(sync, gi=gi, ec=ec, spt=spt, bufs=bufs):
                            si = gi * EC + ec
                            if si >= 2:
                                sync.wait_ge(sem.h["TE"], tok[("f6u", si - 2)])
                            if gi < EC:
                                sync.wait_ge(sem.h["STH"], tok[("xrsp", ec)])
                            elif gi < 2 * EC:
                                sync.wait_ge(sem.h["STH"], tok[("xksp", ec)])
                            sem.inc(sync.dma_start(out=bufs[ec % 2][0][:],
                                                   in_=spt[0, ec]),
                                    "LH%d" % (ec % 2), 16)
                            tok[("f6l", si)] = sem.inc(
                                sync.dma_start(out=bufs[ec % 2][1][:],
                                               in_=spt[1, ec]),
                                "LH%d" % (ec % 2), 16)

                        @block.tensor
                        def _(tensor, gi=gi, ec=ec, bufs=bufs):
                            si = gi * EC + ec
                            tensor.wait_ge(sem.h["LH%d" % (ec % 2)],
                                           tok[("f6l", si)])
                            if ec == 0:
                                tensor.wait_ge(sem.h["VE"], tok[("w6sp", gi)])
                                if gi >= 2:
                                    tensor.wait_ge(sem.h["VE"],
                                                   tok[("fev", gi - 2)])
                            ps = psFs[gi % 2]
                            whi_ = w6h[gi % 2]
                            wlo_ = w6l[gi % 2]
                            first = (ec == 0)
                            ins = None
                            for qt in range(NQT):
                                sl = slice(qt * QT, (qt + 1) * QT)
                                tensor.matmul(ps[:, sl], wsl(whi_, ec),
                                              bufs[ec % 2][0][:, sl],
                                              start=first, stop=False)
                                tensor.matmul(ps[:, sl], wsl(whi_, ec),
                                              bufs[ec % 2][1][:, sl],
                                              start=False, stop=False)
                                ins = tensor.matmul(ps[:, sl], wsl(wlo_, ec),
                                                    bufs[ec % 2][0][:, sl],
                                                    start=False,
                                                    stop=(ec == EC - 1))
                            tok[("f6u", si)] = sem.inc(ins, "TE", 1)
                    tok[("fmm", gi)] = tok[("f6u", gi * EC + EC - 1)]

                def emit_rev(dc):
                    @block.scalar
                    def _(scalar, dc=dc):
                        scalar.wait_ge(sem.h["TE"], tok[("fmm", dc)])
                        if dc >= 2:
                            scalar.wait_ge(sem.h["STR"], tok[("rsp", dc - 2)])
                        tok[("rev", dc)] = sem.inc(
                            scalar.activation(r32s[dc % 2][:], psFs[dc % 2][:],
                                              ACT.Sigmoid), "SC", 1)

                    @block.vector
                    def _(vector, dc=dc):
                        vector.wait_ge(sem.h["SC"], tok[("rev", dc)])
                        tok[("fev", dc)] = sem.inc(
                            vector.tensor_copy(s6[:, 0:1],
                                               r32s[dc % 2][:, 0:1]), "VE", 1)

                    @block.sync
                    def _(sync, dc=dc):
                        sync.wait_ge(sem.h["SC"], tok[("rev", dc)])
                        tok[("rsp", dc)] = sem.inc(
                            sync.dma_start(out=r_sp[dc], in_=r32s[dc % 2][:]),
                            "STR", 16)

                def emit_kev(dc, second):
                    gi = EC + dc

                    @block.scalar
                    def _(scalar, dc=dc, gi=gi, second=second):
                        scalar.wait_ge(sem.h["TE"], tok[("fmm", gi)])
                        if second:
                            scalar.wait_ge(sem.h["VE"], tok[("kact", dc - 1)])
                        tok[("kev", dc)] = sem.inc(
                            scalar.activation(s6[:], psFs[gi % 2][:], ACT.Relu),
                            "SC", 1)

                    @block.vector
                    def _(vector, dc=dc, gi=gi):
                        vector.wait_ge(sem.h["SC"], tok[("kev", dc)])
                        vector.tensor_tensor(y32[:], s6[:], s6[:], ALU.mult)
                        vector.tensor_copy(kchunk_hi(dc), y32[:])
                        tok[("kact", dc)] = sem.inc(
                            vector.scalar_tensor_tensor(
                                kchunk_lo(dc), y32[:], 1.0, kchunk_hi(dc),
                                ALU.mult, ALU.subtract), "VE", 1)
                        tok[("fev", gi)] = tok[("kact", dc)]

                for dp in range(EC // 2):
                    w6chain(2 * dp)
                    w6chain(2 * dp + 1)
                    fmm2_streamed(0, dp, xr_sp, xs6)
                    emit_rev(2 * dp)
                    emit_rev(2 * dp + 1)

                for dp in range(EC // 2):
                    w6chain(EC + 2 * dp)
                    w6chain(EC + 2 * dp + 1)
                    fmm2_streamed(EC, dp, xk_sp, xs4)
                    emit_kev(2 * dp, False)
                    emit_kev(2 * dp + 1, True)

                for dc in range(EC):
                    gi = 2 * EC + dc
                    w6chain(gi)

                    @block.tensor
                    def _(tensor, dc=dc, gi=gi):
                        tensor.wait_ge(sem.h["VE"], tok[("w6sp", gi)])
                        tensor.wait_ge(sem.h["VE"], tok[("fev", gi - 2)])
                        tensor.wait_ge(sem.h["VE"], tok[("kact", EC - 1)])
                        ps = psFs[gi % 2]
                        whi_ = w6h[gi % 2]
                        wlo_ = w6l[gi % 2]
                        ins = None
                        for ec in range(EC):
                            first = (ec == 0)
                            for qt in range(NQT):
                                sl = slice(qt * QT, (qt + 1) * QT)
                                tensor.matmul(ps[:, sl], wsl(whi_, ec),
                                              kchunk_hi(ec)[:, sl],
                                              start=first, stop=False)
                                tensor.matmul(ps[:, sl], wsl(whi_, ec),
                                              kchunk_lo(ec)[:, sl],
                                              start=False, stop=False)
                                ins = tensor.matmul(ps[:, sl], wsl(wlo_, ec),
                                                    kchunk_hi(ec)[:, sl],
                                                    start=False,
                                                    stop=(ec == EC - 1))
                        tok[("fmm", gi)] = sem.inc(ins, "TE", 1)

                    @block.sync
                    def _(sync, dc=dc):
                        if dc >= 2:
                            sync.wait_ge(sem.h["VE"], tok[("yev", dc - 2)])
                        sem.inc(sync.dma_start(out=r32s[dc % 2][:],
                                               in_=r_sp[dc]),
                                "LX%d" % (dc % 2), 16)
                        tok[("r6", dc)] = sem.cur("LX%d" % (dc % 2))
                        sem.inc(sync.dma_start(out=xs4[dc % 2][0][:],
                                               in_=x1_sp[0, dc]),
                                "LH%d" % (dc % 2), 16)
                        tok[("x6", dc)] = sem.inc(
                            sync.dma_start(out=xs4[dc % 2][1][:],
                                           in_=x1_sp[1, dc]),
                            "LH%d" % (dc % 2), 16)

                    @block.vector
                    def _(vector, dc=dc, gi=gi):
                        vector.wait_ge(sem.h["TE"], tok[("fmm", gi)])
                        vector.wait_ge(sem.h["LX%d" % (dc % 2)], tok[("r6", dc)])
                        vector.wait_ge(sem.h["LH%d" % (dc % 2)], tok[("x6", dc)])
                        if dc >= 1:
                            vector.wait_ge(sem.h["STY"], tok[("ysp", dc - 1)])
                        vector.tensor_tensor(s6[:], xs4[dc % 2][0][:],
                                             xs4[dc % 2][1][:], ALU.add)
                        vector.scalar_tensor_tensor(y32[:], psFs[gi % 2][:], 1.0,
                                                    r32s[dc % 2][:], ALU.mult,
                                                    ALU.mult)
                        tok[("yev", dc)] = sem.inc(
                            vector.tensor_tensor(y32[:], y32[:], s6[:], ALU.add),
                            "VE", 1)
                        tok[("fev", gi)] = tok[("yev", dc)]

                    @block.sync
                    def _(sync, dc=dc):
                        sync.wait_ge(sem.h["VE"], tok[("yev", dc)])
                        tok[("ysp", dc)] = sem.inc(
                            sync.dma_start(out=y[dc * 128:(dc + 1) * 128, :],
                                           in_=y32[:]), "STY", 16)

                @block.sync
                def _(sync):
                    sync.wait_ge(sem.h["STY"], sem.cur("STY"))
                    sync.wait_ge(sem.h["STR"], sem.cur("STR"))
    lower_extended_insts(nc)
    return nc


_BUILD_CACHE = {}


def _get_nc(tap=None):
    if tap not in _BUILD_CACHE:
        _BUILD_CACHE[tap] = build(tap)
    return _BUILD_CACHE[tap]


def _stage_inputs(kw):
    x = np.asarray(kw["L0_attn_x"])
    B = x.shape[0]
    xt = np.ascontiguousarray(x.transpose(0, 2, 1))
    c = _host_constants()
    base = {("c_" + k): v for k, v in c.items()}
    for name, key in [("bq", "L0_attn_bq"), ("bk", "L0_attn_bk"),
                      ("bv", "L0_attn_bv"), ("bg", "L0_attn_bg"),
                      ("bo", "L0_attn_bo"), ("gamma", "L0_attn_gamma"),
                      ("beta", "L0_attn_beta"), ("mu_r", "L0_ffn_mu_r"),
                      ("mu_k", "L0_ffn_mu_k")]:
        base["p_" + name] = _col8(np.asarray(kw[key]))
    base["p_bd"] = np.ascontiguousarray(np.asarray(kw["L0_attn_bd"])[None, :])
    base["w_dT"] = np.ascontiguousarray(
        np.asarray(kw["L0_attn_wd"]).T.reshape(EC, 128, H))
    for m, key in [("q", "L0_attn_wq"), ("k", "L0_attn_wk"), ("v", "L0_attn_wv"),
                   ("g", "L0_attn_wg"), ("o", "L0_attn_wo"), ("r", "L0_ffn_W_r"),
                   ("k2", "L0_ffn_W_k"), ("v2", "L0_ffn_W_v")]:
        base["w_" + m] = _blk_pair(np.ascontiguousarray(np.asarray(kw[key]).T))
    in_maps = []
    for b in range(B):
        d = dict(base)
        d["x_fm"] = xt[b]
        in_maps.append(d)
    return in_maps


def kernel(**kw):
    in_maps = _stage_inputs(kw)
    nc = _get_nc(None)
    res = run_bass_kernel_spmd(nc, in_maps, core_ids=list(range(8)))
    out = np.stack([np.asarray(r["y"]).T for r in res.results], axis=0)
    return out.astype(np.float32)


# version 12
# speedup vs baseline: 1.1052x; 1.0102x over previous
import contextlib
import math

import numpy as np
import concourse.bass as bass
import concourse.mybir as mybir
from concourse.bass_utils import run_bass_kernel_spmd
from concourse.bass_isa import ReduceOp
from concourse.library_config import attn as _attn_lib
from concourse.library_overlay import lower_extended_insts

F32 = mybir.dt.float32
F16 = mybir.dt.float16
ALU = mybir.AluOpType
ACT = mybir.ActivationFunctionType
AX = mybir.AxisListType

T = 2048
E = 1024
H = 16
HD = 64
EC = 8
NPAIR = 8
KC = 16
QT = 512
NQT = T // QT
LN2_ = math.log(2.0)
WQ, WK, WV, WG = 0, 1, 2, 3
SERIAL_ATTN = False


def _host_constants():
    t = np.arange(T, dtype=np.float64)
    j = np.arange(HD)
    inv = 1.0 / (10000.0 ** ((2.0 * (j % 32)) / HD))
    ang = t[None, :] * inv[:, None]
    cos = np.cos(ang)
    sin = np.sin(ang)
    sgn = np.where(j % 2 == 0, -1.0, 1.0)
    sin_s = sin * sgn[:, None]
    cos2 = np.concatenate([cos, cos], axis=0).astype(np.float32)
    sin2 = np.concatenate([sin_s, sin_s], axis=0).astype(np.float32)
    ii = np.arange(128)
    U = (ii[:, None] > ii[None, :]).astype(np.float32)
    L = (ii[:, None] <= ii[None, :]).astype(np.float32)
    kramp = (ii[:, None] + 128.0 * np.arange(KC)[None, :]).astype(np.float32)
    qramp = np.arange(QT, dtype=np.float32)[None, :]
    ident = np.eye(128, dtype=np.float32)
    ones_col = np.ones((128, 1), dtype=np.float32)
    return dict(cos2=cos2, sin2=sin2, U=U, L=L, kramp=kramp, qramp=qramp,
                ident=ident, ones_col=ones_col)


def _blk_pair(wt):
    return np.ascontiguousarray(wt.reshape(EC, 128, EC, 128).transpose(2, 1, 0, 3))


def _col8(v):
    return np.ascontiguousarray(v.reshape(8, 128).T)


class _Sems:
    def __init__(self, nc, stack, names):
        self.h = {}
        self.n = {}
        for name in names:
            self.h[name] = stack.enter_context(nc.semaphore(name))
            self.n[name] = 0

    def inc(self, ins, name, k=1):
        ins.then_inc(self.h[name], k)
        self.n[name] += k
        return self.n[name]

    def cur(self, name):
        return self.n[name]


def build(tap=None):
    nc = bass.Bass("TRN2", target_bir_lowering=False,
                   detect_race_conditions=False)

    x_fm = nc.dram_tensor("x_fm", [E, T], F32, kind="ExternalInput").ap()
    consts = {
        "cos2": [128, T], "sin2": [128, T], "U": [128, 128], "L": [128, 128],
        "kramp": [128, KC], "qramp": [1, QT], "ident": [128, 128],
        "ones_col": [128, 1],
    }
    cst = {k: nc.dram_tensor("c_" + k, sh, F32, kind="ExternalInput").ap()
           for k, sh in consts.items()}
    pnames = ["bq", "bk", "bv", "bg", "bo", "gamma", "beta", "mu_r", "mu_k"]
    par = {k: nc.dram_tensor("p_" + k, [128, 8], F32, kind="ExternalInput").ap()
           for k in pnames}
    bd = nc.dram_tensor("p_bd", [1, H], F32, kind="ExternalInput").ap()
    wdT = nc.dram_tensor("w_dT", [EC, 128, H], F32, kind="ExternalInput").ap()
    wblk = {m: nc.dram_tensor("w_" + m, [8, 128, EC, 128], F32,
                              kind="ExternalInput").ap()
            for m in ["q", "k", "v", "g", "o", "r", "k2", "v2"]}

    def sp16(name, tapped):
        kind = "ExternalOutput" if tapped else "Internal"
        return nc.dram_tensor(name, [2, EC, 128, T], F16, kind=kind).ap()

    h_sp = sp16("h_sp", tap in ("h", "all", "allq"))
    gon_sp = sp16("gon_sp", tap in ("gon", "all", "allq"))
    x1_sp = sp16("x1_sp", tap in ("x1", "all", "allq"))
    r_sp = nc.dram_tensor("r_sp", [EC, 128, T], F32,
                          kind="ExternalOutput" if tap in ("r", "all", "allq")
                          else "Internal").ap()
    if tap == "qk":
        dbg_qk = nc.dram_tensor("dbg_qk", [4, 128, T], F16,
                                kind="ExternalOutput").ap()
    if tap == "og":
        dbg_og = nc.dram_tensor("dbg_og", [2, 64, T], F32,
                                kind="ExternalOutput").ap()
    if tap in ("pb", "pb0"):
        dbg_pb = nc.dram_tensor("dbg_pb", [2, 128, KC * QT], F32,
                                kind="ExternalOutput").ap()
    if tap == "cs0":
        dbg_cs = nc.dram_tensor("dbg_cs", [66, QT], F32,
                                kind="ExternalOutput").ap()
    if tap in ("ld", "hm"):
        dbg_ld = nc.dram_tensor("dbg_ld", [4, 1, H], F32,
                                kind="ExternalOutput").ap()
        dbg_hm = nc.dram_tensor("dbg_hm", [128, EC], F32,
                                kind="ExternalOutput").ap()
        dbg_hs = nc.dram_tensor("dbg_hs", [2, 128, EC], F32,
                                kind="ExternalOutput").ap()
    y = nc.dram_tensor("y", [E, T], F32, kind="ExternalOutput").ap()
    if tap == "all":
        dbg_b = nc.dram_tensor("dbg_b", [NPAIR, 4, 2], F32,
                               kind="ExternalOutput").ap()
        dbg_ogall = nc.dram_tensor("dbg_ogall", [NPAIR, 2, 64, T], F32,
                                   kind="ExternalOutput").ap()
        dbg_gate = nc.dram_tensor("dbg_gate", [NPAIR, 2, 64, T], F32,
                                  kind="ExternalOutput").ap()
        dbg_r2 = nc.dram_tensor("dbg_r2", [NPAIR, 1, 16], F32,
                                kind="ExternalOutput").ap()

    TAP_PAIRS = 0 if tap in ("ld", "hm") else (1 if tap in ("qk", "og", "pb", "pb0", "cs0") else NPAIR)
    tap_now = [None]

    with contextlib.ExitStack() as st:
        sem = _Sems(nc, st, ["LD", "TE", "SC", "VE", "GP",
                             "LX0", "LX1", "LCS", "LW0", "LW1", "LH0", "LH1",
                             "STH", "STG", "STX", "STR", "STY"])
        sb = lambda name, shape, dt=F32: st.enter_context(
            nc.sbuf_tensor(name, shape, dt))
        block = st.enter_context(nc.Block())

        c_U = sb("sb_U", [128, 128])
        c_L = sb("sb_L", [128, 128])
        c_kramp = sb("sb_kramp", [128, KC])
        c_qramp = sb("sb_qramp", [1, QT])
        c_ident = sb("sb_ident", [128, 128])
        c_ones = sb("sb_ones", [128, 1])
        ones16 = sb("sb_ones16", [128, 1], F16)
        p_sb = {k: sb("p" + k, [128, 8]) for k in pnames}
        p_bd = sb("pbd", [1, H])
        pdrow = sb("pdrow", [1, H])
        ldrow = sb("ldrow", [1, H])
        ldnrow = sb("ldnrow", [1, H])
        hrow = sb("hrow", [1, H])
        ldb = sb("ldb", [128, H])
        ldks = sb("ldks", [128, H * KC])
        eks = sb("eks", [128, H * KC])
        hsum = sb("hsum", [128, EC])
        hsum2 = sb("hsum2", [128, EC])
        hmean = sb("hmean", [128, EC])
        browq = sb("browq", [1, 2])
        browk = sb("browk", [1, 2])
        bbc = sb("bbc", [128, 2])
        ldnb = sb("ldnb", [128, 2])
        bfh = sb("bfh", [128, 2 * KC])
        gnb = sb("gnb", [64, 4])
        gnc = sb("gnc", [64, 4])
        r2t = sb("r2t", [1, 16])
        lqrow = sb("lqrow", [1, 8])
        ctb = sb("ctb", [128, 2])
        flsh = sb("flsh", [8, 1024])

        tok = {}
        prev = {}
        stB = None

        def vsync(vector, ins):
            t = sem.inc(ins, "VE", 1)
            vector.wait_ge(sem.h["VE"], t)
            return t

        def vpub(vector, ins):
            t = sem.inc(ins, "VE", 1)
            vector.wait_ge(sem.h["VE"], t)
            return sem.inc(vector.tensor_copy(flsh[:, 512:1024],
                                              flsh[:, 0:512]), "VE", 1)

        def spub(scalar, ins):
            t = sem.inc(ins, "SC", 1)
            scalar.wait_ge(sem.h["SC"], t)
            return sem.inc(scalar.activation(flsh[:, 0:512],
                                             flsh[:, 512:1024], ACT.Identity),
                           "SC", 1)

        with nc.sbuf_tensor("xb0", [128, T], F32) as xb0, \
             nc.sbuf_tensor("xb1", [128, T], F32) as xb1, \
             nc.sbuf_tensor("sq0", [128, T], F32) as sq0, \
             nc.sbuf_tensor("sq1", [128, T], F32) as sq1, \
             nc.sbuf_tensor("m1row", [1, T], F32) as m1row, \
             nc.sbuf_tensor("rstd1row", [1, T], F32) as rstd1row, \
             nc.sbuf_tensor("m1b", [128, T], F32) as m1b, \
             nc.sbuf_tensor("rstd1b", [128, T], F32) as rstd1b, \
             nc.sbuf_tensor("hhib", [128, T], F16) as hhib, \
             nc.sbuf_tensor("hlob", [128, T], F16) as hlob, \
             nc.sbuf_tensor("wdTs", [128, EC * H], F32) as w_dT, \
             nc.psum_tensor("psS", [1, T], F32) as psS, \
             nc.psum_tensor("psQ", [1, T], F32) as psQ:

            xb = [xb0, xb1]
            sq = [sq0, sq1]
            t1b = sq0
            h32b = sq1
            vrow = psS[0:1, T // 2:T]

            @block.sync
            def _(sync):
                for k, tile in [("U", c_U), ("L", c_L), ("kramp", c_kramp),
                                ("qramp", c_qramp), ("ident", c_ident),
                                ("ones_col", c_ones)]:
                    sem.inc(sync.dma_start(out=tile[:], in_=cst[k]), "LD", 16)
                for k in pnames:
                    sem.inc(sync.dma_start(out=p_sb[k][:], in_=par[k]), "LD", 16)
                sem.inc(sync.dma_start(out=p_bd[:], in_=bd), "LD", 16)
                for ec in range(EC):
                    sem.inc(sync.dma_start(out=w_dT[:, ec * H:(ec + 1) * H],
                                           in_=wdT[ec]), "LD", 16)
                tok["consts"] = sem.cur("LD")

            @block.gpsimd
            def _(gpsimd):
                gpsimd.load_library(_attn_lib)
                sem.inc(gpsimd.engine_nop(), "GP", 1)

            @block.vector
            def _(vector):
                vector.wait_ge(sem.h["LD"], tok["consts"])
                vector.memset(ones16[:], 1.0)
                vector.memset(flsh[:], 0.0)
                vector.memset(ctb[:, 0:1], 1e-5)
                sem.inc(vector.memset(ctb[:, 1:2], -5.0 * LN2_), "VE", 1)

            for ec in range(EC):
                @block.sync
                def _(sync, ec=ec):
                    if ec >= 2:
                        sync.wait_ge(sem.h["TE"], tok[("st_te", ec - 2)])
                    tok[("x0", ec)] = sem.inc(
                        sync.dma_start(out=xb[ec % 2][:],
                                       in_=x_fm[ec * 128:(ec + 1) * 128, :]),
                        "LX%d" % (ec % 2), 16)

                @block.scalar
                def _(scalar, ec=ec):
                    scalar.wait_ge(sem.h["LX%d" % (ec % 2)], tok[("x0", ec)])
                    if ec >= 2:
                        scalar.wait_ge(sem.h["TE"], tok[("st_te", ec - 2)])
                    tok[("sq", ec)] = sem.inc(
                        scalar.activation(sq[ec % 2][:], xb[ec % 2][:], ACT.Square),
                        "SC", 1)

                @block.tensor
                def _(tensor, ec=ec):
                    tensor.wait_ge(sem.h["LX%d" % (ec % 2)], tok[("x0", ec)])
                    if ec == 0:
                        tensor.wait_ge(sem.h["LD"], tok["consts"])
                    for qt in range(NQT):
                        tensor.matmul(psS[0:1, qt * QT:(qt + 1) * QT], c_ones[:],
                                      xb[ec % 2][:, qt * QT:(qt + 1) * QT],
                                      start=(ec == 0), stop=(ec == EC - 1))
                    tensor.wait_ge(sem.h["SC"], tok[("sq", ec)])
                    ins = None
                    for qt in range(NQT):
                        ins = tensor.matmul(psQ[0:1, qt * QT:(qt + 1) * QT],
                                            c_ones[:],
                                            sq[ec % 2][:, qt * QT:(qt + 1) * QT],
                                            start=(ec == 0), stop=(ec == EC - 1))
                    tok[("st_te", ec)] = sem.inc(ins, "TE", 1)

            @block.vector
            def _(vector):
                vector.wait_ge(sem.h["TE"], tok[("st_te", EC - 1)])
                vector.tensor_scalar(m1row[:], psS[0:1, :], 1.0 / E, None, ALU.mult)
                vector.tensor_tensor(rstd1row[:], m1row[:], m1row[:], ALU.mult)
                tok["var"] = sem.inc(
                    vector.scalar_tensor_tensor(rstd1row[:], psQ[0:1, :], 1.0 / E,
                                                rstd1row[:], ALU.mult,
                                                ALU.subtract), "VE", 1)

            @block.scalar
            def _(scalar):
                scalar.wait_ge(sem.h["VE"], tok["var"])
                tok["sqrt"] = sem.inc(
                    scalar.activation(rstd1row[:], rstd1row[:], ACT.Sqrt,
                                      bias=ctb[0:1, 0:1]), "SC", 1)

            @block.vector
            def _(vector):
                vector.wait_ge(sem.h["SC"], tok["sqrt"])
                tok["rstd"] = vpub(
                    vector, vector.reciprocal(rstd1row[:], rstd1row[:]))

            @block.gpsimd
            def _(gpsimd):
                gpsimd.wait_ge(sem.h["VE"], tok["rstd"])
                gpsimd.partition_broadcast(m1b[:], m1row[:], 128)
                tok["bcast"] = sem.inc(
                    gpsimd.partition_broadcast(rstd1b[:], rstd1row[:], 128), "GP", 1)

            for ec in range(EC):
                @block.sync
                def _(sync, ec=ec):
                    if ec >= 2:
                        sync.wait_ge(sem.h["VE"], tok[("h32", ec - 2)])
                    else:
                        sync.wait_ge(sem.h["SC"], tok[("sq", EC - 1)])
                        sync.wait_ge(sem.h["TE"], tok[("st_te", EC - 1)])
                    tok[("x1", ec)] = sem.inc(
                        sync.dma_start(out=xb[ec % 2][:],
                                       in_=x_fm[ec * 128:(ec + 1) * 128, :]),
                        "LX%d" % (ec % 2), 16)

                @block.vector
                def _(vector, ec=ec):
                    vector.wait_ge(sem.h["LX%d" % (ec % 2)], tok[("x1", ec)])
                    if ec == 0:
                        vector.wait_ge(sem.h["GP"], tok["bcast"])
                    if ec >= 1:
                        vector.wait_ge(sem.h["SC"], tok[("hhi", ec - 1)])
                    vector.tensor_tensor(t1b[:], xb[ec % 2][:], m1b[:], ALU.subtract)
                    tok[("h32", ec)] = sem.inc(
                        vector.tensor_tensor(h32b[:], t1b[:], rstd1b[:], ALU.mult),
                        "VE", 1)

                @block.scalar
                def _(scalar, ec=ec):
                    scalar.wait_ge(sem.h["VE"], tok[("h32", ec)])
                    if ec >= 1:
                        scalar.wait_ge(sem.h["STH"], tok[("hsp", ec - 1)])
                    tok[("hhi", ec)] = sem.inc(
                        scalar.activation(hhib[:], h32b[:], ACT.Copy), "SC", 1)

                @block.vector
                def _(vector, ec=ec):
                    vector.wait_ge(sem.h["SC"], tok[("hhi", ec)])
                    vector.scalar_tensor_tensor(
                        hlob[:], h32b[:], 1.0, hhib[:], ALU.mult, ALU.subtract)
                    vsync(vector, vector.reduce_sum(hsum[:, ec:ec + 1], hhib[:],
                                                    AX.X))
                    tok[("hlo", ec)] = vsync(
                        vector, vector.reduce_sum(hsum2[:, ec:ec + 1], hlob[:],
                                                  AX.X))

                @block.sync
                def _(sync, ec=ec):
                    sync.wait_ge(sem.h["VE"], tok[("hlo", ec)])
                    sem.inc(sync.dma_start(out=h_sp[0, ec], in_=hhib[:]), "STH", 16)
                    tok[("hsp", ec)] = sem.inc(
                        sync.dma_start(out=h_sp[1, ec], in_=hlob[:]), "STH", 16)

            @block.vector
            def _(vector):
                vector.wait_ge(sem.h["VE"], tok[("hlo", EC - 1)])
                tok["hmean"] = vpub(
                    vector,
                    vector.tensor_tensor(hmean[:], hsum[:], hsum2[:], ALU.add))

            if tap == "hm":
                @block.sync
                def _(sync):
                    sync.wait_ge(sem.h["VE"], tok["hmean"])
                    sem.inc(sync.dma_start(out=dbg_hm, in_=hmean[:]), "STY", 16)
                    sem.inc(sync.dma_start(out=dbg_hs[0], in_=hsum[:]), "STY", 16)
                    sem.inc(sync.dma_start(out=dbg_hs[1], in_=hsum2[:]), "STY", 16)
                    sync.wait_ge(sem.h["STY"], sem.cur("STY"))
                    sync.wait_ge(sem.h["STH"], sem.cur("STH"))

            @block.tensor
            def _(tensor):
                tensor.wait_ge(sem.h["VE"], tok["hmean"])
                ins = None
                for ec in range(EC):
                    ins = tensor.matmul(psS[0:1, 0:H], hmean[:, ec:ec + 1],
                                        w_dT[:, ec * H:(ec + 1) * H],
                                        start=(ec == 0), stop=(ec == EC - 1),
                                        skip_group_check=True)
                tok["pdmm"] = sem.inc(ins, "TE", 1)

            @block.vector
            def _(vector):
                vector.wait_ge(sem.h["TE"], tok["pdmm"])
                tok["pdl"] = vpub(
                    vector,
                    vector.scalar_tensor_tensor(pdrow[:], psS[0:1, 0:H], 1.0 / T,
                                                p_bd[:], ALU.mult, ALU.add))

            @block.scalar
            def _(scalar):
                scalar.wait_ge(sem.h["VE"], tok["pdl"])
                ts = sem.inc(scalar.activation(pdrow[:], pdrow[:], ACT.Sigmoid),
                             "SC", 1)
                scalar.wait_ge(sem.h["SC"], ts)
                tok["pd"] = spub(
                    scalar,
                    scalar.activation(hrow[:], pdrow[:], ACT.Exp,
                                      bias=ctb[0:1, 1:2], scale=-8.0 * LN2_))

            @block.vector
            def _(vector):
                vector.wait_ge(sem.h["SC"], tok["pd"])
                tok["w1"] = vpub(
                    vector,
                    vector.tensor_scalar(ldnrow[:], hrow[:], -1.0, 1.0, ALU.mult,
                                         ALU.add))

            @block.scalar
            def _(scalar):
                scalar.wait_ge(sem.h["VE"], tok["w1"])
                tok["ld"] = spub(
                    scalar, scalar.activation(ldrow[:], ldnrow[:], ACT.Ln))

            @block.vector
            def _(vector):
                vector.wait_ge(sem.h["SC"], tok["ld"])
                tok["ldn"] = sem.inc(
                    vector.tensor_scalar(ldnrow[:], ldrow[:], -1.0, None, ALU.mult),
                    "VE", 1)

            @block.gpsimd
            def _(gpsimd):
                gpsimd.wait_ge(sem.h["VE"], tok["ldn"])
                ins = None
                for h in range(H):
                    ins = gpsimd.partition_broadcast(ldb[:, h:h + 1],
                                                     ldrow[0:1, h:h + 1], 128)
                tok["ldb"] = sem.inc(ins, "GP", 1)

            @block.vector
            def _(vector):
                vector.wait_ge(sem.h["GP"], tok["ldb"])
                ins = None
                for h in range(H):
                    ins = vector.tensor_scalar(ldks[:, h * KC:(h + 1) * KC],
                                               c_kramp[:], ldb[:, h:h + 1],
                                               None, ALU.mult)
                tok["ldks"] = sem.inc(ins, "VE", 1)

            @block.scalar
            def _(scalar):
                scalar.wait_ge(sem.h["VE"], tok["ldks"])
                tok["eks"] = sem.inc(
                    scalar.activation(eks[:], ldks[:], ACT.Exp, scale=-1.0), "SC", 1)

        if tap == "h":
            @block.sync
            def _(sync):
                sync.wait_ge(sem.h["STH"], sem.cur("STH"))
                sync.wait_ge(sem.h["SC"], sem.cur("SC"))
            lower_extended_insts(nc)
            return nc

        if tap == "hm":
            lower_extended_insts(nc)
            return nc

        if tap == "ld":
            @block.sync
            def _(sync):
                sync.wait_ge(sem.h["SC"], tok["eks"])
                sem.inc(sync.dma_start(out=dbg_ld[0], in_=pdrow[:]), "STY", 16)
                sem.inc(sync.dma_start(out=dbg_ld[1], in_=hrow[:]), "STY", 16)
                sem.inc(sync.dma_start(out=dbg_ld[2], in_=ldrow[:]), "STY", 16)
                sem.inc(sync.dma_start(out=dbg_ld[3], in_=ldnrow[:]), "STY", 16)
                sem.inc(sync.dma_start(out=dbg_hm, in_=hmean[:]), "STY", 16)
                sem.inc(sync.dma_start(out=dbg_hs[0], in_=hsum[:]), "STY", 16)
                sem.inc(sync.dma_start(out=dbg_hs[1], in_=hsum2[:]), "STY", 16)
                sync.wait_ge(sem.h["STY"], sem.cur("STY"))
                sync.wait_ge(sem.h["STH"], sem.cur("STH"))
            lower_extended_insts(nc)
            return nc

        stB = st.enter_context(contextlib.ExitStack())
        sbB = lambda name, shape, dt=F32: stB.enter_context(
            nc.sbuf_tensor(name, shape, dt))
        arena = sbB("arena", [128, 2 * KC * QT])
        wst_ = sbB("wst0", [128, EC * 128])
        wst = [wst_, wst_]
        whi = [sbB(f"whi{i}", [128, EC * 128], F16) for i in range(4)]
        wlo = [sbB(f"wlo{i}", [128, EC * 128], F16) for i in range(4)]
        hstr = [[sbB(f"hstr{i}_{d}", [128, T], F16) for d in range(2)]
                for i in range(2)]
        qhi = sbB("qhi", [128, T], F16)
        qlo = sbB("qlo", [128, T], F16)
        khi = sbB("khi", [128, T], F16)
        klo = sbB("klo", [128, T], F16)
        vtm = [sbB(f"vtm{hh}", [128, KC * 65]) for hh in range(2)]
        vsc = [sbB(f"vsc{hh}", [128, KC * 65]) for hh in range(2)]
        gateh = [sbB(f"gateh{hh}", [64, T]) for hh in range(2)]
        ogt = [sbB(f"ogt{hh}", [64, T]) for hh in range(2)]
        stripL = sbB("stripL", [128, 2 * 2 * 128])
        stripU = sbB("stripU", [128, 2 * 2 * 128])
        csc1 = sbB("csc1", [65, QT])
        linvb = sbB("linvb", [64, QT])
        dbb1v = sbB("dbb1", [65, QT])
        dbb1 = [dbb1v, dbb1v]
        r1t = sbB("r1t", [1, QT])
        sqsc = sbB("sqsc", [128, T], F16)

        cosv = arena[:, 0:T]
        sinv = arena[:, T:2 * T]
        pbuf = [arena[:, 0:KC * QT], arena[:, KC * QT:2 * KC * QT]]
        shuf_mask = [i + 1 if i % 2 == 0 else i - 1 for i in range(32)]

        def wsl(buf, ec):
            return buf[:, ec * 128:(ec + 1) * 128]

        def _mm(tensor, dst, lhsT, rhs, key, state, stop=False):
            first = key not in state
            state[key] = True
            return tensor.matmul(dst, lhsT, rhs, start=first, stop=stop,
                                 skip_group_check=True)

        def emit_wchain(pp):
            wm = [("q", ("hu1", pp - 1, EC - 1)), ("k", ("hu1", pp - 1, EC - 1)),
                  ("v", ("hu2", pp - 1, EC - 1)), ("g", ("hu2", pp - 1, EC - 1))]
            for i, (m, conflict) in enumerate(wm):
                @block.sync
                def _(sync, pp=pp, i=i, m=m):
                    guard = ("wsp", pp, i - 1) if i >= 1 else \
                        ("wsp", pp - 1, 3)
                    if guard in tok:
                        sync.wait_ge(sem.h["VE"], tok[guard])
                    tok[("w32", pp, i)] = sem.inc(
                        sync.dma_start(out=wst[i % 2][:], in_=wblk[m][pp]),
                        "LW%d" % (i % 2), 16)

                @block.scalar
                def _(scalar, pp=pp, i=i, conflict=conflict):
                    scalar.wait_ge(sem.h["LW%d" % (i % 2)], tok[("w32", pp, i)])
                    if pp > 0:
                        scalar.wait_ge(sem.h["TE"], tok[conflict])
                    tok[("whi", pp, i)] = sem.inc(
                        scalar.activation(whi[i][:], wst[i % 2][:], ACT.Copy),
                        "SC", 1)

                @block.vector
                def _(vector, pp=pp, i=i):
                    vector.wait_ge(sem.h["SC"], tok[("whi", pp, i)])
                    tok[("wsp", pp, i)] = sem.inc(
                        vector.scalar_tensor_tensor(
                            wlo[i][:], wst[i % 2][:], 1.0, whi[i][:],
                            ALU.mult, ALU.subtract), "VE", 1)

        emit_wchain(0)

        for p in range(TAP_PAIRS):
            with nc.psum_tensor(f"psA_{p}", [128, T], F32) as psA, \
                 nc.psum_tensor(f"psB_{p}", [128, T], F32) as psB, \
                 nc.sbuf_tensor(f"q32_{p}", [128, T], F32) as q32:

                s1 = arena[:, 2 * T:3 * T]

                def wbuf_hi(i):
                    return whi[i]

                def wbuf_lo(i):
                    return wlo[i]

                @block.gpsimd
                def _(gpsimd, p=p):
                    if p > 0:
                        gpsimd.wait_ge(sem.h["TE"], prev[("pvdone", p - 1)])
                    sem.inc(gpsimd.dma_start(out=cosv, in_=cst["cos2"]),
                            "LCS", 16)
                    tok[("cs", p)] = sem.inc(
                        gpsimd.dma_start(out=sinv, in_=cst["sin2"]), "LCS", 16)

                for ec in range(EC):
                    @block.sync
                    def _(sync, p=p, ec=ec):
                        if ec >= 2:
                            sync.wait_ge(sem.h["TE"], tok[("hu1", p, ec - 2)])
                        elif p > 0:
                            sync.wait_ge(sem.h["TE"],
                                         prev[("hu2", p - 1, EC - 2 + ec)])
                        else:
                            sync.wait_ge(sem.h["STH"], tok[("hsp", EC - 1)])
                        sem.inc(sync.dma_start(out=hstr[ec % 2][0][:],
                                               in_=h_sp[0, ec]),
                                "LH%d" % (ec % 2), 16)
                        tok[("hl1", p, ec)] = sem.inc(
                            sync.dma_start(out=hstr[ec % 2][1][:],
                                           in_=h_sp[1, ec]),
                            "LH%d" % (ec % 2), 16)

                    @block.tensor
                    def _(tensor, p=p, ec=ec):
                        tensor.wait_ge(sem.h["LH%d" % (ec % 2)],
                                       tok[("hl1", p, ec)])
                        if ec == 0:
                            if p > 0:
                                tensor.wait_ge(sem.h["SC"],
                                               prev[("expdone", p - 1)])
                                tensor.wait_ge(sem.h["VE"],
                                               prev[("combdone", p - 1)])
                            tensor.wait_ge(sem.h["VE"], tok[("wsp", p, WK)])
                        ins = None
                        for i, ps in ((WQ, psA), (WK, psB)):
                            first = (ec == 0)
                            for qt in range(NQT):
                                sl = slice(qt * QT, (qt + 1) * QT)
                                tensor.matmul(ps[:, sl], wsl(wbuf_hi(i), ec),
                                              hstr[ec % 2][0][:, sl],
                                              start=first, stop=False)
                                tensor.matmul(ps[:, sl], wsl(wbuf_hi(i), ec),
                                              hstr[ec % 2][1][:, sl],
                                              start=False, stop=False)
                                ins = tensor.matmul(ps[:, sl], wsl(wbuf_lo(i), ec),
                                                    hstr[ec % 2][0][:, sl],
                                                    start=False,
                                                    stop=(ec == EC - 1))
                        tok[("hu1", p, ec)] = sem.inc(ins, "TE", 1)
                tok[("qkmm", p)] = tok[("hu1", p, EC - 1)]

                @block.scalar
                def _(scalar, p=p):
                    scalar.wait_ge(sem.h["TE"], tok[("qkmm", p)])
                    if p > 0:
                        scalar.wait_ge(sem.h["VE"], prev[("rope", p - 1)])
                    tok[("qev", p)] = sem.inc(
                        scalar.activation(q32[:], psA[:], ACT.Identity,
                                          bias=p_sb["bq"][:, p:p + 1]), "SC", 1)

                def rope_ops(vector, srct, dhi, dlo):
                    vector.stream_shuffle(s1[:], srct[:], shuf_mask)
                    vector.tensor_tensor(s1[:], s1[:], sinv, ALU.mult)
                    vector.tensor_tensor(srct[:], srct[:], cosv, ALU.mult)
                    vector.tensor_tensor(srct[:], srct[:], s1[:], ALU.add)
                    vector.tensor_copy(dhi[:], srct[:])
                    return vector.scalar_tensor_tensor(dlo[:], srct[:], 1.0,
                                                       dhi[:], ALU.mult,
                                                       ALU.subtract)

                @block.vector
                def _(vector, p=p):
                    vector.wait_ge(sem.h["SC"], tok[("qev", p)])
                    vector.wait_ge(sem.h["LCS"], tok[("cs", p)])
                    if p > 0:
                        vector.wait_ge(sem.h["TE"], prev[("scdone", p - 1)])
                    tok[("ropeq", p)] = sem.inc(rope_ops(vector, q32, qhi, qlo),
                                                "VE", 1)

                @block.scalar
                def _(scalar, p=p):
                    scalar.wait_ge(sem.h["VE"], tok[("ropeq", p)])
                    tok[("kev", p)] = sem.inc(
                        scalar.activation(q32[:], psB[:], ACT.Identity,
                                          bias=p_sb["bk"][:, p:p + 1]), "SC", 1)

                @block.vector
                def _(vector, p=p):
                    vector.wait_ge(sem.h["SC"], tok[("kev", p)])
                    tok[("rope", p)] = sem.inc(rope_ops(vector, q32, khi, klo),
                                               "VE", 1)
                    prev[("rope", p)] = tok[("rope", p)]

                for ec in range(EC):
                    @block.sync
                    def _(sync, p=p, ec=ec):
                        if ec >= 2:
                            sync.wait_ge(sem.h["TE"], tok[("hu2", p, ec - 2)])
                        else:
                            sync.wait_ge(sem.h["TE"], tok[("hu1", p, EC - 2 + ec)])
                        sem.inc(sync.dma_start(out=hstr[ec % 2][0][:],
                                               in_=h_sp[0, ec]),
                                "LH%d" % (ec % 2), 16)
                        tok[("hl2", p, ec)] = sem.inc(
                            sync.dma_start(out=hstr[ec % 2][1][:],
                                           in_=h_sp[1, ec]),
                            "LH%d" % (ec % 2), 16)

                    @block.tensor
                    def _(tensor, p=p, ec=ec):
                        tensor.wait_ge(sem.h["LH%d" % (ec % 2)],
                                       tok[("hl2", p, ec)])
                        if ec == 0:
                            tensor.wait_ge(sem.h["SC"], tok[("kev", p)])
                            tensor.wait_ge(sem.h["VE"], tok[("wsp", p, WV)])
                            tensor.wait_ge(sem.h["VE"], tok[("wsp", p, WG)])
                        ins = None
                        for i, ps in ((WV, psA), (WG, psB)):
                            first = (ec == 0)
                            for qt in range(NQT):
                                sl = slice(qt * QT, (qt + 1) * QT)
                                tensor.matmul(ps[:, sl], wsl(wbuf_hi(i), ec),
                                              hstr[ec % 2][0][:, sl],
                                              start=first, stop=False)
                                tensor.matmul(ps[:, sl], wsl(wbuf_hi(i), ec),
                                              hstr[ec % 2][1][:, sl],
                                              start=False, stop=False)
                                ins = tensor.matmul(ps[:, sl], wsl(wbuf_lo(i), ec),
                                                    hstr[ec % 2][0][:, sl],
                                                    start=False,
                                                    stop=(ec == EC - 1))
                        tok[("hu2", p, ec)] = sem.inc(ins, "TE", 1)
                        prev[("hu2", p, ec)] = tok[("hu2", p, ec)]
                tok[("vgmm", p)] = tok[("hu2", p, EC - 1)]
                prev[("projdone", p)] = tok[("vgmm", p)]

                @block.scalar
                def _(scalar, p=p):
                    scalar.wait_ge(sem.h["VE"], tok[("rope", p)])
                    if p > 0:
                        scalar.wait_ge(sem.h["VE"], prev[("gsq", p - 1)])
                    tok[("sqq", p)] = sem.inc(
                        scalar.activation(sqsc[:], qhi[:], ACT.Square), "SC", 1)

                @block.scalar
                def _(scalar, p=p):
                    scalar.wait_ge(sem.h["TE"], tok[("vgmm", p)])
                    scalar.wait_ge(sem.h["VE"], tok[("rope", p)])
                    tok[("vev", p)] = sem.inc(
                        scalar.activation(q32[:], psA[:], ACT.Identity,
                                          bias=p_sb["bv"][:, p:p + 1]), "SC", 1)
                    if p > 0:
                        scalar.wait_ge(sem.h["VE"], prev[("gonve", p - 1)])
                    scalar.activation(gateh[0][:], psB[0:64, :], ACT.Silu,
                                      bias=p_sb["bg"][0:64, p:p + 1])
                    tok[("gev", p)] = sem.inc(
                        scalar.activation(gateh[1][:], psB[64:128, :], ACT.Silu,
                                          bias=p_sb["bg"][64:128, p:p + 1]),
                        "SC", 1)

                if p + 1 < TAP_PAIRS:
                    emit_wchain(p + 1)

                @block.tensor
                def _(tensor, p=p):
                    tensor.wait_ge(sem.h["SC"], tok[("vev", p)])
                    if p > 0:
                        tensor.wait_ge(sem.h["VE"], prev[("vscd", p - 1)])
                    ins = None
                    for c in range(KC):
                        ins = tensor.matmul(psA[:, c * 128:(c + 1) * 128],
                                            q32[:, c * 128:(c + 1) * 128],
                                            c_ident[:], start=(c % 4 == 0),
                                            stop=(c % 4 == 3),
                                            skip_group_check=True)
                    tok[("vt", p)] = sem.inc(ins, "TE", 1)

                @block.vector
                def _(vector, p=p):
                    vector.wait_ge(sem.h["TE"], tok[("vt", p)])
                    ins = None
                    for c in range(KC):
                        src = psA[:, c * 128:(c + 1) * 128]
                        vector.tensor_copy(vtm[0][:, c * 65:c * 65 + 64],
                                           src[:, 0:64])
                        ins = vector.tensor_copy(vtm[1][:, c * 65:c * 65 + 64],
                                                 src[:, 64:128])
                    tok[("vtcall", p)] = sem.inc(ins, "VE", 1)

                @block.vector
                def _(vector, p=p):
                    vector.memset(vtm[0][:, 64::65], 1.0)
                    vector.memset(vtm[1][:, 64::65], 1.0)
                    vector.wait_ge(sem.h["SC"], tok["eks"])
                    ins = None
                    for hh in range(2):
                        h = 2 * p + hh
                        for c in range(KC):
                            ins = vector.tensor_scalar(
                                vsc[hh][:, c * 65:(c + 1) * 65],
                                vtm[hh][:, c * 65:(c + 1) * 65],
                                eks[:, h * KC + c:h * KC + c + 1], None, ALU.mult)
                    tok[("vscd", p)] = sem.inc(ins, "VE", 1)
                    prev[("vscd", p)] = tok[("vscd", p)]

                @block.tensor
                def _(tensor, p=p):
                    tensor.wait_ge(sem.h["SC"], tok[("sqq", p)])
                    tensor.wait_ge(sem.h["SC"], tok[("gev", p)])
                    tensor.wait_ge(sem.h["VE"], tok[("vtcall", p)])
                    for qt in range(NQT):
                        sl = slice(qt * QT, (qt + 1) * QT)
                        tensor.matmul(psB[0:1, sl], ones16[0:64, 0:1],
                                      sqsc[0:64, sl], start=True, stop=True,
                                      skip_group_check=True)
                        ins = tensor.matmul(psA[0:1, sl], ones16[64:128, 0:1],
                                            sqsc[64:128, sl], start=True,
                                            stop=True, tile_position=(64, 0),
                                            skip_group_check=True)
                    tok[("nrmq", p)] = sem.inc(ins, "TE", 1)

                @block.vector
                def _(vector, p=p):
                    vector.wait_ge(sem.h["TE"], tok[("nrmq", p)])
                    vsync(vector, vector.tensor_reduce(browq[0:1, 0:1],
                                                       psB[0:1, :], AX.X, ALU.max))
                    tok[("nrq", p)] = vsync(
                        vector, vector.tensor_reduce(browq[0:1, 1:2], psA[0:1, :],
                                                     AX.X, ALU.max))

                @block.scalar
                def _(scalar, p=p):
                    scalar.wait_ge(sem.h["TE"], tok[("nrmq", p)])
                    tok[("sqk", p)] = sem.inc(
                        scalar.activation(sqsc[:], khi[:], ACT.Square), "SC", 1)

                @block.tensor
                def _(tensor, p=p):
                    tensor.wait_ge(sem.h["VE"], tok[("nrq", p)])
                    tensor.wait_ge(sem.h["SC"], tok[("sqk", p)])
                    for qt in range(NQT):
                        sl = slice(qt * QT, (qt + 1) * QT)
                        tensor.matmul(psB[0:1, sl], ones16[0:64, 0:1],
                                      sqsc[0:64, sl], start=True, stop=True,
                                      skip_group_check=True)
                        ins = tensor.matmul(psA[0:1, sl], ones16[64:128, 0:1],
                                            sqsc[64:128, sl], start=True,
                                            stop=True, tile_position=(64, 0),
                                            skip_group_check=True)
                    tok[("nrmk", p)] = sem.inc(ins, "TE", 1)

                @block.vector
                def _(vector, p=p):
                    vector.wait_ge(sem.h["TE"], tok[("nrmk", p)])
                    vsync(vector, vector.tensor_reduce(browk[0:1, 0:1],
                                                       psB[0:1, :], AX.X, ALU.max))
                    vsync(vector, vector.tensor_reduce(browk[0:1, 1:2],
                                                       psA[0:1, :], AX.X, ALU.max))
                    tok[("nred", p)] = vpub(
                        vector,
                        vector.tensor_tensor(browq[:], browq[:], browk[:],
                                             ALU.mult))

                @block.scalar
                def _(scalar, p=p):
                    scalar.wait_ge(sem.h["VE"], tok[("nred", p)])
                    tok[("brow", p)] = spub(
                        scalar, scalar.activation(browq[:], browq[:], ACT.Sqrt,
                                                  scale=0.0629))

                if tap == "all":
                    @block.sync
                    def _(sync, p=p):
                        sync.wait_ge(sem.h["SC"], tok[("brow", p)])
                        sem.inc(sync.dma_start(out=dbg_b[p, 0:1, :],
                                               in_=browq[:]), "STY", 16)
                        sem.inc(sync.dma_start(out=dbg_b[p, 1:2, :],
                                               in_=browk[:]), "STY", 16)

                @block.gpsimd
                def _(gpsimd, p=p):
                    gpsimd.wait_ge(sem.h["SC"], tok[("brow", p)])
                    if tap == "all":
                        gpsimd.wait_ge(sem.h["STY"], sem.cur("STY"))
                    gpsimd.partition_broadcast(bbc[:, 0:1], browq[0:1, 0:1], 128)
                    tok[("bbc", p)] = sem.inc(
                        gpsimd.partition_broadcast(bbc[:, 1:2], browq[0:1, 1:2],
                                                   128), "GP", 1)

                @block.vector
                def _(vector, p=p):
                    vector.wait_ge(sem.h["GP"], tok[("bbc", p)])
                    vector.wait_ge(sem.h["VE"], tok["ldks"])
                    ins = None
                    for hh in range(2):
                        h = 2 * p + hh
                        vector.tensor_scalar(bfh[:, hh * KC:(hh + 1) * KC],
                                             ldks[:, h * KC:(h + 1) * KC],
                                             bbc[:, hh:hh + 1], None, ALU.subtract)
                        vector.tensor_scalar(ldnb[:, hh:hh + 1],
                                             ldb[:, h:h + 1], -1.0, None,
                                             ALU.mult)
                        ins = vector.tensor_scalar(lqrow[0:1, hh * 4:hh * 4 + 4],
                                                   c_kramp[0:1, 0:KC:4],
                                                   ldnrow[0:1, h:h + 1], None,
                                                   ALU.mult)
                    tok[("bfh", p)] = vpub(vector, ins)

            with nc.psum_tensor(f"scp0_{p}", [128, 2 * QT], F32) as scp0, \
                 nc.psum_tensor(f"scp1_{p}", [128, 2 * QT], F32) as scp1, \
                 nc.psum_tensor(f"opA_{p}", [65, QT], F32) as opA, \
                 nc.psum_tensor(f"ofA_{p}", [65, QT], F32) as ofA, \
                 nc.psum_tensor(f"opB_{p}", [65, QT], F32) as opB, \
                 nc.psum_tensor(f"ofB_{p}", [65, QT], F32) as ofB, \
                 nc.sbuf_tensor(f"on32_{p}", [64, T], F32) as on32, \
                 nc.sbuf_tensor(f"gonh_{p}", [64, T], F16) as gonhi, \
                 nc.sbuf_tensor(f"gonl_{p}", [64, T], F16) as gonlo, \
                 nc.sbuf_tensor(f"gsum_{p}", [64, 4], F32) as gsum:

                scp = [scp0, scp1]
                ophh = [opA, opB]
                ofhh = [ofA, ofB]

                for qt in range(1 if tap == "pb0" else NQT):
                    pv_state = {}
                    for c in range(KC):
                        @block.tensor
                        def _(tensor, p=p, qt=qt, c=c):
                            g = qt * KC + c
                            if g >= 2:
                                gg = g - 2
                                tensor.wait_ge(
                                    sem.h["SC"],
                                    tok[("exp", p, gg // KC, gg % KC)])
                            if g == 0:
                                tensor.wait_ge(sem.h["VE"], tok[("rope", p)])
                                tensor.wait_ge(sem.h["VE"], tok[("nred", p)])
                            gsl = slice(qt * QT, (qt + 1) * QT)
                            par = c % 2
                            ksl = slice(c * 128, (c + 1) * 128)
                            ins = None
                            for hh in range(2):
                                ph = slice(hh * 64, hh * 64 + 64)
                                tp = (hh * 64, 0)
                                dst = scp[par][:, hh * QT:hh * QT + QT]
                                tensor.matmul(dst, khi[ph, ksl], qhi[ph, gsl],
                                              start=True, stop=False,
                                              tile_position=tp,
                                              skip_group_check=True)
                                tensor.matmul(dst, khi[ph, ksl], qlo[ph, gsl],
                                              start=False, stop=False,
                                              tile_position=tp,
                                              skip_group_check=True)
                                ins = tensor.matmul(dst, klo[ph, ksl],
                                                    qhi[ph, gsl],
                                                    start=False, stop=True,
                                                    tile_position=tp,
                                                    skip_group_check=True)
                            tok[("sc", p, qt, c)] = sem.inc(ins, "TE", 1)

                        @block.scalar
                        def _(scalar, p=p, qt=qt, c=c):
                            scalar.wait_ge(sem.h["TE"], tok[("sc", p, qt, c)])
                            if qt == 0 and c == 0:
                                scalar.wait_ge(sem.h["VE"], tok[("bfh", p)])
                                if p > 0:
                                    scalar.wait_ge(sem.h["TE"],
                                                   prev[("pvdone", p - 1)])
                            par = c % 2
                            ins = None
                            for hh in range(2):
                                src = scp[par][:, hh * QT:hh * QT + QT]
                                dst = pbuf[hh][:, c * QT:(c + 1) * QT]
                                ins = scalar.activation(
                                    dst, src, ACT.Exp, scale=0.125,
                                    bias=bfh[:, hh * KC + c:hh * KC + c + 1])
                            tok[("exp", p, qt, c)] = sem.inc(ins, "SC", 1)

                        def emit_strip(c, p=p, qt=qt):
                            @block.vector
                            def _(vector, p=p, qt=qt, c=c):
                                j = c - 4 * qt
                                vector.wait_ge(sem.h["SC"],
                                               tok[("exp", p, qt, c)])
                                if j >= 2:
                                    vector.wait_ge(sem.h["TE"],
                                                   tok[("pvc", p, qt, c - 2)])
                                ins = None
                                for hh in range(2):
                                    pb = pbuf[hh][:, c * QT + j * 128:
                                                  c * QT + j * 128 + 128]
                                    ssl = slice((j % 2) * 256 + hh * 128,
                                                (j % 2) * 256 + hh * 128 + 128)
                                    vector.tensor_tensor(stripL[:, ssl], pb,
                                                         c_L[:], ALU.mult)
                                    ins = vector.tensor_tensor(stripU[:, ssl], pb,
                                                               c_U[:], ALU.mult)
                                tok[("strip", p, qt, c)] = sem.inc(ins, "VE", 1)

                        if c // 4 == qt and not SERIAL_ATTN:
                            emit_strip(c)

                        def emit_pv(cc, p=p, qt=qt, pv_state=pv_state):
                            @block.tensor
                            def _(tensor, p=p, qt=qt, cc=cc):
                                tensor.wait_ge(sem.h["SC"],
                                               tok[("exp", p, qt, cc)])
                                if cc == 0:
                                    tensor.wait_ge(sem.h["VE"], tok[("vscd", p)])
                                    if qt > 0:
                                        tensor.wait_ge(sem.h["VE"],
                                                       tok[("comb", p, qt - 1)])
                                strad = (cc // 4 == qt)
                                if strad:
                                    tensor.wait_ge(sem.h["VE"],
                                                   tok[("strip", p, qt, cc)])
                                ins = None
                                for hh in range(2):
                                    vt = vtm[hh][:, cc * 65:(cc + 1) * 65]
                                    vs = vsc[hh][:, cc * 65:(cc + 1) * 65]
                                    pb = pbuf[hh][:, cc * QT:(cc + 1) * QT]
                                    op = ophh[hh]
                                    of = ofhh[hh]
                                    if cc < 4 * qt:
                                        ins = _mm(tensor, op[:, :], vs, pb,
                                                  ("op", hh), pv_state)
                                    elif cc >= 4 * (qt + 1):
                                        ins = _mm(tensor, of[:, :], vt, pb,
                                                  ("of", hh), pv_state,
                                                  stop=(cc == KC - 1))
                                    else:
                                        j = cc - 4 * qt
                                        w_f = 128 * j
                                        w_p0 = w_f + 128
                                        if w_f > 0:
                                            _mm(tensor, of[:, 0:w_f], vt,
                                                pb[:, 0:w_f], ("of", hh), pv_state)
                                        if QT - w_p0 > 0:
                                            _mm(tensor, op[:, w_p0:QT], vs,
                                                pb[:, w_p0:QT], ("op", hh),
                                                pv_state)
                                        ssl = slice((j % 2) * 256 + hh * 128,
                                                    (j % 2) * 256 + hh * 128 + 128)
                                        _mm(tensor, op[:, w_f:w_p0], vs,
                                            stripL[:, ssl], ("op", hh), pv_state,
                                            stop=True)
                                        ins = _mm(tensor, of[:, w_f:w_p0], vt,
                                                  stripU[:, ssl], ("of", hh),
                                                  pv_state, stop=(cc == KC - 1))
                                tok[("pvc", p, qt, cc)] = sem.inc(ins, "TE", 1)
                                if cc == KC - 1:
                                    tok[("pv", p, qt)] = tok[("pvc", p, qt, cc)]

                        if not SERIAL_ATTN:
                            if c >= 1:
                                emit_pv(c - 1)
                            if c == KC - 1:
                                emit_pv(c)
                    if SERIAL_ATTN:
                        for c in range(KC):
                            if c // 4 == qt:
                                emit_strip(c)
                            emit_pv(c)

                    for hh in range(2):
                        @block.vector
                        def _(vector, p=p, qt=qt, hh=hh):
                            gsl = slice(qt * QT, (qt + 1) * QT)
                            vector.wait_ge(sem.h["VE"], tok[("bfh", p)])
                            tok[("dr", p, qt, hh)] = sem.inc(
                                vector.tensor_scalar(
                                    r1t[:], c_qramp[:],
                                    ldnrow[0:1, 2 * p + hh:2 * p + hh + 1],
                                    lqrow[0:1, hh * 4 + qt:hh * 4 + qt + 1],
                                    ALU.mult, ALU.add), "VE", 1)

                        @block.scalar
                        def _(scalar, p=p, qt=qt, hh=hh):
                            scalar.wait_ge(sem.h["VE"], tok[("dr", p, qt, hh)])
                            tok[("dre", p, qt, hh)] = sem.inc(
                                scalar.activation(r1t[:], r1t[:], ACT.Exp),
                                "SC", 1)

                        @block.gpsimd
                        def _(gpsimd, p=p, qt=qt, hh=hh):
                            gpsimd.wait_ge(sem.h["SC"], tok[("dre", p, qt, hh)])
                            if hh == 1:
                                gpsimd.wait_ge(sem.h["VE"],
                                               tok[("linv", p, qt, 0)])
                            tok[("dbb", p, qt, hh)] = sem.inc(
                                gpsimd.partition_broadcast(dbb1[hh][:], r1t[:],
                                                           65), "GP", 1)

                        @block.vector
                        def _(vector, p=p, qt=qt, hh=hh):
                            vector.wait_ge(sem.h["TE"], tok[("pv", p, qt)])
                            vector.wait_ge(sem.h["GP"], tok[("dbb", p, qt, hh)])
                            vector.tensor_tensor(csc1[:], ofhh[hh][:],
                                                 dbb1[hh][:], ALU.mult)
                            vector.scalar_tensor_tensor(csc1[:], ophh[hh][:],
                                                        1.0, csc1[:], ALU.mult,
                                                        ALU.add)
                            tok[("linv", p, qt, hh)] = vpub(
                                vector, vector.reciprocal(r1t[:], csc1[64:65, :]))

                        if tap == "cs0" and qt == 0 and hh == 0:
                            @block.sync
                            def _(sync, p=p, qt=qt, hh=hh):
                                sync.wait_ge(sem.h["VE"], tok[("linv", p, qt, hh)])
                                sem.inc(sync.dma_start(out=dbg_cs[0:65, :],
                                                       in_=csc1[:]), "STY", 16)
                                sem.inc(sync.dma_start(out=dbg_cs[65:66, :],
                                                       in_=r1t[:]), "STY", 16)

                        @block.gpsimd
                        def _(gpsimd, p=p, qt=qt, hh=hh):
                            gpsimd.wait_ge(sem.h["VE"], tok[("linv", p, qt, hh)])
                            if tap == "cs0" and qt == 0 and hh == 0:
                                gpsimd.wait_ge(sem.h["STY"], sem.cur("STY"))
                            tok[("lbc", p, qt, hh)] = sem.inc(
                                gpsimd.partition_broadcast(linvb[:], r1t[:], 64),
                                "GP", 1)

                        @block.vector
                        def _(vector, p=p, qt=qt, hh=hh):
                            gsl = slice(qt * QT, (qt + 1) * QT)
                            vector.wait_ge(sem.h["GP"], tok[("lbc", p, qt, hh)])
                            tok[("comb", p, qt)] = sem.inc(
                                vector.tensor_tensor(ogt[hh][:, gsl],
                                                     csc1[0:64, :], linvb[:],
                                                     ALU.mult), "VE", 1)

                qq = 0 if tap == "pb0" else NQT - 1
                prev[("pvdone", p)] = tok[("pv", p, qq)]
                prev[("scdone", p)] = tok[("sc", p, qq, KC - 1)]
                prev[("combdone", p)] = tok[("comb", p, qq)]
                prev[("expdone", p)] = tok[("exp", p, qq, KC - 1)]

                if tap in ("pb", "pb0"):
                    @block.sync
                    def _(sync, p=p):
                        qq = 0 if tap == "pb0" else NQT - 1
                        sync.wait_ge(sem.h["SC"], tok[("exp", p, qq, KC - 1)])
                        sem.inc(sync.dma_start(out=dbg_pb[0], in_=pbuf[0]),
                                "STY", 16)
                        sem.inc(sync.dma_start(out=dbg_pb[1], in_=pbuf[1]),
                                "STY", 16)
                        sync.wait_ge(sem.h["STY"], sem.cur("STY"))
                    tap_now[0] = tap
                    break

                if tap in ("og", "cs0"):
                    @block.sync
                    def _(sync, p=p):
                        sync.wait_ge(sem.h["VE"], tok[("comb", p, NQT - 1)])
                        if tap == "og":
                            sem.inc(sync.dma_start(out=dbg_og[0], in_=ogt[0][:]),
                                    "STY", 16)
                            sem.inc(sync.dma_start(out=dbg_og[1], in_=ogt[1][:]),
                                    "STY", 16)
                        sync.wait_ge(sem.h["STY"], sem.cur("STY"))
                    tap_now[0] = tap
                    break

                if tap == "all":
                    @block.sync
                    def _(sync, p=p):
                        sync.wait_ge(sem.h["VE"], tok[("comb", p, NQT - 1)])
                        for hh in range(2):
                            sem.inc(sync.dma_start(out=dbg_ogall[p, hh],
                                                   in_=ogt[hh][:]), "STY", 16)
                            sem.inc(sync.dma_start(out=dbg_gate[p, hh],
                                                   in_=gateh[hh][:]), "STY", 16)

                for hh in range(2):
                    @block.vector
                    def _(vector, p=p, hh=hh):
                        vector.wait_ge(sem.h["VE"], tok[("comb", p, NQT - 1)])
                        tok[("gsum", p, hh)] = vsync(
                            vector, vector.reduce_sum(gsum[:, hh:hh + 1],
                                                      ogt[hh][:], AX.X))
                        vector.tensor_tensor(sqsc[0:64, :], ogt[hh][:],
                                             ogt[hh][:], ALU.mult)
                        tok[("gsq", p, hh)] = (vpub if hh == 1 else vsync)(
                            vector, vector.reduce_sum(gsum[:, 2 + hh:3 + hh],
                                                      sqsc[0:64, :], AX.X))
                prev[("gsq", p)] = tok[("gsq", p, 1)]

                @block.gpsimd
                def _(gpsimd, p=p):
                    gpsimd.wait_ge(sem.h["VE"], tok[("gsq", p, 1)])
                    gpsimd.wait_ge(sem.h["VE"], tok[("gsq", p, 0)])
                    tok[("gmm", p)] = sem.inc(
                        gpsimd.partition_all_reduce(gsum[:], gsum[:], 64,
                                                    ReduceOp.add), "GP", 1)

                @block.vector
                def _(vector, p=p):
                    vector.wait_ge(sem.h["GP"], tok[("gmm", p)])
                    NTOT = float(T * 64)
                    vsync(vector,
                          vector.tensor_scalar(r2t[0:1, 0:4], gsum[0:1, 0:4],
                                               1.0 / NTOT, None, ALU.mult))
                    vsync(vector,
                          vector.tensor_tensor(r2t[0:1, 4:6], r2t[0:1, 0:2],
                                               r2t[0:1, 0:2], ALU.mult))
                    tok[("gvar", p)] = vpub(
                        vector, vector.tensor_tensor(r2t[0:1, 2:4],
                                                     r2t[0:1, 2:4],
                                                     r2t[0:1, 4:6],
                                                     ALU.subtract))


                @block.scalar
                def _(scalar, p=p):
                    scalar.wait_ge(sem.h["VE"], tok[("gvar", p)])
                    tok[("gsqrt", p)] = spub(
                        scalar,
                        scalar.activation(r2t[0:1, 2:4], r2t[0:1, 2:4], ACT.Sqrt,
                                          bias=ctb[0:1, 0:1]))

                @block.vector
                def _(vector, p=p):
                    vector.wait_ge(sem.h["SC"], tok[("gsqrt", p)])
                    tok[("grstd", p)] = vpub(
                        vector, vector.reciprocal(r2t[0:1, 6:8], r2t[0:1, 2:4]))

                if tap == "all":
                    @block.sync
                    def _(sync, p=p):
                        sync.wait_ge(sem.h["VE"], tok[("grstd", p)])
                        sem.inc(sync.dma_start(out=dbg_r2[p], in_=r2t[:]),
                                "STY", 16)

                @block.gpsimd
                def _(gpsimd, p=p):
                    gpsimd.wait_ge(sem.h["VE"], tok[("grstd", p)])
                    ins = None
                    for hh in range(2):
                        gpsimd.partition_broadcast(gnb[:, hh:hh + 1],
                                                   r2t[0:1, hh:hh + 1], 64)
                        ins = gpsimd.partition_broadcast(gnb[:, 2 + hh:3 + hh],
                                                         r2t[0:1, 6 + hh:7 + hh],
                                                         64)
                    tok[("gnb", p)] = sem.inc(ins, "GP", 1)

                for hh in range(2):
                    @block.vector
                    def _(vector, p=p, hh=hh):
                        vector.wait_ge(sem.h["GP"], tok[("gnb", p)])
                        if hh == 0:
                            vector.wait_ge(sem.h["SC"], tok[("gev", p)])
                            if p > 0:
                                vector.wait_ge(sem.h["STG"],
                                               prev[("gonst", p - 1)])
                        else:
                            vector.wait_ge(sem.h["STG"], tok[("gonst", p, 0)])
                        ph = slice(hh * 64, hh * 64 + 64)
                        vsync(vector, vector.tensor_copy(
                            gnc[:, 0:1], p_sb["gamma"][ph, p:p + 1]))
                        vsync(vector, vector.tensor_copy(
                            gnc[:, 1:2], p_sb["beta"][ph, p:p + 1]))
                        vsync(vector, vector.tensor_tensor(
                            gnc[:, 2:3], gnc[:, 0:1],
                            gnb[:, 2 + hh:3 + hh], ALU.mult))
                        vsync(vector, vector.scalar_tensor_tensor(
                            gnc[:, 3:4], gnb[:, hh:hh + 1], 1.0,
                            gnc[:, 2:3], ALU.mult, ALU.mult))
                        vsync(vector, vector.tensor_tensor(
                            gnc[:, 3:4], gnc[:, 3:4],
                            gnc[:, 1:2], ALU.subtract))
                        vector.tensor_scalar(on32[:], ogt[hh][:], gnc[:, 2:3],
                                             gnc[:, 3:4], ALU.mult, ALU.subtract)
                        vector.tensor_tensor(on32[:], on32[:], gateh[hh][:],
                                             ALU.mult)
                        vector.tensor_copy(gonhi[:], on32[:])
                        tok[("gon", p, hh)] = sem.inc(
                            vector.scalar_tensor_tensor(gonlo[:], on32[:], 1.0,
                                                        gonhi[:], ALU.mult,
                                                        ALU.subtract), "VE", 1)

                    @block.gpsimd
                    def _(gpsimd, p=p, hh=hh):
                        gpsimd.wait_ge(sem.h["VE"], tok[("gon", p, hh)])
                        ph = slice(hh * 64, hh * 64 + 64)
                        sem.inc(gpsimd.dma_start(out=gon_sp[0, p, ph],
                                                 in_=gonhi[:]), "STG", 16)
                        tok[("gonst", p, hh)] = sem.inc(
                            gpsimd.dma_start(out=gon_sp[1, p, ph], in_=gonlo[:]),
                            "STG", 16)
                if tap == "all":
                    @block.sync
                    def _(sync, p=p):
                        sync.wait_ge(sem.h["STY"], sem.cur("STY"))
                prev[("gonve", p)] = tok[("gon", p, 1)]
                prev[("gonst", p)] = tok[("gonst", p, 1)]

        if tap == "qk":
            @block.sync
            def _(sync):
                sync.wait_ge(sem.h["VE"], tok[("rope", 0)])
                sem.inc(sync.dma_start(out=dbg_qk[0], in_=qhi[:]), "STY", 16)
                sem.inc(sync.dma_start(out=dbg_qk[1], in_=qlo[:]), "STY", 16)
                sem.inc(sync.dma_start(out=dbg_qk[2], in_=khi[:]), "STY", 16)
                sem.inc(sync.dma_start(out=dbg_qk[3], in_=klo[:]), "STY", 16)
                sync.wait_ge(sem.h["STY"], sem.cur("STY"))
            lower_extended_insts(nc)
            return nc

        if tap in ("og", "gon", "pb", "pb0", "cs0"):
            @block.sync
            def _(sync):
                for s in ["STH", "STG", "STY"]:
                    sync.wait_ge(sem.h[s], sem.cur(s))
                sync.wait_ge(sem.h["SC"], sem.cur("SC"))
                sync.wait_ge(sem.h["VE"], sem.cur("VE"))
            lower_extended_insts(nc)
            return nc

        stB.close()

        _k5 = "ExternalOutput" if tap in ("all", "allq") else "Internal"
        xr_sp = nc.dram_tensor("xr_sp", [2, EC, 128, T], F16, kind=_k5).ap()
        xk_sp = nc.dram_tensor("xk_sp", [2, EC, 128, T], F16, kind=_k5).ap()

        with contextlib.ExitStack() as s3:
            sb3 = lambda name, shape, dt=F32: s3.enter_context(
                nc.sbuf_tensor(name, shape, dt))
            gstr = [[sb3(f"gstr{i}_{d}", [128, T], F16) for d in range(2)]
                    for i in range(2)]
            w3 = [sb3(f"w3st{i}", [128, EC * 128]) for i in range(2)]
            w3hi = [sb3(f"w3hi{i}", [128, EC * 128], F16) for i in range(2)]
            w3lo = [sb3(f"w3lo{i}", [128, EC * 128], F16) for i in range(2)]
            xb3 = [sb3(f"xb3_{i}", [128, T]) for i in range(2)]
            x1c = sb3("x1c", [128, T])
            x1hi = sb3("x1hi", [128, T], F16)
            x1lo = sb3("x1lo", [128, T], F16)
            psO = s3.enter_context(nc.psum_tensor("psO", [128, T], F32))
            psO2 = s3.enter_context(nc.psum_tensor("psO2", [128, T], F32))
            psOs = [psO, psO2]

            @block.sync
            def _(sync):
                sync.wait_ge(sem.h["STG"], sem.cur("STG"))
                sync.wait_ge(sem.h["VE"], sem.cur("VE"))
                sync.wait_ge(sem.h["SC"], sem.cur("SC"))
                sync.wait_ge(sem.h["TE"], sem.cur("TE"))
                sync.wait_ge(sem.h["GP"], sem.cur("GP"))
                sem.inc(sync.dma_start(out=w3[0][:], in_=wblk["o"][0]),
                        "LW0", 16)
                tok[("w3", 0)] = sem.cur("LW0")

            def wchain(gi, nxt_src, fmm_key):
                sl = gi % 2

                @block.scalar
                def _(scalar, gi=gi, sl=sl):
                    scalar.wait_ge(sem.h["LW%d" % sl], tok[("w3", gi)])
                    if gi >= 2:
                        scalar.wait_ge(sem.h["TE"], tok[(fmm_key, gi - 2)])
                    tok[("w3hi", gi)] = sem.inc(
                        scalar.activation(w3hi[sl][:], w3[sl][:], ACT.Copy),
                        "SC", 1)

                @block.vector
                def _(vector, gi=gi, sl=sl):
                    vector.wait_ge(sem.h["SC"], tok[("w3hi", gi)])
                    tok[("w3sp", gi)] = sem.inc(
                        vector.scalar_tensor_tensor(
                            w3lo[sl][:], w3[sl][:], 1.0, w3hi[sl][:],
                            ALU.mult, ALU.subtract), "VE", 1)

                @block.sync
                def _(sync, gi=gi, nxt_src=nxt_src):
                    if nxt_src is not None:
                        osl = (gi + 1) % 2
                        if gi >= 1:
                            sync.wait_ge(sem.h["SC"], tok[("w3hi", gi)])
                        if ("w3sp", gi - 1) in tok:
                            sync.wait_ge(sem.h["VE"], tok[("w3sp", gi - 1)])
                        sem.inc(sync.dma_start(out=w3[osl][:], in_=nxt_src),
                                "LW%d" % osl, 16)
                        tok[("w3", gi + 1)] = sem.cur("LW%d" % osl)

            def emit_x1(dc):
                @block.sync
                def _(sync, dc=dc):
                    if dc >= 2:
                        sync.wait_ge(sem.h["VE"], tok[("x1c", dc - 2)])
                    tok[("x3", dc)] = sem.inc(
                        sync.dma_start(out=xb3[dc % 2][:],
                                       in_=x_fm[dc * 128:(dc + 1) * 128, :]),
                        "LX%d" % (dc % 2), 16)

                @block.vector
                def _(vector, dc=dc):
                    vector.wait_ge(sem.h["TE"], tok[("p3mm", dc)])
                    vector.wait_ge(sem.h["LX%d" % (dc % 2)], tok[("x3", dc)])
                    if dc >= 1:
                        vector.wait_ge(sem.h["STX"], tok[("x1sp", dc - 1)])
                    vector.tensor_scalar(x1c[:], psOs[dc % 2][:],
                                         p_sb["bo"][:, dc:dc + 1], None, ALU.add)
                    tok[("x1c", dc)] = sem.inc(
                        vector.tensor_tensor(x1c[:], x1c[:], xb3[dc % 2][:],
                                             ALU.add), "VE", 1)

                @block.scalar
                def _(scalar, dc=dc):
                    scalar.wait_ge(sem.h["VE"], tok[("x1c", dc)])
                    tok[("x1hi", dc)] = sem.inc(
                        scalar.activation(x1hi[:], x1c[:], ACT.Copy), "SC", 1)

                @block.vector
                def _(vector, dc=dc):
                    vector.wait_ge(sem.h["SC"], tok[("x1hi", dc)])
                    tok[("x1ev", dc)] = sem.inc(
                        vector.scalar_tensor_tensor(
                            x1lo[:], x1c[:], 1.0, x1hi[:], ALU.mult,
                            ALU.subtract), "VE", 1)

                @block.sync
                def _(sync, dc=dc):
                    sync.wait_ge(sem.h["VE"], tok[("x1ev", dc)])
                    sem.inc(sync.dma_start(out=x1_sp[0, dc], in_=x1hi[:]),
                            "STX", 16)
                    tok[("x1sp", dc)] = sem.inc(
                        sync.dma_start(out=x1_sp[1, dc], in_=x1lo[:]), "STX", 16)

            for dp in range(EC // 2):
                g0, g1 = 2 * dp, 2 * dp + 1
                wchain(g0, wblk["o"][g1], "p3mm")
                wchain(g1, wblk["o"][g1 + 1] if g1 < EC - 1 else None, "p3mm")

                for ec in range(EC):
                    @block.sync
                    def _(sync, dp=dp, ec=ec):
                        if ec >= 2:
                            sync.wait_ge(sem.h["TE"],
                                         tok[("p3u2", dp, ec - 2)])
                        elif dp > 0:
                            sync.wait_ge(sem.h["TE"],
                                         tok[("p3u2", dp - 1, EC - 2 + ec)])
                        sem.inc(sync.dma_start(out=gstr[ec % 2][0][:],
                                               in_=gon_sp[0, ec]),
                                "LH%d" % (ec % 2), 16)
                        tok[("g3", dp, ec)] = sem.inc(
                            sync.dma_start(out=gstr[ec % 2][1][:],
                                           in_=gon_sp[1, ec]),
                            "LH%d" % (ec % 2), 16)

                    @block.tensor
                    def _(tensor, dp=dp, ec=ec):
                        g0, g1 = 2 * dp, 2 * dp + 1
                        tensor.wait_ge(sem.h["LH%d" % (ec % 2)],
                                       tok[("g3", dp, ec)])
                        if ec == 0:
                            tensor.wait_ge(sem.h["VE"], tok[("w3sp", g0)])
                            tensor.wait_ge(sem.h["VE"], tok[("w3sp", g1)])
                            for gg in (g0 - 2, g1 - 2):
                                if gg >= 0:
                                    tensor.wait_ge(sem.h["VE"],
                                                   tok[("x1ev", gg)])
                        first = (ec == 0)
                        ins = None
                        for dc in (g0, g1):
                            ps = psOs[dc % 2]
                            whi_ = w3hi[dc % 2]
                            wlo_ = w3lo[dc % 2]
                            for qt in range(NQT):
                                sl = slice(qt * QT, (qt + 1) * QT)
                                tensor.matmul(ps[:, sl], wsl(whi_, ec),
                                              gstr[ec % 2][0][:, sl],
                                              start=first, stop=False)
                                tensor.matmul(ps[:, sl], wsl(whi_, ec),
                                              gstr[ec % 2][1][:, sl],
                                              start=False, stop=False)
                                ins = tensor.matmul(ps[:, sl], wsl(wlo_, ec),
                                                    gstr[ec % 2][0][:, sl],
                                                    start=False,
                                                    stop=(ec == EC - 1))
                        tok[("p3u2", dp, ec)] = sem.inc(ins, "TE", 1)
                tok[("p3mm", g0)] = tok[("p3u2", dp, EC - 1)]
                tok[("p3mm", g1)] = tok[("p3u2", dp, EC - 1)]
                emit_x1(g0)
                emit_x1(g1)

        if tap == "x1":
            @block.sync
            def _(sync):
                sync.wait_ge(sem.h["STX"], sem.cur("STX"))
            lower_extended_insts(nc)
            return nc

        with contextlib.ExitStack() as s4:
            sb4 = lambda name, shape, dt=F32: s4.enter_context(
                nc.sbuf_tensor(name, shape, dt))
            xs4 = [[sb4(f"xs4_{i}_{d}", [128, T], F16) for d in range(2)]
                   for i in range(2)]
            t4 = [sb4(f"t4_{i}", [128, T]) for i in range(2)]
            sq4 = sb4("sq4", [128, T])
            m2b_ = sb4("m2b", [128, T])
            rstd2b_ = sb4("rstd2b", [128, T])

            with nc.sbuf_tensor("m2row", [1, T], F32) as m2row, \
                 nc.sbuf_tensor("rstd2row", [1, T], F32) as rstd2row, \
                 nc.psum_tensor("psS2", [1, T], F32) as psS2, \
                 nc.psum_tensor("psQ2", [1, T], F32) as psQ2:

                for ec in range(EC):
                    @block.sync
                    def _(sync, ec=ec):
                        if ec >= 2:
                            sync.wait_ge(sem.h["TE"], tok[("s4te", ec - 2)])
                        if ec == 0:
                            sync.wait_ge(sem.h["STX"], tok[("x1sp", EC - 1)])
                        sem.inc(sync.dma_start(out=xs4[ec % 2][0][:],
                                               in_=x1_sp[0, ec]),
                                "LH%d" % (ec % 2), 16)
                        tok[("x4", ec)] = sem.inc(
                            sync.dma_start(out=xs4[ec % 2][1][:],
                                           in_=x1_sp[1, ec]),
                            "LH%d" % (ec % 2), 16)

                    @block.vector
                    def _(vector, ec=ec):
                        vector.wait_ge(sem.h["LH%d" % (ec % 2)], tok[("x4", ec)])
                        if ec >= 2:
                            vector.wait_ge(sem.h["TE"], tok[("s4te", ec - 2)])
                        tok[("t4", ec)] = sem.inc(
                            vector.tensor_tensor(t4[ec % 2][:], xs4[ec % 2][0][:],
                                                 xs4[ec % 2][1][:], ALU.add),
                            "VE", 1)

                    @block.scalar
                    def _(scalar, ec=ec):
                        scalar.wait_ge(sem.h["VE"], tok[("t4", ec)])
                        if ec >= 1:
                            scalar.wait_ge(sem.h["TE"], tok[("s4te", ec - 1)])
                        tok[("sq4", ec)] = sem.inc(
                            scalar.activation(sq4[:], t4[ec % 2][:], ACT.Square),
                            "SC", 1)

                    @block.tensor
                    def _(tensor, ec=ec):
                        tensor.wait_ge(sem.h["VE"], tok[("t4", ec)])
                        for qt in range(NQT):
                            tensor.matmul(psS2[0:1, qt * QT:(qt + 1) * QT],
                                          c_ones[:],
                                          t4[ec % 2][:, qt * QT:(qt + 1) * QT],
                                          start=(ec == 0), stop=(ec == EC - 1))
                        tensor.wait_ge(sem.h["SC"], tok[("sq4", ec)])
                        ins = None
                        for qt in range(NQT):
                            ins = tensor.matmul(
                                psQ2[0:1, qt * QT:(qt + 1) * QT], c_ones[:],
                                sq4[:, qt * QT:(qt + 1) * QT],
                                start=(ec == 0), stop=(ec == EC - 1))
                        tok[("s4te", ec)] = sem.inc(ins, "TE", 1)

                @block.vector
                def _(vector):
                    vector.wait_ge(sem.h["TE"], tok[("s4te", EC - 1)])
                    vector.tensor_scalar(m2row[:], psS2[0:1, :], 1.0 / E, None,
                                         ALU.mult)
                    vector.tensor_tensor(rstd2row[:], m2row[:], m2row[:],
                                         ALU.mult)
                    tok["var2"] = sem.inc(
                        vector.scalar_tensor_tensor(rstd2row[:], psQ2[0:1, :],
                                                    1.0 / E, rstd2row[:],
                                                    ALU.mult, ALU.subtract),
                        "VE", 1)

                @block.scalar
                def _(scalar):
                    scalar.wait_ge(sem.h["VE"], tok["var2"])
                    tok["sqrt2"] = sem.inc(
                        scalar.activation(rstd2row[:], rstd2row[:], ACT.Sqrt,
                                          bias=ctb[0:1, 0:1]), "SC", 1)

                @block.vector
                def _(vector):
                    vector.wait_ge(sem.h["SC"], tok["sqrt2"])
                    tok["rstd2"] = vpub(
                        vector, vector.reciprocal(rstd2row[:], rstd2row[:]))

                @block.gpsimd
                def _(gpsimd):
                    gpsimd.wait_ge(sem.h["VE"], tok["rstd2"])
                    gpsimd.partition_broadcast(m2b_[:], m2row[:], 128)
                    tok["bcast2"] = sem.inc(
                        gpsimd.partition_broadcast(rstd2b_[:], rstd2row[:], 128),
                        "GP", 1)

            with nc.sbuf_tensor("g32", [128, T + 1], F32) as g32, \
                 nc.sbuf_tensor("xm32", [128, T], F32) as xm32, \
                 nc.sbuf_tensor("xphi", [128, T], F16) as xphi, \
                 nc.sbuf_tensor("xplo", [128, T], F16) as xplo, \
                 nc.sbuf_tensor("mur2", [128, 8], F32) as mur2, \
                 nc.sbuf_tensor("muk2", [128, 8], F32) as muk2:

                @block.vector
                def _(vector):
                    vector.wait_ge(sem.h["GP"], tok["bcast2"])
                    vector.tensor_scalar(mur2[:], p_sb["mu_r"][:], -1.0, 1.0,
                                         ALU.mult, ALU.add)
                    vector.memset(g32[:, 0:1], 0.0)
                    tok["mu2"] = vpub(
                        vector,
                        vector.tensor_scalar(muk2[:], p_sb["mu_k"][:], -1.0, 1.0,
                                             ALU.mult, ALU.add))

                for ec in range(EC):
                    @block.sync
                    def _(sync, ec=ec):
                        if ec >= 2:
                            sync.wait_ge(sem.h["VE"], tok[("gdone", ec - 2)])
                        sem.inc(sync.dma_start(out=xs4[ec % 2][0][:],
                                               in_=x1_sp[0, ec]),
                                "LH%d" % (ec % 2), 16)
                        tok[("x5", ec)] = sem.inc(
                            sync.dma_start(out=xs4[ec % 2][1][:],
                                           in_=x1_sp[1, ec]),
                            "LH%d" % (ec % 2), 16)

                    for which, mucol, mu2col, spt in (
                            ("xr", p_sb["mu_r"], mur2, xr_sp),
                            ("xk", p_sb["mu_k"], muk2, xk_sp)):
                        @block.vector
                        def _(vector, ec=ec, which=which, mucol=mucol,
                              mu2col=mu2col):
                            if which == "xk":
                                vector.wait_ge(sem.h["STH"],
                                               tok[("xrsp", ec)])
                            if which == "xr":
                                vector.wait_ge(sem.h["LH%d" % (ec % 2)],
                                               tok[("x5", ec)])
                                if ec == 0:
                                    vector.wait_ge(sem.h["VE"], tok["mu2"])
                                if ec >= 1:
                                    vector.wait_ge(sem.h["STH"],
                                                   tok[("xksp", ec - 1)])
                                vector.tensor_tensor(t4[ec % 2][:],
                                                     xs4[ec % 2][0][:],
                                                     xs4[ec % 2][1][:], ALU.add)
                                vector.tensor_tensor(t4[ec % 2][:],
                                                     t4[ec % 2][:], m2b_[:],
                                                     ALU.subtract)
                                vector.tensor_tensor(g32[:, 1:T + 1],
                                                     t4[ec % 2][:], rstd2b_[:],
                                                     ALU.mult)
                            vector.tensor_scalar(xm32[:], g32[:, 1:T + 1],
                                                 mucol[:, ec:ec + 1], None,
                                                 ALU.mult)
                            vector.scalar_tensor_tensor(xm32[:], g32[:, 0:T],
                                                        mu2col[:, ec:ec + 1],
                                                        xm32[:], ALU.mult,
                                                        ALU.add)
                            vector.tensor_copy(xphi[:], xm32[:])
                            tok[(which, ec)] = sem.inc(
                                vector.scalar_tensor_tensor(
                                    xplo[:], xm32[:], 1.0, xphi[:], ALU.mult,
                                    ALU.subtract), "VE", 1)
                            if which == "xk":
                                tok[("gdone", ec)] = tok[("xk", ec)]

                        @block.sync
                        def _(sync, ec=ec, which=which, spt=spt):
                            sync.wait_ge(sem.h["VE"], tok[(which, ec)])
                            sem.inc(sync.dma_start(out=spt[0, ec], in_=xphi[:]),
                                    "STH", 16)
                            tok[(which + "sp", ec)] = sem.inc(
                                sync.dma_start(out=spt[1, ec], in_=xplo[:]),
                                "STH", 16)

            with nc.sbuf_tensor("w6a", [128, EC * 128], F32) as w6a, \
                 nc.sbuf_tensor("w6b", [128, EC * 128], F32) as w6b, \
                 nc.sbuf_tensor("w6ha", [128, EC * 128], F16) as w6ha, \
                 nc.sbuf_tensor("w6hb", [128, EC * 128], F16) as w6hb, \
                 nc.sbuf_tensor("w6la", [128, EC * 128], F16) as w6la, \
                 nc.sbuf_tensor("w6lb", [128, EC * 128], F16) as w6lb, \
                 nc.sbuf_tensor("kahi", [128, EC * T], F16) as _kh, \
                 nc.sbuf_tensor("kalo", [128, EC * T], F16) as _kl, \
                 nc.sbuf_tensor("r32a", [128, T], F32) as r32a, \
                 nc.sbuf_tensor("r32b", [128, T], F32) as r32c, \
                 nc.sbuf_tensor("xs6a0", [128, T], F16) as xs6a0, \
                 nc.sbuf_tensor("xs6a1", [128, T], F16) as xs6a1, \
                 nc.sbuf_tensor("xs6b0", [128, T], F16) as xs6b0, \
                 nc.sbuf_tensor("xs6b1", [128, T], F16) as xs6b1, \
                 nc.sbuf_tensor("s6", [128, T], F32) as s6, \
                 nc.sbuf_tensor("y32", [128, T], F32) as y32, \
                 nc.psum_tensor("psF", [128, T], F32) as psF, \
                 nc.psum_tensor("psF2", [128, T], F32) as psF2:

                def kchunk_hi(dc):
                    return _kh[:, dc * T:(dc + 1) * T]

                def kchunk_lo(dc):
                    return _kl[:, dc * T:(dc + 1) * T]

                psFs = [psF, psF2]
                w6 = [w6a, w6b]
                w6h = [w6ha, w6hb]
                w6l = [w6la, w6lb]
                r32s = [r32a, r32c]

                FSRC = ([("r", d) for d in range(EC)]
                        + [("k2", d) for d in range(EC)]
                        + [("v2", d) for d in range(EC)])

                @block.sync
                def _(sync):
                    sync.wait_ge(sem.h["VE"], tok[("gdone", EC - 1)])
                    sem.inc(sync.dma_start(out=w6[0][:], in_=wblk["r"][0]),
                            "LW0", 16)
                    tok[("w6", 0)] = sem.cur("LW0")

                def w6chain(gi):
                    sl = gi % 2

                    @block.scalar
                    def _(scalar, gi=gi, sl=sl):
                        scalar.wait_ge(sem.h["LW%d" % sl], tok[("w6", gi)])
                        if gi >= 2:
                            scalar.wait_ge(sem.h["TE"], tok[("fmm", gi - 2)])
                        tok[("w6hi", gi)] = sem.inc(
                            scalar.activation(w6h[sl][:], w6[sl][:], ACT.Copy),
                            "SC", 1)

                    @block.vector
                    def _(vector, gi=gi, sl=sl):
                        vector.wait_ge(sem.h["SC"], tok[("w6hi", gi)])
                        tok[("w6sp", gi)] = sem.inc(
                            vector.scalar_tensor_tensor(
                                w6l[sl][:], w6[sl][:], 1.0, w6h[sl][:],
                                ALU.mult, ALU.subtract), "VE", 1)

                    @block.sync
                    def _(sync, gi=gi):
                        if gi + 1 < len(FSRC):
                            osl = (gi + 1) % 2
                            sync.wait_ge(sem.h["SC"], tok[("w6hi", gi)])
                            if ("w6sp", gi - 1) in tok:
                                sync.wait_ge(sem.h["VE"], tok[("w6sp", gi - 1)])
                            m, d = FSRC[gi + 1]
                            sem.inc(sync.dma_start(out=w6[osl][:],
                                                   in_=wblk[m][d]),
                                    "LW%d" % osl, 16)
                            tok[("w6", gi + 1)] = sem.cur("LW%d" % osl)

                xs6 = [[xs6a0, xs6a1], [xs6b0, xs6b1]]

                def fmm2_streamed(base, dp, spt, bufs):
                    for ec in range(EC):
                        @block.sync
                        def _(sync, base=base, dp=dp, ec=ec, spt=spt,
                              bufs=bufs):
                            if ec >= 2:
                                sync.wait_ge(sem.h["TE"],
                                             tok[("f6u2", base, dp, ec - 2)])
                            elif dp > 0:
                                sync.wait_ge(
                                    sem.h["TE"],
                                    tok[("f6u2", base, dp - 1, EC - 2 + ec)])
                            if base == 0:
                                sync.wait_ge(sem.h["STH"], tok[("xrsp", ec)])
                            else:
                                sync.wait_ge(sem.h["STH"], tok[("xksp", ec)])
                            sem.inc(sync.dma_start(out=bufs[ec % 2][0][:],
                                                   in_=spt[0, ec]),
                                    "LH%d" % (ec % 2), 16)
                            tok[("f6l2", base, dp, ec)] = sem.inc(
                                sync.dma_start(out=bufs[ec % 2][1][:],
                                               in_=spt[1, ec]),
                                "LH%d" % (ec % 2), 16)

                        @block.tensor
                        def _(tensor, base=base, dp=dp, ec=ec, bufs=bufs):
                            g0, g1 = base + 2 * dp, base + 2 * dp + 1
                            tensor.wait_ge(sem.h["LH%d" % (ec % 2)],
                                           tok[("f6l2", base, dp, ec)])
                            if ec == 0:
                                tensor.wait_ge(sem.h["VE"], tok[("w6sp", g0)])
                                tensor.wait_ge(sem.h["VE"], tok[("w6sp", g1)])
                                for gg in (g0 - 2, g1 - 2):
                                    if ("fev", gg) in tok:
                                        tensor.wait_ge(sem.h["VE"],
                                                       tok[("fev", gg)])
                            first = (ec == 0)
                            ins = None
                            for g in (g0, g1):
                                ps = psFs[g % 2]
                                whi_ = w6h[g % 2]
                                wlo_ = w6l[g % 2]
                                for qt in range(NQT):
                                    sl = slice(qt * QT, (qt + 1) * QT)
                                    tensor.matmul(ps[:, sl], wsl(whi_, ec),
                                                  bufs[ec % 2][0][:, sl],
                                                  start=first, stop=False)
                                    tensor.matmul(ps[:, sl], wsl(whi_, ec),
                                                  bufs[ec % 2][1][:, sl],
                                                  start=False, stop=False)
                                    ins = tensor.matmul(ps[:, sl],
                                                        wsl(wlo_, ec),
                                                        bufs[ec % 2][0][:, sl],
                                                        start=False,
                                                        stop=(ec == EC - 1))
                            tok[("f6u2", base, dp, ec)] = sem.inc(ins, "TE", 1)
                    last = tok[("f6u2", base, dp, EC - 1)]
                    tok[("fmm", base + 2 * dp)] = last
                    tok[("fmm", base + 2 * dp + 1)] = last

                def fmm_streamed(gi, spt, bufs):
                    for ec in range(EC):
                        @block.sync
                        def _(sync, gi=gi, ec=ec, spt=spt, bufs=bufs):
                            si = gi * EC + ec
                            if si >= 2:
                                sync.wait_ge(sem.h["TE"], tok[("f6u", si - 2)])
                            if gi < EC:
                                sync.wait_ge(sem.h["STH"], tok[("xrsp", ec)])
                            elif gi < 2 * EC:
                                sync.wait_ge(sem.h["STH"], tok[("xksp", ec)])
                            sem.inc(sync.dma_start(out=bufs[ec % 2][0][:],
                                                   in_=spt[0, ec]),
                                    "LH%d" % (ec % 2), 16)
                            tok[("f6l", si)] = sem.inc(
                                sync.dma_start(out=bufs[ec % 2][1][:],
                                               in_=spt[1, ec]),
                                "LH%d" % (ec % 2), 16)

                        @block.tensor
                        def _(tensor, gi=gi, ec=ec, bufs=bufs):
                            si = gi * EC + ec
                            tensor.wait_ge(sem.h["LH%d" % (ec % 2)],
                                           tok[("f6l", si)])
                            if ec == 0:
                                tensor.wait_ge(sem.h["VE"], tok[("w6sp", gi)])
                                if gi >= 2:
                                    tensor.wait_ge(sem.h["VE"],
                                                   tok[("fev", gi - 2)])
                            ps = psFs[gi % 2]
                            whi_ = w6h[gi % 2]
                            wlo_ = w6l[gi % 2]
                            first = (ec == 0)
                            ins = None
                            for qt in range(NQT):
                                sl = slice(qt * QT, (qt + 1) * QT)
                                tensor.matmul(ps[:, sl], wsl(whi_, ec),
                                              bufs[ec % 2][0][:, sl],
                                              start=first, stop=False)
                                tensor.matmul(ps[:, sl], wsl(whi_, ec),
                                              bufs[ec % 2][1][:, sl],
                                              start=False, stop=False)
                                ins = tensor.matmul(ps[:, sl], wsl(wlo_, ec),
                                                    bufs[ec % 2][0][:, sl],
                                                    start=False,
                                                    stop=(ec == EC - 1))
                            tok[("f6u", si)] = sem.inc(ins, "TE", 1)
                    tok[("fmm", gi)] = tok[("f6u", gi * EC + EC - 1)]

                def emit_rev(dc):
                    @block.scalar
                    def _(scalar, dc=dc):
                        scalar.wait_ge(sem.h["TE"], tok[("fmm", dc)])
                        if dc >= 2:
                            scalar.wait_ge(sem.h["STR"], tok[("rsp", dc - 2)])
                        tok[("rev", dc)] = sem.inc(
                            scalar.activation(r32s[dc % 2][:], psFs[dc % 2][:],
                                              ACT.Sigmoid), "SC", 1)

                    @block.vector
                    def _(vector, dc=dc):
                        vector.wait_ge(sem.h["SC"], tok[("rev", dc)])
                        tok[("fev", dc)] = sem.inc(
                            vector.tensor_copy(s6[:, 0:1],
                                               r32s[dc % 2][:, 0:1]), "VE", 1)

                    @block.sync
                    def _(sync, dc=dc):
                        sync.wait_ge(sem.h["SC"], tok[("rev", dc)])
                        tok[("rsp", dc)] = sem.inc(
                            sync.dma_start(out=r_sp[dc], in_=r32s[dc % 2][:]),
                            "STR", 16)

                def emit_kev(dc, second):
                    gi = EC + dc

                    @block.scalar
                    def _(scalar, dc=dc, gi=gi, second=second):
                        scalar.wait_ge(sem.h["TE"], tok[("fmm", gi)])
                        if second:
                            scalar.wait_ge(sem.h["VE"], tok[("kact", dc - 1)])
                        tok[("kev", dc)] = sem.inc(
                            scalar.activation(s6[:], psFs[gi % 2][:], ACT.Relu),
                            "SC", 1)

                    @block.vector
                    def _(vector, dc=dc, gi=gi):
                        vector.wait_ge(sem.h["SC"], tok[("kev", dc)])
                        vector.tensor_tensor(y32[:], s6[:], s6[:], ALU.mult)
                        vector.tensor_copy(kchunk_hi(dc), y32[:])
                        tok[("kact", dc)] = sem.inc(
                            vector.scalar_tensor_tensor(
                                kchunk_lo(dc), y32[:], 1.0, kchunk_hi(dc),
                                ALU.mult, ALU.subtract), "VE", 1)
                        tok[("fev", gi)] = tok[("kact", dc)]

                for dp in range(EC // 2):
                    w6chain(2 * dp)
                    w6chain(2 * dp + 1)
                    fmm2_streamed(0, dp, xr_sp, xs6)
                    emit_rev(2 * dp)
                    emit_rev(2 * dp + 1)

                for dp in range(EC // 2):
                    w6chain(EC + 2 * dp)
                    w6chain(EC + 2 * dp + 1)
                    fmm2_streamed(EC, dp, xk_sp, xs4)
                    emit_kev(2 * dp, False)
                    emit_kev(2 * dp + 1, True)

                for dc in range(EC):
                    gi = 2 * EC + dc
                    w6chain(gi)

                    @block.tensor
                    def _(tensor, dc=dc, gi=gi):
                        tensor.wait_ge(sem.h["VE"], tok[("w6sp", gi)])
                        tensor.wait_ge(sem.h["VE"], tok[("fev", gi - 2)])
                        tensor.wait_ge(sem.h["VE"], tok[("kact", EC - 1)])
                        ps = psFs[gi % 2]
                        whi_ = w6h[gi % 2]
                        wlo_ = w6l[gi % 2]
                        ins = None
                        for ec in range(EC):
                            first = (ec == 0)
                            for qt in range(NQT):
                                sl = slice(qt * QT, (qt + 1) * QT)
                                tensor.matmul(ps[:, sl], wsl(whi_, ec),
                                              kchunk_hi(ec)[:, sl],
                                              start=first, stop=False)
                                tensor.matmul(ps[:, sl], wsl(whi_, ec),
                                              kchunk_lo(ec)[:, sl],
                                              start=False, stop=False)
                                ins = tensor.matmul(ps[:, sl], wsl(wlo_, ec),
                                                    kchunk_hi(ec)[:, sl],
                                                    start=False,
                                                    stop=(ec == EC - 1))
                        tok[("fmm", gi)] = sem.inc(ins, "TE", 1)

                    @block.sync
                    def _(sync, dc=dc):
                        if dc >= 2:
                            sync.wait_ge(sem.h["VE"], tok[("yev", dc - 2)])
                        sem.inc(sync.dma_start(out=r32s[dc % 2][:],
                                               in_=r_sp[dc]),
                                "LX%d" % (dc % 2), 16)
                        tok[("r6", dc)] = sem.cur("LX%d" % (dc % 2))
                        sem.inc(sync.dma_start(out=xs4[dc % 2][0][:],
                                               in_=x1_sp[0, dc]),
                                "LH%d" % (dc % 2), 16)
                        tok[("x6", dc)] = sem.inc(
                            sync.dma_start(out=xs4[dc % 2][1][:],
                                           in_=x1_sp[1, dc]),
                            "LH%d" % (dc % 2), 16)

                    @block.vector
                    def _(vector, dc=dc, gi=gi):
                        vector.wait_ge(sem.h["TE"], tok[("fmm", gi)])
                        vector.wait_ge(sem.h["LX%d" % (dc % 2)], tok[("r6", dc)])
                        vector.wait_ge(sem.h["LH%d" % (dc % 2)], tok[("x6", dc)])
                        if dc >= 1:
                            vector.wait_ge(sem.h["STY"], tok[("ysp", dc - 1)])
                        vector.tensor_tensor(s6[:], xs4[dc % 2][0][:],
                                             xs4[dc % 2][1][:], ALU.add)
                        vector.scalar_tensor_tensor(y32[:], psFs[gi % 2][:], 1.0,
                                                    r32s[dc % 2][:], ALU.mult,
                                                    ALU.mult)
                        tok[("yev", dc)] = sem.inc(
                            vector.tensor_tensor(y32[:], y32[:], s6[:], ALU.add),
                            "VE", 1)
                        tok[("fev", gi)] = tok[("yev", dc)]

                    @block.sync
                    def _(sync, dc=dc):
                        sync.wait_ge(sem.h["VE"], tok[("yev", dc)])
                        tok[("ysp", dc)] = sem.inc(
                            sync.dma_start(out=y[dc * 128:(dc + 1) * 128, :],
                                           in_=y32[:]), "STY", 16)

                @block.sync
                def _(sync):
                    sync.wait_ge(sem.h["STY"], sem.cur("STY"))
                    sync.wait_ge(sem.h["STR"], sem.cur("STR"))
    lower_extended_insts(nc)
    return nc


_BUILD_CACHE = {}


def _get_nc(tap=None):
    if tap not in _BUILD_CACHE:
        _BUILD_CACHE[tap] = build(tap)
    return _BUILD_CACHE[tap]


def _stage_inputs(kw):
    x = np.asarray(kw["L0_attn_x"])
    B = x.shape[0]
    xt = np.ascontiguousarray(x.transpose(0, 2, 1))
    c = _host_constants()
    base = {("c_" + k): v for k, v in c.items()}
    for name, key in [("bq", "L0_attn_bq"), ("bk", "L0_attn_bk"),
                      ("bv", "L0_attn_bv"), ("bg", "L0_attn_bg"),
                      ("bo", "L0_attn_bo"), ("gamma", "L0_attn_gamma"),
                      ("beta", "L0_attn_beta"), ("mu_r", "L0_ffn_mu_r"),
                      ("mu_k", "L0_ffn_mu_k")]:
        base["p_" + name] = _col8(np.asarray(kw[key]))
    base["p_bd"] = np.ascontiguousarray(np.asarray(kw["L0_attn_bd"])[None, :])
    base["w_dT"] = np.ascontiguousarray(
        np.asarray(kw["L0_attn_wd"]).T.reshape(EC, 128, H))
    for m, key in [("q", "L0_attn_wq"), ("k", "L0_attn_wk"), ("v", "L0_attn_wv"),
                   ("g", "L0_attn_wg"), ("o", "L0_attn_wo"), ("r", "L0_ffn_W_r"),
                   ("k2", "L0_ffn_W_k"), ("v2", "L0_ffn_W_v")]:
        base["w_" + m] = _blk_pair(np.ascontiguousarray(np.asarray(kw[key]).T))
    in_maps = []
    for b in range(B):
        d = dict(base)
        d["x_fm"] = xt[b]
        in_maps.append(d)
    return in_maps


def kernel(**kw):
    in_maps = _stage_inputs(kw)
    nc = _get_nc(None)
    res = run_bass_kernel_spmd(nc, in_maps, core_ids=list(range(8)))
    out = np.stack([np.asarray(r["y"]).T for r in res.results], axis=0)
    return out.astype(np.float32)
